# version 02
# baseline (speedup 1.0000x reference)
import contextlib
import numpy as np
import concourse.bass as bass
import concourse.mybir as mybir
from concourse.bass_utils import run_bass_kernel_spmd

F32 = mybir.dt.float32
BF16 = mybir.dt.bfloat16
I32 = mybir.dt.int32
U8 = mybir.dt.uint8
U32 = mybir.dt.uint32
AF = mybir.ActivationFunctionType
ALU = mybir.AluOpType
AX = mybir.AxisListType

T = 2048
D = 2048
ND = D // 128
TCH = 512
NT = T // TCH
NTT = T // 128
EPS_LN = 1e-5
EPS_RMS = 1e-8
TWO_PI = 6.283185307179586
PI = 3.141592653589793

ENGS = ("pe", "act", "dve", "pool", "sp")
_BLKATTR = {"pe": "tensor", "act": "scalar", "dve": "vector", "pool": "gpsimd", "sp": "sync"}
_ENGSEM = {"pe": ("c_pe", 1), "act": ("c_act", 1), "dve": ("c_dve", 1),
           "pool": ("c_pool", 1), "sp": ("c_sp", 1)}
_DMASEM = {"sp": ("d_sp", 16), "act": ("d_act", 16), "pool": ("d_pool", 16)}


class Prog:
    def __init__(self, name="prog"):
        self.nc = bass.Bass("TRN2", target_bir_lowering=False)
        self.stack = contextlib.ExitStack()
        self.ops = {e: [] for e in ENGS}
        self.sem_count = {}
        self.frontier = {}
        self.eng_waited = {e: {} for e in ENGS}
        self.sems = {}
        for s in ("c_pe", "c_act", "c_dve", "c_pool", "c_sp", "d_sp", "d_act", "d_pool"):
            self.sems[s] = self.stack.enter_context(self.nc.semaphore(s))
            self.sem_count[s] = 0
        self.sb = {}
        self.ps = {}
        self.dram = {}
        self._finalized = False

    def dram_t(self, name, shape, dt, kind):
        ap = self.nc.dram_tensor(name, shape, dt, kind=kind).ap()
        self.dram[name] = ap
        return ap

    def sbuf(self, name, shape, dt, stack=None):
        st = stack if stack is not None else self.stack
        t = st.enter_context(self.nc.sbuf_tensor(name, shape, dt))
        self.sb[name] = t
        return t

    def psum(self, name, shape=(128, 512), dt=F32, stack=None):
        st = stack if stack is not None else self.stack
        t = st.enter_context(self.nc.psum_tensor(name, list(shape), dt))
        self.ps[name] = t
        return t

    def phase_stack(self):
        return contextlib.ExitStack()

    def _wait_closures(self, eng):
        out = []
        w = self.eng_waited[eng]
        for s, v in self.frontier.items():
            if v > 0 and w.get(s, 0) < v:
                sem = self.sems[s]
                out.append(lambda e, sem=sem, v=v: e.wait_ge(sem, v))
                w[s] = v
        return out

    def emit(self, eng, fn, dma=False, mark=True):
        self.ops[eng].extend(self._wait_closures(eng))
        if mark:
            sname, amt = _DMASEM[eng] if dma else _ENGSEM[eng]
            sem = self.sems[sname]
            self.sem_count[sname] += amt

            def wrapped(e, fn=fn, sem=sem, amt=amt):
                instr = fn(e)
                instr.then_inc(sem, amt)
            self.ops[eng].append(wrapped)
        else:
            self.ops[eng].append(lambda e, fn=fn: fn(e))

    def barrier(self):
        for s, c in self.sem_count.items():
            self.frontier[s] = c

    def finalize(self):
        assert not self._finalized
        self._finalized = True
        self.barrier()
        for e in ENGS:
            if self.ops[e]:
                self.ops[e].extend(self._wait_closures(e))
        blk = self.stack.enter_context(self.nc.Block())
        for e in ENGS:
            if not self.ops[e]:
                continue
            oplist = self.ops[e]

            def body(engobj, oplist=oplist):
                for op in oplist:
                    op(engobj)
            getattr(blk, _BLKATTR[e])(body)
        self.stack.close()
        return self.nc

    def dma(self, queue, out, in_, **kw):
        self.emit(queue, lambda e: e.dma_start(out=out, in_=in_, **kw), dma=True)

    def dma_t(self, out, in_):
        self.emit("sp", lambda e: e.dma_start_transpose(out=out, in_=in_), dma=True)



def _hl_col(v):
    k = v.shape[0] // 128
    return np.ascontiguousarray(v.reshape(k, 128).T)


def host_prep(inp):
    H = {}
    H["ones1x128"] = np.ones((1, 128), dtype=np.float32)
    H["ones128x1"] = np.ones((128, 1), dtype=np.float32)
    H["iota8"] = np.tile(np.arange(8, dtype=np.float32), (128, 1))
    H["iota128"] = np.tile(np.arange(128, dtype=np.float32), (128, 1))
    H["iota16"] = np.tile(np.arange(16, dtype=np.float32), (128, 1))
    H["identity128"] = np.eye(128, dtype=np.float32)
    H["c_eps5"] = np.full((128, 1), EPS_LN, dtype=np.float32)
    H["c_eps8"] = np.full((128, 1), EPS_RMS, dtype=np.float32)
    H["c_one"] = np.full((128, 1), 1.0, dtype=np.float32)
    H["nu_log"] = _hl_col(inp["L0_attn_nu_log"])
    H["theta_log"] = _hl_col(inp["L0_attn_theta_log"])
    H["l0a_D"] = _hl_col(inp["L0_attn_D"])
    H["BreT"] = np.ascontiguousarray(inp["L0_attn_B_re"].T)
    H["BimT"] = np.ascontiguousarray(inp["L0_attn_B_im"].T)
    H["CreT"] = np.ascontiguousarray(inp["L0_attn_C_re"].T)
    H["CimT"] = np.ascontiguousarray(inp["L0_attn_C_im"].T)
    H["gwT"] = np.ascontiguousarray(inp["L0_ffn_gate_w"].T)
    H["moe_wg"] = inp["L0_ffn_w_gate"]
    H["moe_wu_hc"] = np.ascontiguousarray(
        inp["L0_ffn_w_up"].reshape(8, D, 11, 128).transpose(0, 2, 1, 3))
    H["moe_wd"] = inp["L0_ffn_w_down"]
    H["shg"] = _hl_col(inp["L0_ffn_shgate_w"])
    HS = 2816
    H["shwg_b"] = np.ascontiguousarray(
        inp["L0_ffn_shw_gate"].reshape(D, HS // 128, 128).transpose(1, 0, 2))
    H["shwu_b"] = np.ascontiguousarray(
        inp["L0_ffn_shw_up"].reshape(D, HS // 128, 128).transpose(1, 0, 2))
    H["shwd_b"] = np.ascontiguousarray(
        inp["L0_ffn_shw_down"].reshape(22, 128, ND, 128).transpose(2, 0, 1, 3).reshape(ND, 2816, 128))
    H["n1w"] = _hl_col(inp["L1_attn_norm1_w"])
    H["n2w"] = _hl_col(inp["L1_attn_norm2_w"])
    def _mblk(w):
        m = w.shape[1] // 128
        return np.ascontiguousarray(w.reshape(D, m, 128).transpose(1, 0, 2))
    H["leftW_b"] = _mblk(inp["L1_attn_left_W"])
    H["rightW_b"] = _mblk(inp["L1_attn_right_W"])
    H["gaW_b"] = _mblk(inp["L1_attn_ga_W"])
    H["gxW_b"] = _mblk(inp["L1_attn_gx_W"])
    H["outW_b"] = _mblk(inp["L1_attn_out_W"])
    H["left_b"] = _hl_col(inp["L1_attn_left_b"])
    H["right_b"] = _hl_col(inp["L1_attn_right_b"])
    H["conv_b"] = _hl_col(inp["L1_attn_conv_b"])
    H["ga_b"] = _hl_col(inp["L1_attn_ga_b"])
    H["gx_b"] = _hl_col(inp["L1_attn_gx_b"])
    H["out_b"] = _hl_col(inp["L1_attn_out_b"])
    H["a_param"] = _hl_col(inp["L1_attn_a_param"])
    H["conv_w"] = np.ascontiguousarray(
        inp["L1_attn_conv_w"].T.reshape(16, 128, 4).transpose(1, 0, 2).reshape(128, 64))
    F = 6144
    H["up1_b"] = _hl_col(inp["L1_attn_up1_b"])
    H["up2_b"] = _hl_col(inp["L1_attn_up2_b"])
    H["down_b"] = _hl_col(inp["L1_attn_down_b"])
    H["up1W_b"] = _mblk(inp["L1_attn_up1_W"])
    H["up2W_b"] = _mblk(inp["L1_attn_up2_W"])
    H["downW_b"] = np.ascontiguousarray(
        inp["L1_attn_down_W"].reshape(F, ND, 128).transpose(1, 0, 2))
    HQ = 5632
    H["wqgT_b"] = np.ascontiguousarray(
        inp["L1_ffn_wq_gate"].T.reshape(D, HQ // 128, 128).transpose(1, 0, 2))
    H["wquT_b"] = np.ascontiguousarray(
        inp["L1_ffn_wq_up"].T.reshape(D, HQ // 128, 128).transpose(1, 0, 2))
    H["wqdT_b"] = np.ascontiguousarray(
        inp["L1_ffn_wq_down"].T.reshape(HQ, ND, 128).transpose(1, 0, 2))
    H["s_gate"] = _hl_col(inp["L1_ffn_s_gate"])
    H["s_up"] = _hl_col(inp["L1_ffn_s_up"])
    H["s_down"] = _hl_col(inp["L1_ffn_s_down"])
    return H


CONST_NAMES = ["ones1x128", "ones128x1", "iota8", "iota128", "iota16",
               "identity128", "c_eps5", "c_eps8", "c_one"]


def declare_inputs(p, H, names):
    for n in names:
        a = H[n]
        dt = I32 if a.dtype == np.int32 else F32
        p.dram_t(n, list(a.shape), dt, "ExternalInput")


def load_consts(p):
    c = {}
    c["ones1x128"] = p.sbuf("cst_ones1x128", [1, 128], BF16)
    c["ones128x1"] = p.sbuf("cst_ones128x1", [128, 1], BF16)
    for n, shape in [("iota8", [128, 8]), ("iota128", [128, 128]), ("iota16", [128, 16]),
                     ("identity128", [128, 128]), ("c_eps5", [128, 1]), ("c_eps8", [128, 1]),
                     ("c_one", [128, 1])]:
        c[n] = p.sbuf("cst_" + n, shape, F32)
        p.dma("sp", c[n][:], p.dram[n])
    p.dma("pool", c["ones1x128"][:], p.dram["ones1x128"])
    p.dma("pool", c["ones128x1"][:], p.dram["ones128x1"])
    p.barrier()
    return c



def emit_ln(p, r_src, xn_tm, tag):
    st = p.phase_stack()
    PAR = 4
    rt = [p.sbuf(f"{tag}_ln_r{j}", [128, D], F32, st) for j in range(PAR)]
    xt = [p.sbuf(f"{tag}_ln_x{j}", [128, D], BF16, st) for j in range(PAR)]
    stats = [p.sbuf(f"{tag}_ln_s{j}", [128, 24], F32, st) for j in range(PAR)]
    mv = [p.sbuf(f"{tag}_ln_mv{j}", [128, 2], F32, st) for j in range(PAR)]
    rstd = [p.sbuf(f"{tag}_ln_rs{j}", [128, 1], F32, st) for j in range(PAR)]
    eps = p.sb["cst_c_eps5"]
    for i0 in range(0, NTT, PAR):
        for j in range(PAR):
            i = i0 + j
            p.dma("sp", rt[j][:], r_src[i * 128:(i + 1) * 128, :])
        p.barrier()
        for j in range(PAR):
            for q in range(4):
                p.emit("dve", lambda e, j=j, q=q: e.bn_stats(
                    out=stats[j][:, q * 6:(q + 1) * 6], in_=rt[j][:, q * 512:(q + 1) * 512]))
        p.barrier()
        for j in range(PAR):
            p.emit("dve", lambda e, j=j: e.bn_aggr(out=mv[j][:], in_=stats[j][:]))
        p.barrier()
        for j in range(PAR):
            p.emit("act", lambda e, j=j: e.activation(
                out=rstd[j][:], in_=mv[j][:, 1:2], func=AF.Ln, bias=eps[:]))
        p.barrier()
        for j in range(PAR):
            p.emit("act", lambda e, j=j: e.activation(
                out=rstd[j][:], in_=rstd[j][:], func=AF.Exp, scale=-0.5))
        p.barrier()
        for j in range(PAR):
            p.emit("dve", lambda e, j=j: e.tensor_scalar(
                out=xt[j][:], in0=rt[j][:], scalar1=mv[j][:, 0:1], scalar2=rstd[j][:],
                op0=ALU.subtract, op1=ALU.mult))
        p.barrier()
        for j in range(PAR):
            i = i0 + j
            p.dma("sp", xn_tm[i * 128:(i + 1) * 128, :], xt[j][:])
        p.barrier()
    st.close()


def emit_load_xn_fm(p, xn_tm, xn_fm, tag):
    for k in range(ND):
        p.dma_t(xn_fm[k][:], xn_tm[0:T, k * 128:(k + 1) * 128])
    p.barrier()


def emit_resid(p, r_src, adds_fm, r_dst, tag):
    st = p.phase_stack()
    PAR = 2
    rt = [p.sbuf(f"{tag}_rs_r{j}", [128, D], F32, st) for j in range(PAR)]
    at = [[p.sbuf(f"{tag}_rs_a{j}_{m}", [128, D], BF16, st) for m in range(len(adds_fm))]
          for j in range(PAR)]
    for i0 in range(0, NTT, PAR):
        for j in range(PAR):
            i = i0 + j
            p.dma("sp", rt[j][:], r_src[i * 128:(i + 1) * 128, :])
            for m, a in enumerate(adds_fm):
                p.dma_t(at[j][m][:], a[0:D, i * 128:(i + 1) * 128])
        p.barrier()
        for m in range(len(adds_fm)):
            for j in range(PAR):
                p.emit("dve", lambda e, j=j, m=m: e.tensor_tensor(
                    out=rt[j][:], in0=rt[j][:], in1=at[j][m][:], op=ALU.add))
            p.barrier()
        for j in range(PAR):
            i = i0 + j
            p.dma("sp", r_dst[i * 128:(i + 1) * 128, :], rt[j][:])
        p.barrier()
    st.close()


def emit_softplus(p, out, in_, tmp, tag):
    one = p.sb["cst_c_one"]
    p.emit("dve", lambda e: e.tensor_scalar(out=tmp[:], in0=in_[:], scalar1=-1.0, op0=ALU.mult,
                                            scalar2=None))
    p.barrier()
    p.emit("dve", lambda e: e.tensor_tensor(out=tmp[:], in0=tmp[:], in1=in_[:], op=ALU.max))
    p.barrier()
    p.emit("dve", lambda e: e.tensor_scalar(out=tmp[:], in0=tmp[:], scalar1=-1.0, op0=ALU.mult,
                                            scalar2=None))
    p.barrier()
    p.emit("act", lambda e: e.activation(out=tmp[:], in_=tmp[:], func=AF.Exp))
    p.barrier()
    p.emit("act", lambda e: e.activation(out=tmp[:], in_=tmp[:], func=AF.Ln, bias=one[:]))
    p.barrier()
    p.emit("dve", lambda e: e.tensor_scalar(out=out[:], in0=in_[:], scalar1=0.0, op0=ALU.max,
                                            scalar2=None))
    p.barrier()
    p.emit("dve", lambda e: e.tensor_tensor(out=out[:], in0=out[:], in1=tmp[:], op=ALU.add))
    p.barrier()


def emit_sin_pm_pi(p, out, ang, tag):
    p.emit("act", lambda e: e.activation(out=out[:], in_=ang[:], func=AF.Sin))


def emit_mod_2pi(p, out, in_, itmp, ftmp, tag):
    p.emit("dve", lambda e: e.tensor_scalar(out=out[:], in0=in_[:], scalar1=1.0 / TWO_PI,
                                            op0=ALU.mult, scalar2=None))
    p.barrier()
    p.emit("dve", lambda e: e.tensor_copy(out=itmp[:], in_=out[:]))
    p.barrier()
    p.emit("dve", lambda e: e.tensor_copy(out=ftmp[:], in_=itmp[:]))
    p.barrier()
    p.emit("dve", lambda e: e.tensor_tensor(out=itmp[:], in0=ftmp[:], in1=out[:], op=ALU.is_gt))
    p.barrier()
    p.emit("dve", lambda e: e.scalar_tensor_tensor(out=ftmp[:], in0=itmp[:], scalar=-1.0,
                                                   op0=ALU.mult, op1=ALU.add, in1=ftmp[:]))
    p.barrier()
    p.emit("dve", lambda e: e.tensor_tensor(out=out[:], in0=out[:], in1=ftmp[:], op=ALU.subtract))
    p.barrier()
    p.emit("dve", lambda e: e.tensor_scalar(out=out[:], in0=out[:], scalar1=TWO_PI,
                                            op0=ALU.mult, scalar2=None))
    p.barrier()


def emit_fold_to_pi(p, ang, msk, tag):
    p.emit("dve", lambda e: e.tensor_scalar(out=msk[:], in0=ang[:], scalar1=PI, op0=ALU.is_gt,
                                            scalar2=None))
    p.barrier()
    p.emit("dve", lambda e: e.scalar_tensor_tensor(out=ang[:], in0=msk[:], scalar=-TWO_PI,
                                                   op0=ALU.mult, op1=ALU.add, in1=ang[:]))
    p.barrier()



def emit_l0a(p, out_fm):
    NS = 2
    st = p.phase_stack()
    nu_l = p.sbuf("l0a_nul", [128, 2], F32, st)
    th_l = p.sbuf("l0a_thl", [128, 2], F32, st)
    Dv = p.sbuf("l0a_Dv", [128, 16], F32, st)
    p.dma("sp", nu_l[:], p.dram["nu_log"])
    p.dma("sp", th_l[:], p.dram["theta_log"])
    p.dma("sp", Dv[:], p.dram["l0a_D"])
    p.barrier()
    nu = p.sbuf("l0a_nu", [128, 2], F32, st)
    tmp2 = p.sbuf("l0a_tmp2", [128, 2], F32, st)
    emit_softplus(p, nu, nu_l, tmp2, "l0a")
    mag = p.sbuf("l0a_mag", [128, 2], F32, st)
    theta = p.sbuf("l0a_theta", [128, 2], F32, st)
    gam = p.sbuf("l0a_gam", [128, 2], F32, st)
    p.emit("act", lambda e: e.activation(out=mag[:], in_=nu[:], func=AF.Exp))
    p.emit("act", lambda e: e.activation(out=theta[:], in_=th_l[:], func=AF.Exp))
    p.barrier()
    p.emit("act", lambda e: e.activation(out=mag[:], in_=mag[:], func=AF.Exp, scale=-1.0))
    p.barrier()
    p.emit("act", lambda e: e.activation(out=gam[:], in_=mag[:], func=AF.Square))
    p.barrier()
    p.emit("dve", lambda e: e.tensor_scalar(out=gam[:], in0=gam[:], scalar1=-1.0, scalar2=1.0,
                                            op0=ALU.mult, op1=ALU.add))
    p.barrier()
    p.emit("dve", lambda e: e.tensor_scalar(out=gam[:], in0=gam[:], scalar1=0.0, op0=ALU.max,
                                            scalar2=None))
    p.barrier()
    p.emit("act", lambda e: e.activation(out=gam[:], in_=gam[:], func=AF.Sqrt))
    p.barrier()

    iota128 = p.sb["cst_iota128"]
    iota16 = p.sb["cst_iota16"]
    ang_j = p.sbuf("l0a_angj", [128, 128], F32, st)
    itmp = p.sbuf("l0a_itmp", [128, 128], I32, st)
    ftmp = p.sbuf("l0a_ftmp", [128, 128], F32, st)
    cj = p.sbuf("l0a_cj", [128, 128], F32, st)
    sj = p.sbuf("l0a_sj", [128, 128], F32, st)
    ang_c = p.sbuf("l0a_angc", [128, 16], F32, st)
    itmp16 = p.sbuf("l0a_itmp16", [128, 16], I32, st)
    ftmp16 = p.sbuf("l0a_ftmp16", [128, 16], F32, st)
    cc = p.sbuf("l0a_cc", [128, 16], F32, st)
    sc = p.sbuf("l0a_sc", [128, 16], F32, st)
    th128 = p.sbuf("l0a_th128", [128, 2], F32, st)
    itmp2 = p.sbuf("l0a_itmp2", [128, 2], I32, st)
    p.emit("dve", lambda e: e.tensor_scalar(out=th128[:], in0=theta[:], scalar1=128.0,
                                            op0=ALU.mult, scalar2=None))
    p.barrier()
    emit_mod_2pi(p, th128, th128, itmp2, tmp2, "l0a")
    tabs = {}
    s1 = p.sbuf("l0a_s1", [128, T], F32, st)
    s2 = p.sbuf("l0a_s2", [128, T], F32, st)
    for ns in range(NS):
        for nm in ("re", "im"):
            tabs[(ns, nm)] = p.sbuf(f"l0a_tab{ns}{nm}", [128, T], F32, st)
        thc = theta[:, ns:ns + 1]
        p.emit("dve", lambda e, thc=thc: e.tensor_scalar(
            out=ang_j[:], in0=iota128[:], scalar1=thc, op0=ALU.mult, scalar2=None))
        p.barrier()
        emit_mod_2pi(p, ang_j, ang_j, itmp, ftmp, "l0a")
        p.emit("dve", lambda e: e.tensor_copy(out=sj[:], in_=ang_j[:]))
        p.barrier()
        emit_fold_to_pi(p, sj, ftmp, "l0a")
        p.emit("act", lambda e: e.activation(out=sj[:], in_=sj[:], func=AF.Sin))
        p.barrier()
        p.emit("dve", lambda e: e.tensor_scalar(out=cj[:], in0=ang_j[:], scalar1=-1.0,
                                                scalar2=PI / 2, op0=ALU.mult, op1=ALU.add))
        p.barrier()
        p.emit("dve", lambda e: e.tensor_scalar(out=ftmp[:], in0=cj[:], scalar1=-PI,
                                                op0=ALU.is_lt, scalar2=None))
        p.barrier()
        p.emit("dve", lambda e: e.scalar_tensor_tensor(out=cj[:], in0=ftmp[:], scalar=TWO_PI,
                                                       op0=ALU.mult, op1=ALU.add, in1=cj[:]))
        p.barrier()
        p.emit("act", lambda e: e.activation(out=cj[:], in_=cj[:], func=AF.Sin))
        p.barrier()
        th128c = th128[:, ns:ns + 1]
        p.emit("dve", lambda e, th128c=th128c: e.tensor_scalar(
            out=ang_c[:], in0=iota16[:], scalar1=th128c, op0=ALU.mult, scalar2=None))
        p.barrier()
        emit_mod_2pi(p, ang_c, ang_c, itmp16, ftmp16, "l0a")
        p.emit("dve", lambda e: e.tensor_copy(out=sc[:], in_=ang_c[:]))
        p.barrier()
        emit_fold_to_pi(p, sc, ftmp16, "l0a")
        p.emit("act", lambda e: e.activation(out=sc[:], in_=sc[:], func=AF.Sin))
        p.barrier()
        p.emit("dve", lambda e: e.tensor_scalar(out=cc[:], in0=ang_c[:], scalar1=-1.0,
                                                scalar2=PI / 2, op0=ALU.mult, op1=ALU.add))
        p.barrier()
        p.emit("dve", lambda e: e.tensor_scalar(out=ftmp16[:], in0=cc[:], scalar1=-PI,
                                                op0=ALU.is_lt, scalar2=None))
        p.barrier()
        p.emit("dve", lambda e: e.scalar_tensor_tensor(out=cc[:], in0=ftmp16[:], scalar=TWO_PI,
                                                       op0=ALU.mult, op1=ALU.add, in1=cc[:]))
        p.barrier()
        p.emit("act", lambda e: e.activation(out=cc[:], in_=cc[:], func=AF.Sin))
        p.barrier()
        Tre, Tim = tabs[(ns, "re")], tabs[(ns, "im")]
        for c in range(16):
            sl = slice(c * 128, (c + 1) * 128)
            ccc = cc[:, c:c + 1]
            scc = sc[:, c:c + 1]
            p.emit("dve", lambda e, sl=sl, ccc=ccc, Tre=Tre: e.tensor_scalar(
                out=Tre[:, sl], in0=cj[:], scalar1=ccc, op0=ALU.mult, scalar2=None))
            p.emit("dve", lambda e, sl=sl, ccc=ccc, Tim=Tim: e.tensor_scalar(
                out=Tim[:, sl], in0=sj[:], scalar1=ccc, op0=ALU.mult, scalar2=None))
            p.emit("dve", lambda e, sl=sl, scc=scc: e.tensor_scalar(
                out=s1[:, sl], in0=sj[:], scalar1=scc, op0=ALU.mult, scalar2=None))
            p.emit("dve", lambda e, sl=sl, scc=scc: e.tensor_scalar(
                out=s2[:, sl], in0=cj[:], scalar1=scc, op0=ALU.mult, scalar2=None))
        p.barrier()
        p.emit("dve", lambda e, Tre=Tre: e.tensor_tensor(out=Tre[:], in0=Tre[:], in1=s1[:],
                                                         op=ALU.subtract))
        p.emit("dve", lambda e, Tim=Tim: e.tensor_tensor(out=Tim[:], in0=Tim[:], in1=s2[:],
                                                         op=ALU.add))
        p.barrier()

    BU = {}
    for nm in ("re", "im"):
        for ns in range(NS):
            BU[(ns, nm)] = p.sbuf(f"l0a_bu{ns}{nm}", [128, T], F32, st)
    bst = p.phase_stack()
    bre = [p.sbuf(f"l0a_bre{k}", [128, 256], BF16, bst) for k in range(ND)]
    bim = [p.sbuf(f"l0a_bim{k}", [128, 256], BF16, bst) for k in range(ND)]
    for k in range(ND):
        p.dma("pool", bre[k][:], p.dram["BreT"][k * 128:(k + 1) * 128, :])
        p.dma("pool", bim[k][:], p.dram["BimT"][k * 128:(k + 1) * 128, :])
    xnt = [p.sbuf(f"l0a_xnt{j}", [128, TCH], BF16, bst) for j in range(2 * NT)]
    banks = [p.psum(f"l0a_psb{i}", stack=bst) for i in range(8)]
    p.barrier()
    xn_tm = p.dram["XN_TM"]
    for nm, btiles in (("re", bre), ("im", bim)):
        for k in range(ND):
            for tc in range(NT):
                j = (k % 2) * NT + tc
                p.dma_t(xnt[j][:], xn_tm[tc * TCH:(tc + 1) * TCH, k * 128:(k + 1) * 128])
            p.barrier()

            def mmk(e, btiles=btiles, k=k):
                for ns in range(NS):
                    for tc in range(NT):
                        j = (k % 2) * NT + tc
                        instr = e.matmul(banks[ns * NT + tc][:],
                                         btiles[k][:, ns * 128:(ns + 1) * 128],
                                         xnt[j][:], start=(k == 0), stop=(k == ND - 1))
                return instr
            p.emit("pe", mmk)
            p.barrier()
        for ns in range(NS):
            gc = gam[:, ns:ns + 1]
            bu = BU[(ns, nm)]
            for tc in range(NT):
                tsl = slice(tc * TCH, (tc + 1) * TCH)
                p.emit("dve", lambda e, bu=bu, tsl=tsl, gc=gc, b=banks[ns * NT + tc]:
                       e.tensor_scalar(out=bu[:, tsl], in0=b[:], scalar1=gc,
                                       op0=ALU.mult, scalar2=None))
        p.barrier()
    bst.close()

    for ns in range(NS):
        Tre, Tim = tabs[(ns, "re")], tabs[(ns, "im")]
        bre_, bim_ = BU[(ns, "re")], BU[(ns, "im")]
        p.emit("dve", lambda e, Tim=Tim, bim_=bim_: e.tensor_tensor(
            out=s1[:], in0=Tim[:], in1=bim_[:], op=ALU.mult))
        p.emit("dve", lambda e, Tim=Tim, bre_=bre_: e.tensor_tensor(
            out=s2[:], in0=Tim[:], in1=bre_[:], op=ALU.mult))
        p.barrier()
        p.emit("dve", lambda e, Tre=Tre, bre_=bre_: e.tensor_tensor(
            out=bre_[:], in0=Tre[:], in1=bre_[:], op=ALU.mult))
        p.barrier()
        p.emit("dve", lambda e, bre_=bre_: e.tensor_tensor(
            out=bre_[:], in0=bre_[:], in1=s1[:], op=ALU.add))
        p.emit("dve", lambda e, Tre=Tre, bim_=bim_: e.tensor_tensor(
            out=bim_[:], in0=Tre[:], in1=bim_[:], op=ALU.mult))
        p.barrier()
        p.emit("dve", lambda e, bim_=bim_: e.tensor_tensor(
            out=bim_[:], in0=bim_[:], in1=s2[:], op=ALU.subtract))
        p.barrier()

    mt = p.sbuf("l0a_mt", [128, T], F32, st)
    G = {}
    for nm in ("re", "im"):
        for ns in range(NS):
            G[(ns, nm)] = p.sbuf(f"l0a_g{ns}{nm}", [128, T], BF16, st)
    for ns in range(NS):
        p.emit("dve", lambda e: e.memset(mt[:], 1.0))
        p.barrier()
        mc = mag[:, ns:ns + 1]
        p.emit("dve", lambda e, mc=mc: e.tensor_scalar(
            out=mt[:], in0=mt[:], scalar1=mc, op0=ALU.mult, scalar2=None))
        p.barrier()
        for nm in ("re", "im"):
            g_ = G[(ns, nm)]
            v_ = BU[(ns, nm)]
            p.emit("dve", lambda e, g_=g_, v_=v_: e.tensor_tensor_scan(
                out=g_[:], data0=mt[:], data1=v_[:], initial=0.0,
                op0=ALU.mult, op1=ALU.add))
            p.barrier()

    H = {}
    for nm in ("re", "imN"):
        for ns in range(NS):
            H[(ns, nm)] = p.sbuf(f"l0a_h{ns}{nm}", [128, T], BF16, st)
    for ns in range(NS):
        Tre, Tim = tabs[(ns, "re")], tabs[(ns, "im")]
        gre, gim = G[(ns, "re")], G[(ns, "im")]
        p.emit("dve", lambda e, Tre=Tre, gre=gre: e.tensor_tensor(
            out=s1[:], in0=Tre[:], in1=gre[:], op=ALU.mult))
        p.emit("dve", lambda e, Tim=Tim, gim=gim: e.tensor_tensor(
            out=s2[:], in0=Tim[:], in1=gim[:], op=ALU.mult))
        p.barrier()
        hre = H[(ns, "re")]
        p.emit("dve", lambda e, hre=hre: e.tensor_tensor(
            out=hre[:], in0=s1[:], in1=s2[:], op=ALU.subtract))
        p.barrier()
        p.emit("dve", lambda e, Tre=Tre, gim=gim: e.tensor_tensor(
            out=s1[:], in0=Tre[:], in1=gim[:], op=ALU.mult))
        p.emit("dve", lambda e, Tim=Tim, gre=gre: e.tensor_tensor(
            out=s2[:], in0=Tim[:], in1=gre[:], op=ALU.mult))
        p.barrier()
        p.emit("dve", lambda e: e.tensor_tensor(
            out=s1[:], in0=s1[:], in1=s2[:], op=ALU.add))
        p.barrier()
        hin = H[(ns, "imN")]
        p.emit("dve", lambda e, hin=hin: e.tensor_scalar(
            out=hin[:], in0=s1[:], scalar1=-1.0, op0=ALU.mult, scalar2=None))
        p.barrier()

    yst = p.phase_stack()
    cre = [p.sbuf(f"l0a_cre{ns}", [128, D], BF16, yst) for ns in range(NS)]
    cim = [p.sbuf(f"l0a_cim{ns}", [128, D], BF16, yst) for ns in range(NS)]
    for ns in range(NS):
        p.dma("pool", cre[ns][:], p.dram["CreT"][ns * 128:(ns + 1) * 128, :])
        p.dma("pool", cim[ns][:], p.dram["CimT"][ns * 128:(ns + 1) * 128, :])
    ot = p.sbuf("l0a_ot", [128, T], BF16, yst)
    xnk = [p.sbuf(f"l0a_yxn{j}", [128, T], BF16, yst) for j in range(2)]
    banks2 = [p.psum(f"l0a_yps{i}", stack=yst) for i in range(4)]
    p.barrier()
    for k in range(ND):
        xb = xnk[k % 2]
        p.dma_t(xb[:], xn_tm[0:T, k * 128:(k + 1) * 128])
        for tc in range(NT):
            tsl = slice(tc * TCH, (tc + 1) * TCH)

            def ymm(e, k=k, tsl=tsl, pb=banks2[tc]):
                first = True
                for ns in range(NS):
                    e.matmul(pb[:], cre[ns][:, k * 128:(k + 1) * 128],
                             H[(ns, "re")][:, tsl], start=first, stop=False)
                    first = False
                    instr = e.matmul(pb[:], cim[ns][:, k * 128:(k + 1) * 128],
                                     H[(ns, "imN")][:, tsl], start=False, stop=(ns == NS - 1))
                return instr
            p.emit("pe", ymm)
        p.barrier()
        dc = Dv[:, k:k + 1]
        for tc in range(NT):
            tsl = slice(tc * TCH, (tc + 1) * TCH)
            p.emit("dve", lambda e, tsl=tsl, dc=dc, xb=xb, pb=banks2[tc]:
                   e.scalar_tensor_tensor(out=ot[:, tsl], in0=xb[:, tsl], scalar=dc,
                                          op0=ALU.mult, op1=ALU.add, in1=pb[:]))
        p.barrier()
        p.dma("sp", out_fm[k * 128:(k + 1) * 128, :], ot[:])
        p.barrier()
    yst.close()
    st.close()



def load_block_kpj(p, dst, src2d, K):
    p.dma("pool", dst[:].rearrange("p (k j) -> p k j", j=128),
          src2d.rearrange("(k p) j -> p k j", p=128))


def emit_l0f(p, out_fm):
    st = p.phase_stack()
    banks = [p.psum(f"l0f_ps{i}", stack=st) for i in range(8)]
    xn_fm = [p.sbuf(f"l0f_xn{k}", [128, T], BF16, st) for k in range(ND)]
    cb8 = p.sbuf("l0f_cb8", [8, 128], F32, st)
    SGRb = p.sbuf("l0f_sgr", [1, T], BF16, st)
    cbr_d = p.dram["CBR_D"]
    out_acc = p.dram["OUT_ACC"]
    emit_load_xn_fm(p, p.dram["XN_TM"], xn_fm, "l0f")

    rst = p.phase_stack()
    gwt = p.sbuf("l0f_gwt", [128, ND * 8], BF16, rst)
    shgb = p.sbuf("l0f_shgb", [128, 16], BF16, rst)
    probs = p.sbuf("l0f_probs", [128, 8], F32, rst)
    srt = p.sbuf("l0f_srt", [128, 8], F32, rst)
    idx = p.sbuf("l0f_idx", [128, 8], U32, rst)
    cmb = p.sbuf("l0f_cmb", [128, 8], F32, rst)
    msk = p.sbuf("l0f_msk", [128, 8], F32, rst)
    m1 = p.sbuf("l0f_m1", [128, 1], F32, rst)
    w1 = p.sbuf("l0f_w1", [128, 1], F32, rst)
    w2 = p.sbuf("l0f_w2", [128, 1], F32, rst)
    i1f = p.sbuf("l0f_i1f", [128, 1], F32, rst)
    i2f = p.sbuf("l0f_i2f", [128, 1], F32, rst)
    for k in range(ND):
        p.dma("pool", gwt[:, k * 8:(k + 1) * 8], p.dram["gwT"][k * 128:(k + 1) * 128, :])
    p.dma("pool", shgb[:], p.dram["shg"])
    iota8 = p.sb["cst_iota8"]
    ident = p.sb["cst_identity128"]
    p.barrier()
    for i in range(NTT):
        tsl128 = slice(i * 128, (i + 1) * 128)
        pb = banks[0]

        def logit_mm(e, tsl128=tsl128, pb=pb):
            for k in range(ND):
                instr = e.matmul(pb[:, 0:8], xn_fm[k][:, tsl128], gwt[:, k * 8:(k + 1) * 8],
                                 start=(k == 0), stop=(k == ND - 1))
            return instr
        p.emit("pe", logit_mm)
        p.barrier()
        p.emit("dve", lambda e, pb=pb: e.reduce_max(out=m1[:], in_=pb[:, 0:8], axis=AX.X))
        p.barrier()
        p.emit("dve", lambda e: e.tensor_scalar(out=m1[:], in0=m1[:], scalar1=-1.0,
                                                op0=ALU.mult, scalar2=None))
        p.barrier()
        p.emit("act", lambda e, pb=pb: e.activation(out=probs[:], in_=pb[:, 0:8],
                                                    func=AF.Exp, bias=m1[:]))
        p.barrier()
        p.emit("dve", lambda e: e.reduce_sum(out=w1[:], in_=probs[:], axis=AX.X))
        p.barrier()
        p.emit("dve", lambda e: e.reciprocal(out=w2[:], in_=w1[:]))
        p.barrier()
        p.emit("dve", lambda e: e.tensor_scalar(out=probs[:], in0=probs[:], scalar1=w2[:],
                                                op0=ALU.mult, scalar2=None))
        p.barrier()
        p.emit("dve", lambda e: e.max(out=srt[:], in_=probs[:]))
        p.barrier()
        p.emit("dve", lambda e: e.max_index(out=idx[:], in_max=srt[:], in_values=probs[:]))
        p.barrier()
        p.emit("dve", lambda e: e.tensor_tensor(out=w1[:], in0=srt[:, 0:1], in1=srt[:, 1:2],
                                                op=ALU.add))
        p.barrier()
        p.emit("dve", lambda e: e.reciprocal(out=w1[:], in_=w1[:]))
        p.barrier()
        p.emit("dve", lambda e: e.tensor_scalar(out=w2[:], in0=srt[:, 1:2], scalar1=w1[:],
                                                op0=ALU.mult, scalar2=None))
        p.barrier()
        p.emit("dve", lambda e: e.tensor_scalar(out=w1[:], in0=srt[:, 0:1], scalar1=w1[:],
                                                op0=ALU.mult, scalar2=None))
        p.barrier()
        p.emit("dve", lambda e: e.tensor_copy(out=i1f[:], in_=idx[:, 0:1]))
        p.emit("dve", lambda e: e.tensor_copy(out=i2f[:], in_=idx[:, 1:2]))
        p.barrier()
        p.emit("dve", lambda e: e.tensor_scalar(out=cmb[:], in0=iota8[:], scalar1=i1f[:],
                                                op0=ALU.is_equal, scalar2=None))
        p.emit("dve", lambda e: e.tensor_scalar(out=msk[:], in0=iota8[:], scalar1=i2f[:],
                                                op0=ALU.is_equal, scalar2=None))
        p.barrier()
        p.emit("dve", lambda e: e.tensor_scalar(out=cmb[:], in0=cmb[:], scalar1=w1[:],
                                                op0=ALU.mult, scalar2=None))
        p.barrier()
        p.emit("dve", lambda e: e.scalar_tensor_tensor(out=cmb[:], in0=msk[:], scalar=w2[:],
                                                       op0=ALU.mult, op1=ALU.add, in1=cmb[:]))
        p.barrier()
        p.emit("pe", lambda e, pb=banks[1]: e.transpose(pb[0:8, 0:128], cmb[:], ident[:]))
        p.barrier()
        p.emit("dve", lambda e, pb=banks[1]: e.tensor_copy(
            out=cb8[:], in_=pb[0:8, 0:128]))
        p.barrier()
        p.dma("sp", cbr_d[0:8, tsl128], cb8[:])
        p.barrier()
    for tc in range(NT):
        tsl = slice(tc * TCH, (tc + 1) * TCH)
        pb = banks[0]

        def sg_mm(e, tsl=tsl, pb=pb):
            for k in range(ND):
                instr = e.matmul(pb[0:1, :], shgb[:, k:k + 1], xn_fm[k][:, tsl],
                                 start=(k == 0), stop=(k == ND - 1))
            return instr
        p.emit("pe", sg_mm)
        p.barrier()
        p.emit("act", lambda e, tsl=tsl, pb=pb: e.activation(
            out=SGRb[0:1, tsl], in_=pb[0:1, :], func=AF.Sigmoid))
        p.barrier()
    rst.close()

    ones1 = p.sb["cst_ones1x128"]
    for e in range(9):
        shared = (e == 8)
        NHC = 22 if shared else 11
        est = p.phase_stack()
        H = p.sbuf(f"l0f_h{e}", [128, NHC * T], BF16, est)
        cbt = p.sbuf(f"l0f_cbt{e}", [128, T], BF16, est)
        gs_t = p.sbuf(f"l0f_gs{e}", [128, TCH], BF16, est)
        t1_t = p.sbuf(f"l0f_t1{e}", [128, TCH], BF16, est)
        if shared:
            crow = SGRb
        else:
            crow = p.sbuf(f"l0f_cbrow{e}", [1, T], BF16, est)
            p.dma("pool", crow[:], cbr_d[e:e + 1, :])
            p.barrier()
        for tc in range(NT):
            tsl = slice(tc * TCH, (tc + 1) * TCH)
            pb = banks[4 + tc]
            p.emit("pe", lambda e_, tsl=tsl, pb=pb, crow=crow: e_.matmul(
                pb[:], ones1[:], crow[0:1, tsl], start=True, stop=True))
        p.barrier()
        for tc in range(NT):
            tsl = slice(tc * TCH, (tc + 1) * TCH)
            pb = banks[4 + tc]
            p.emit("act", lambda e_, tsl=tsl, pb=pb, cbt=cbt: e_.activation(
                out=cbt[:, tsl], in_=pb[:], func=AF.Copy))
        p.barrier()
        wst = p.phase_stack()
        if shared:
            wgb = [p.sbuf(f"l0f_wgb{e}_{j}", [128, ND * 128], BF16, wst) for j in range(2)]
            wgt = None
        else:
            wgt = p.sbuf(f"l0f_wgt{e}", [128, ND * 1408], BF16, wst)
            wgb = None
            for k in range(ND):
                p.dma("pool", wgt[:, k * 1408:(k + 1) * 1408],
                      p.dram["moe_wg"][e, k * 128:(k + 1) * 128, :])
        wub = [p.sbuf(f"l0f_wub{e}_{j}", [128, ND * 128], BF16, wst) for j in range(2)]
        p.barrier()
        for hc in range(NHC):
            if shared:
                load_block_kpj(p, wgb[hc % 2], p.dram["shwg_b"][hc], ND)
                load_block_kpj(p, wub[hc % 2], p.dram["shwu_b"][hc], ND)
            else:
                load_block_kpj(p, wub[hc % 2], p.dram["moe_wu_hc"][e, hc], ND)
            p.barrier()
            for tc in range(NT):
                tsl = slice(tc * TCH, (tc + 1) * TCH)
                bg = banks[(tc % 2) * 2]
                bu = banks[(tc % 2) * 2 + 1]

                def gu_mm(e_, hc=hc, tsl=tsl, bg=bg, bu=bu, shared=shared,
                          wgt=wgt, wgb=wgb, wub=wub):
                    for k in range(ND):
                        if shared:
                            lg = wgb[hc % 2][:, k * 128:(k + 1) * 128]
                        else:
                            lg = wgt[:, k * 1408 + hc * 128: k * 1408 + (hc + 1) * 128]
                        e_.matmul(bg[:], lg, xn_fm[k][:, tsl],
                                  start=(k == 0), stop=(k == ND - 1))
                    for k in range(ND):
                        lu = wub[hc % 2][:, k * 128:(k + 1) * 128]
                        instr = e_.matmul(bu[:], lu, xn_fm[k][:, tsl],
                                          start=(k == 0), stop=(k == ND - 1))
                    return instr
                p.emit("pe", gu_mm)
                p.barrier()
                p.emit("act", lambda e_, bg=bg, gs_t=gs_t: e_.activation(
                    out=gs_t[:], in_=bg[:], func=AF.Silu))
                p.emit("dve", lambda e_, bu=bu, tsl=tsl, t1_t=t1_t, cbt=cbt: e_.tensor_tensor(
                    out=t1_t[:], in0=bu[:], in1=cbt[:, tsl], op=ALU.mult))
                p.barrier()
                p.emit("dve", lambda e_, hc=hc, tc=tc, H=H, t1_t=t1_t, gs_t=gs_t:
                       e_.tensor_tensor(
                    out=H[:, hc * T + tc * TCH: hc * T + (tc + 1) * TCH],
                    in0=t1_t[:], in1=gs_t[:], op=ALU.mult))
                p.barrier()
        wst.close()
        dst_ = p.phase_stack()
        if shared:
            wdb = [p.sbuf(f"l0f_wdb{e}_{j}", [128, NHC * 128], BF16, dst_) for j in range(2)]
            wdt = None
        else:
            wdt = p.sbuf(f"l0f_wdt{e}", [128, NHC * T], BF16, dst_)
            wdb = None
            for hc in range(NHC):
                p.dma("pool", wdt[:, hc * T:(hc + 1) * T],
                      p.dram["moe_wd"][e, hc * 128:(hc + 1) * 128, :])
        ot32 = p.sbuf(f"l0f_ot{e}", [128, T], F32, dst_)
        p.barrier()
        for dk in range(ND):
            if shared:
                load_block_kpj(p, wdb[dk % 2], p.dram["shwd_b"][dk], NHC)
                p.barrier()
            for tc in range(NT):
                tsl = slice(tc * TCH, (tc + 1) * TCH)
                pb = banks[4 + ((dk % 2) * 2) + (tc % 2)]

                def dn_mm(e_, dk=dk, tsl=tsl, tc=tc, pb=pb, shared=shared,
                          wdt=wdt, wdb=wdb, NHC=NHC, H=H):
                    for hc in range(NHC):
                        if shared:
                            lw = wdb[dk % 2][:, hc * 128:(hc + 1) * 128]
                        else:
                            lw = wdt[:, hc * T + dk * 128: hc * T + dk * 128 + 128]
                        instr = e_.matmul(pb[:], lw,
                                          H[:, hc * T + tc * TCH: hc * T + (tc + 1) * TCH],
                                          start=(hc == 0), stop=(hc == NHC - 1))
                    return instr
                p.emit("pe", dn_mm)
                p.barrier()
                if tc % 2 == 0:
                    p.emit("act", lambda e_, tsl=tsl, pb=pb, ot32=ot32: e_.activation(
                        out=ot32[:, tsl], in_=pb[:], func=AF.Copy))
                else:
                    p.emit("dve", lambda e_, tsl=tsl, pb=pb, ot32=ot32: e_.tensor_copy(
                        out=ot32[:, tsl], in_=pb[:]))
                p.barrier()
            if e == 0:
                p.dma("pool", out_acc[dk * 128:(dk + 1) * 128, :], ot32[:])
            else:
                p.dma("pool", out_acc[dk * 128:(dk + 1) * 128, :], ot32[:],
                      accum_op=ALU.add)
            p.barrier()
        dst_.close()
        est.close()

    cst_ = p.phase_stack()
    cv32 = [p.sbuf(f"l0f_cv32{j}", [128, T], F32, cst_) for j in range(2)]
    cv16 = [p.sbuf(f"l0f_cv16{j}", [128, T], BF16, cst_) for j in range(2)]
    for k in range(ND):
        j = k % 2
        p.dma("sp", cv32[j][:], out_acc[k * 128:(k + 1) * 128, :])
        p.barrier()
        p.emit("act", lambda e, j=j: e.activation(out=cv16[j][:], in_=cv32[j][:],
                                                  func=AF.Copy))
        p.barrier()
        p.dma("sp", out_fm[k * 128:(k + 1) * 128, :], cv16[j][:])
    p.barrier()
    cst_.close()
    st.close()




PADT = T + 4
NF = 48

def emit_l1a(p, x1_fm, mlp_fm):
    st = p.phase_stack()
    banks = [p.psum(f"l1a_ps{i}", stack=st) for i in range(8)]
    xn_tm = p.dram["XN_TM"]
    eps8 = p.sb["cst_c_eps8"]
    ones1 = p.sb["cst_ones1x128"]
    ones128 = p.sb["cst_ones128x1"]

    n1w = p.sbuf("l1a_n1w", [128, 16], F32, st)
    n2w = p.sbuf("l1a_n2w", [128, 16], F32, st)
    lb = p.sbuf("l1a_lb", [128, 16], F32, st)
    rb = p.sbuf("l1a_rb", [128, 16], F32, st)
    cb = p.sbuf("l1a_cb", [128, 16], F32, st)
    gab = p.sbuf("l1a_gab", [128, 16], F32, st)
    gxb = p.sbuf("l1a_gxb", [128, 16], F32, st)
    ob = p.sbuf("l1a_ob", [128, 16], F32, st)
    u1b = p.sbuf("l1a_u1b", [128, 48], F32, st)
    u2b = p.sbuf("l1a_u2b", [128, 48], F32, st)
    dnb = p.sbuf("l1a_dnb", [128, 16], F32, st)
    apar = p.sbuf("l1a_ap", [128, 16], F32, st)
    c8 = p.sbuf("l1a_c8", [128, 16], F32, st)
    cw = p.sbuf("l1a_cw", [128, 64], F32, st)
    for nm, tile in [("n1w", n1w), ("n2w", n2w), ("left_b", lb), ("right_b", rb),
                     ("conv_b", cb), ("ga_b", gab), ("gx_b", gxb), ("out_b", ob),
                     ("up1_b", u1b), ("up2_b", u2b), ("down_b", dnb), ("a_param", apar),
                     ("conv_w", cw)]:
        p.dma("sp", tile[:], p.dram[nm])
    p.barrier()
    tmp16 = p.sbuf("l1a_tmp16", [128, 16], F32, st)
    emit_softplus(p, c8, apar, tmp16, "l1a")
    p.emit("dve", lambda e: e.tensor_scalar(out=c8[:], in0=c8[:], scalar1=-8.0,
                                            op0=ALU.mult, scalar2=None))
    p.barrier()

    rr1 = p.sbuf("l1a_rr1", [1, T], F32, st)
    rr2 = p.sbuf("l1a_rr2", [1, T], F32, st)
    rr1b = p.sbuf("l1a_rr1b", [1, T], BF16, st)
    rr2b = p.sbuf("l1a_rr2b", [1, T], BF16, st)
    rr1t = p.sbuf("l1a_rr1t", [128, T], BF16, st)
    rr2t = p.sbuf("l1a_rr2t", [128, T], BF16, st)

    sst = p.phase_stack()
    xs = [p.sbuf(f"l1a_xs{j}", [128, T], BF16, sst) for j in range(2)]
    sq = [p.sbuf(f"l1a_sq{j}", [128, T], BF16, sst) for j in range(2)]
    for k in range(ND):
        j = k % 2
        p.dma_t(xs[j][:], xn_tm[0:T, k * 128:(k + 1) * 128])
        p.barrier()
        p.emit("dve", lambda e, j=j: e.tensor_tensor(out=sq[j][:], in0=xs[j][:],
                                                     in1=xs[j][:], op=ALU.mult))
        p.barrier()

        def sqmm(e, j=j, k=k):
            for tc in range(NT):
                instr = e.matmul(banks[4 + tc][0:1, :], ones128[:],
                                 sq[j][:, tc * TCH:(tc + 1) * TCH],
                                 start=(k == 0), stop=(k == ND - 1))
            return instr
        p.emit("pe", sqmm)
        p.barrier()
    for tc in range(NT):
        tsl = slice(tc * TCH, (tc + 1) * TCH)
        p.emit("act", lambda e, tsl=tsl, pb=banks[4 + tc]: e.activation(
            out=rr1[0:1, tsl], in_=pb[0:1, :], func=AF.Ln, scale=1.0 / D,
            bias=eps8[0:1, :]))
    p.barrier()
    p.emit("act", lambda e: e.activation(out=rr1[:], in_=rr1[:], func=AF.Exp, scale=-0.5))
    p.barrier()
    p.emit("act", lambda e: e.activation(out=rr1b[:], in_=rr1[:], func=AF.Copy))
    p.barrier()
    for tc in range(NT):
        tsl = slice(tc * TCH, (tc + 1) * TCH)
        p.emit("pe", lambda e, tsl=tsl, pb=banks[tc]: e.matmul(
            pb[:], ones1[:], rr1b[0:1, tsl], start=True, stop=True))
    p.barrier()
    for tc in range(NT):
        tsl = slice(tc * TCH, (tc + 1) * TCH)
        p.emit("act", lambda e, tsl=tsl, pb=banks[tc]: e.activation(
            out=rr1t[:, tsl], in_=pb[:], func=AF.Copy))
    p.barrier()
    sst.close()

    rlst = p.phase_stack()
    RT = p.sbuf("l1a_rt", [128, ND * PADT], BF16, rlst)
    LEFT = p.sbuf("l1a_left", [128, ND * T], BF16, rlst)
    p.emit("dve", lambda e: e.memset(RT[:], 0.0))
    p.barrier()

    pst = p.phase_stack()
    tn = [p.sbuf(f"l1a_tn{k}", [128, TCH], BF16, pst) for k in range(ND)]
    wrb = [p.sbuf(f"l1a_wrb{j}", [128, ND * 128], BF16, pst) for j in range(2)]
    wlb = [p.sbuf(f"l1a_wlb{j}", [128, ND * 128], BF16, pst) for j in range(2)]
    for tc in range(NT):
        tsl = slice(tc * TCH, (tc + 1) * TCH)
        for k in range(ND):
            p.dma_t(tn[k][:], xn_tm[tsl, k * 128:(k + 1) * 128])
        p.barrier()
        for k in range(ND):
            p.emit("dve", lambda e, k=k, tsl=tsl: e.scalar_tensor_tensor(
                out=tn[k][:], in0=tn[k][:], scalar=n1w[:, k:k + 1], op0=ALU.mult,
                op1=ALU.mult, in1=rr1t[:, tsl]))
        p.barrier()
        for m in range(ND):
            j = m % 2
            load_block_kpj(p, wrb[j], p.dram["rightW_b"][m], ND)
            load_block_kpj(p, wlb[j], p.dram["leftW_b"][m], ND)
            p.barrier()
            br = banks[(m % 2) * 2]
            bl = banks[(m % 2) * 2 + 1]

            def rl_mm(e, j=j, br=br, bl=bl):
                for k in range(ND):
                    e.matmul(br[:], wrb[j][:, k * 128:(k + 1) * 128], tn[k][:],
                             start=(k == 0), stop=(k == ND - 1))
                for k in range(ND):
                    instr = e.matmul(bl[:], wlb[j][:, k * 128:(k + 1) * 128], tn[k][:],
                                     start=(k == 0), stop=(k == ND - 1))
                return instr
            p.emit("pe", rl_mm)
            p.barrier()
            rt_off = m * PADT + 4 + tc * TCH
            p.emit("dve", lambda e, br=br, m=m, rt_off=rt_off: e.tensor_scalar(
                out=RT[:, rt_off:rt_off + TCH], in0=br[:], scalar1=rb[:, m:m + 1],
                op0=ALU.add, scalar2=None))
            p.emit("act", lambda e, bl=bl, m=m, tc=tc: e.activation(
                out=LEFT[:, m * T + tc * TCH: m * T + (tc + 1) * TCH], in_=bl[:],
                func=AF.Gelu, bias=lb[:, m:m + 1]))
            p.barrier()
    pst.close()

    cst_ = p.phase_stack()
    acc1 = p.sbuf("l1a_acc1", [128, T], F32, cst_)
    acc2 = p.sbuf("l1a_acc2", [128, T], F32, cst_)
    for m in range(ND):
        base = m * PADT
        p.emit("dve", lambda e, base=base, m=m: e.tensor_scalar(
            out=acc1[:], in0=RT[:, base + 1: base + 1 + T], scalar1=cw[:, m * 4:m * 4 + 1],
            op0=ALU.mult, scalar2=None))
        p.barrier()
        p.emit("dve", lambda e, base=base, m=m: e.scalar_tensor_tensor(
            out=acc2[:], in0=RT[:, base + 2: base + 2 + T], scalar=cw[:, m * 4 + 1:m * 4 + 2],
            op0=ALU.mult, op1=ALU.add, in1=acc1[:]))
        p.barrier()
        p.emit("dve", lambda e, base=base, m=m: e.scalar_tensor_tensor(
            out=acc1[:], in0=RT[:, base + 3: base + 3 + T], scalar=cw[:, m * 4 + 2:m * 4 + 3],
            op0=ALU.mult, op1=ALU.add, in1=acc2[:]))
        p.barrier()
        p.emit("dve", lambda e, base=base, m=m: e.scalar_tensor_tensor(
            out=acc2[:], in0=RT[:, base + 4: base + 4 + T], scalar=cw[:, m * 4 + 3:m * 4 + 4],
            op0=ALU.mult, op1=ALU.add, in1=acc1[:]))
        p.barrier()
        p.emit("dve", lambda e, base=base, m=m: e.tensor_scalar(
            out=RT[:, base + 4: base + 4 + T], in0=acc2[:], scalar1=cb[:, m:m + 1],
            op0=ALU.add, scalar2=None))
        p.barrier()
    cst_.close()

    gst = p.phase_stack()
    At = p.sbuf("l1a_At", [128, T], F32, gst)
    Bt = p.sbuf("l1a_Bt", [128, T], F32, gst)
    rt_g = p.sbuf("l1a_rtg", [128, TCH], BF16, gst)
    it_g = p.sbuf("l1a_itg", [128, TCH], BF16, gst)
    la_g = p.sbuf("l1a_lag", [128, TCH], F32, gst)
    mu_g = p.sbuf("l1a_mug", [128, TCH], F32, gst)
    t4_g = p.sbuf("l1a_t4g", [128, TCH], BF16, gst)
    rsc = p.sbuf("l1a_rsc", [128, T], BF16, gst)
    wab = [p.sbuf(f"l1a_wab{j}", [128, ND * 128], BF16, gst) for j in range(2)]
    wxb = [p.sbuf(f"l1a_wxb{j}", [128, ND * 128], BF16, gst) for j in range(2)]
    for m in range(ND):
        j = m % 2
        load_block_kpj(p, wab[j], p.dram["gaW_b"][m], ND)
        load_block_kpj(p, wxb[j], p.dram["gxW_b"][m], ND)
        p.barrier()
        for tc in range(NT):
            tsl = slice(tc * TCH, (tc + 1) * TCH)
            ba = banks[(tc % 2) * 2]
            bx = banks[(tc % 2) * 2 + 1]

            def g_mm(e, j=j, ba=ba, bx=bx, tc=tc):
                for k in range(ND):
                    e.matmul(ba[:], wab[j][:, k * 128:(k + 1) * 128],
                             RT[:, k * PADT + 4 + tc * TCH: k * PADT + 4 + (tc + 1) * TCH],
                             start=(k == 0), stop=(k == ND - 1))
                for k in range(ND):
                    instr = e.matmul(bx[:], wxb[j][:, k * 128:(k + 1) * 128],
                                     RT[:, k * PADT + 4 + tc * TCH:
                                        k * PADT + 4 + (tc + 1) * TCH],
                                     start=(k == 0), stop=(k == ND - 1))
                return instr
            p.emit("pe", g_mm)
            p.barrier()
            p.emit("act", lambda e, ba=ba, m=m: e.activation(
                out=rt_g[:], in_=ba[:], func=AF.Sigmoid, bias=gab[:, m:m + 1]))
            p.emit("act", lambda e, bx=bx, m=m: e.activation(
                out=it_g[:], in_=bx[:], func=AF.Sigmoid, bias=gxb[:, m:m + 1]))
            p.barrier()
            p.emit("dve", lambda e, m=m: e.tensor_scalar(
                out=la_g[:], in0=rt_g[:], scalar1=c8[:, m:m + 1], op0=ALU.mult,
                scalar2=None))
            p.barrier()
            p.emit("act", lambda e, tsl=tsl: e.activation(
                out=At[:, tsl], in_=la_g[:], func=AF.Exp))
            p.emit("act", lambda e: e.activation(
                out=mu_g[:], in_=la_g[:], func=AF.Exp, scale=2.0))
            p.barrier()
            p.emit("act", lambda e: e.activation(
                out=mu_g[:], in_=mu_g[:], func=AF.Sqrt, scale=-1.0,
                bias=p.sb["cst_c_one"][:]))
            p.barrier()
            p.emit("dve", lambda e, m=m, tc=tc: e.tensor_tensor(
                out=t4_g[:], in0=it_g[:],
                in1=RT[:, m * PADT + 4 + tc * TCH: m * PADT + 4 + (tc + 1) * TCH],
                op=ALU.mult))
            p.barrier()
            p.emit("dve", lambda e, tsl=tsl: e.tensor_tensor(
                out=Bt[:, tsl], in0=t4_g[:], in1=mu_g[:], op=ALU.mult))
            p.barrier()
        p.emit("dve", lambda e: e.tensor_tensor_scan(
            out=rsc[:], data0=At[:], data1=Bt[:], initial=0.0,
            op0=ALU.mult, op1=ALU.add))
        p.barrier()
        p.emit("dve", lambda e, m=m: e.tensor_tensor(
            out=LEFT[:, m * T:(m + 1) * T], in0=LEFT[:, m * T:(m + 1) * T],
            in1=rsc[:], op=ALU.mult))
        p.barrier()
    gst.close()

    ost = p.phase_stack()
    wob = [p.sbuf(f"l1a_wob{j}", [128, ND * 128], BF16, ost) for j in range(2)]
    xnk = [p.sbuf(f"l1a_oxn{j}", [128, TCH], BF16, ost) for j in range(2)]
    tev = p.sbuf("l1a_tev", [128, TCH], F32, ost)
    for m2 in range(ND):
        j = m2 % 2
        load_block_kpj(p, wob[j], p.dram["outW_b"][m2], ND)
        p.barrier()
        for tc in range(NT):
            tsl = slice(tc * TCH, (tc + 1) * TCH)
            pb = banks[4 + tc % 2]
            p.dma_t(xnk[tc % 2][:], xn_tm[tsl, m2 * 128:(m2 + 1) * 128])

            def o_mm(e, j=j, tc=tc, pb=pb):
                for k in range(ND):
                    instr = e.matmul(pb[:], wob[j][:, k * 128:(k + 1) * 128],
                                     LEFT[:, k * T + tc * TCH: k * T + (tc + 1) * TCH],
                                     start=(k == 0), stop=(k == ND - 1))
                return instr
            p.emit("pe", o_mm)
            p.barrier()
            p.emit("dve", lambda e, pb=pb, m2=m2: e.tensor_scalar(
                out=tev[:], in0=pb[:], scalar1=ob[:, m2:m2 + 1], op0=ALU.add,
                scalar2=None))
            p.barrier()
            rt_off = m2 * PADT + 4 + tc * TCH
            p.emit("dve", lambda e, rt_off=rt_off, tc=tc: e.tensor_tensor(
                out=RT[:, rt_off:rt_off + TCH], in0=tev[:], in1=xnk[tc % 2][:], op=ALU.add))
            p.barrier()
        p.dma("sp", x1_fm[m2 * 128:(m2 + 1) * 128, :],
              RT[:, m2 * PADT + 4: m2 * PADT + 4 + T])
        p.barrier()
    ost.close()

    s2st = p.phase_stack()
    sq2 = p.sbuf("l1a_sq2", [128, T], BF16, s2st)
    for k in range(ND):
        p.emit("dve", lambda e, k=k: e.tensor_tensor(
            out=sq2[:], in0=RT[:, k * PADT + 4: k * PADT + 4 + T],
            in1=RT[:, k * PADT + 4: k * PADT + 4 + T], op=ALU.mult))
        p.barrier()

        def sq2mm(e, k=k):
            for tc in range(NT):
                instr = e.matmul(banks[4 + tc][0:1, :], ones128[:],
                                 sq2[:, tc * TCH:(tc + 1) * TCH],
                                 start=(k == 0), stop=(k == ND - 1))
            return instr
        p.emit("pe", sq2mm)
        p.barrier()
    for tc in range(NT):
        tsl = slice(tc * TCH, (tc + 1) * TCH)
        p.emit("act", lambda e, tsl=tsl, pb=banks[4 + tc]: e.activation(
            out=rr2[0:1, tsl], in_=pb[0:1, :], func=AF.Ln, scale=1.0 / D,
            bias=eps8[0:1, :]))
    p.barrier()
    p.emit("act", lambda e: e.activation(out=rr2[:], in_=rr2[:], func=AF.Exp, scale=-0.5))
    p.barrier()
    p.emit("act", lambda e: e.activation(out=rr2b[:], in_=rr2[:], func=AF.Copy))
    p.barrier()
    for tc in range(NT):
        tsl = slice(tc * TCH, (tc + 1) * TCH)
        p.emit("pe", lambda e, tsl=tsl, pb=banks[tc]: e.matmul(
            pb[:], ones1[:], rr2b[0:1, tsl], start=True, stop=True))
    p.barrier()
    for tc in range(NT):
        tsl = slice(tc * TCH, (tc + 1) * TCH)
        p.emit("act", lambda e, tsl=tsl, pb=banks[tc]: e.activation(
            out=rr2t[:, tsl], in_=pb[:], func=AF.Copy))
    p.barrier()
    for k in range(ND):
        for tc in range(NT):
            tsl = slice(tc * TCH, (tc + 1) * TCH)
            p.emit("dve", lambda e, k=k, tc=tc, tsl=tsl: e.scalar_tensor_tensor(
                out=LEFT[:, k * T + tc * TCH: k * T + (tc + 1) * TCH],
                in0=RT[:, k * PADT + 4 + tc * TCH: k * PADT + 4 + (tc + 1) * TCH],
                scalar=n2w[:, k:k + 1], op0=ALU.mult, op1=ALU.mult, in1=rr2t[:, tsl]))
        p.barrier()
    s2st.close()

    g1_fm = p.dram["G1_FM"]
    ust = p.phase_stack()
    w1b = [p.sbuf(f"l1a_w1b{j}", [128, ND * 128], BF16, ust) for j in range(2)]
    w2b = [p.sbuf(f"l1a_w2b{j}", [128, ND * 128], BF16, ust) for j in range(2)]
    g1t = p.sbuf("l1a_g1t", [128, TCH], BF16, ust)
    gout = p.sbuf("l1a_gout", [128, T], BF16, ust)
    for mc in range(NF):
        j = mc % 2
        load_block_kpj(p, w1b[j], p.dram["up1W_b"][mc], ND)
        load_block_kpj(p, w2b[j], p.dram["up2W_b"][mc], ND)
        p.barrier()
        for tc in range(NT):
            tsl = slice(tc * TCH, (tc + 1) * TCH)
            b1 = banks[(tc % 2) * 2]
            b2 = banks[(tc % 2) * 2 + 1]

            def up_mm(e, j=j, b1=b1, b2=b2, tc=tc):
                for k in range(ND):
                    e.matmul(b1[:], w1b[j][:, k * 128:(k + 1) * 128],
                             LEFT[:, k * T + tc * TCH: k * T + (tc + 1) * TCH],
                             start=(k == 0), stop=(k == ND - 1))
                for k in range(ND):
                    instr = e.matmul(b2[:], w2b[j][:, k * 128:(k + 1) * 128],
                                     LEFT[:, k * T + tc * TCH: k * T + (tc + 1) * TCH],
                                     start=(k == 0), stop=(k == ND - 1))
                return instr
            p.emit("pe", up_mm)
            p.barrier()
            p.emit("act", lambda e, b1=b1, mc=mc: e.activation(
                out=g1t[:], in_=b1[:], func=AF.Gelu, bias=u1b[:, mc:mc + 1]))
            p.barrier()
            p.emit("dve", lambda e, b2=b2, mc=mc, tsl=tsl: e.scalar_tensor_tensor(
                out=gout[:, tsl], in0=b2[:], scalar=u2b[:, mc:mc + 1], op0=ALU.add,
                op1=ALU.mult, in1=g1t[:]))
            p.barrier()
        p.dma("sp", g1_fm[mc * 128:(mc + 1) * 128, :], gout[:])
        p.barrier()
    ust.close()
    rlst.close()

    dst_ = p.phase_stack()
    TH = 1024
    gc = p.sbuf("l1a_gc", [128, NF * TH], BF16, dst_)
    wdb = [p.sbuf(f"l1a_wdb{j}", [128, NF * 128], BF16, dst_) for j in range(2)]
    mt_ = p.sbuf("l1a_mt", [128, TH], BF16, dst_)
    for th in range(2):
        for mc in range(NF):
            p.dma("sp" if mc % 2 == 0 else "act", gc[:, mc * TH:(mc + 1) * TH],
                  g1_fm[mc * 128:(mc + 1) * 128, th * TH:(th + 1) * TH])
        p.barrier()
        for dk in range(ND):
            j = dk % 2
            load_block_kpj(p, wdb[j], p.dram["downW_b"][dk], NF)
            p.barrier()
            for stc in range(2):
                pb = banks[(dk % 2) * 2 + stc]

                def dn_mm(e, j=j, stc=stc, pb=pb):
                    for mc in range(NF):
                        instr = e.matmul(
                            pb[:], wdb[j][:, mc * 128:(mc + 1) * 128],
                            gc[:, mc * TH + stc * TCH: mc * TH + (stc + 1) * TCH],
                            start=(mc == 0), stop=(mc == NF - 1))
                    return instr
                p.emit("pe", dn_mm)
                p.barrier()
                p.emit("dve", lambda e, stc=stc, pb=pb, dk=dk: e.tensor_scalar(
                    out=mt_[:, stc * TCH:(stc + 1) * TCH], in0=pb[:],
                    scalar1=dnb[:, dk:dk + 1], op0=ALU.add, scalar2=None))
                p.barrier()
            p.dma("sp", mlp_fm[dk * 128:(dk + 1) * 128, th * TH:(th + 1) * TH], mt_[:])
            p.barrier()
    dst_.close()
    st.close()



NQ = 44

def emit_l1f(p, out_fm):
    st = p.phase_stack()
    banks = [p.psum(f"l1f_ps{i}", stack=st) for i in range(8)]
    xn_tm = p.dram["XN_TM"]
    gq_fm = p.dram["GQ_FM"]
    ones1 = p.sb["cst_ones1x128"]
    ones128 = p.sb["cst_ones128x1"]
    sg = p.sbuf("l1f_sg", [128, NQ], F32, st)
    su = p.sbuf("l1f_su", [128, NQ], F32, st)
    sd = p.sbuf("l1f_sd", [128, 16], F32, st)
    p.dma("sp", sg[:], p.dram["s_gate"])
    p.dma("sp", su[:], p.dram["s_up"])
    p.dma("sp", sd[:], p.dram["s_down"])
    csr = p.sbuf("l1f_csr", [1, T], F32, st)
    csrb = p.sbuf("l1f_csrb", [1, T], BF16, st)
    CSB = p.sbuf("l1f_csb", [128, T], F32, st)
    p.barrier()

    xst = p.phase_stack()
    xn_fm = [p.sbuf(f"l1f_xn{k}", [128, T], BF16, xst) for k in range(ND)]
    emit_load_xn_fm(p, xn_tm, xn_fm, "l1f")
    wst = p.phase_stack()
    wgb = [p.sbuf(f"l1f_wgb{j}", [128, ND * 128], BF16, wst) for j in range(2)]
    wub = [p.sbuf(f"l1f_wub{j}", [128, ND * 128], BF16, wst) for j in range(2)]
    gst_ = p.sbuf("l1f_gst", [128, TCH], BF16, wst)
    gout = p.sbuf("l1f_gout", [128, T], BF16, wst)
    for mc in range(NQ):
        j = mc % 2
        load_block_kpj(p, wgb[j], p.dram["wqgT_b"][mc], ND)
        load_block_kpj(p, wub[j], p.dram["wquT_b"][mc], ND)
        p.barrier()
        for tc in range(NT):
            tsl = slice(tc * TCH, (tc + 1) * TCH)
            bg = banks[(tc % 2) * 2]
            bu = banks[(tc % 2) * 2 + 1]

            def gu_mm(e, j=j, bg=bg, bu=bu, tc=tc):
                for k in range(ND):
                    e.matmul(bg[:], wgb[j][:, k * 128:(k + 1) * 128],
                             xn_fm[k][:, tc * TCH:(tc + 1) * TCH],
                             start=(k == 0), stop=(k == ND - 1))
                for k in range(ND):
                    instr = e.matmul(bu[:], wub[j][:, k * 128:(k + 1) * 128],
                                     xn_fm[k][:, tc * TCH:(tc + 1) * TCH],
                                     start=(k == 0), stop=(k == ND - 1))
                return instr
            p.emit("pe", gu_mm)
            p.barrier()
            p.emit("act", lambda e, bg=bg, mc=mc: e.activation(
                out=gst_[:], in_=bg[:], func=AF.Silu, scale=sg[:, mc:mc + 1]))
            p.barrier()
            p.emit("dve", lambda e, bu=bu, mc=mc, tsl=tsl: e.scalar_tensor_tensor(
                out=gout[:, tsl], in0=bu[:], scalar=su[:, mc:mc + 1], op0=ALU.mult,
                op1=ALU.mult, in1=gst_[:]))
            p.barrier()

            def cs_mm(e, tc=tc, tsl=tsl, mc=mc):
                return e.matmul(banks[4 + tc][0:1, :], ones128[:], gout[:, tsl],
                                start=(mc == 0), stop=(mc == NQ - 1))
            p.emit("pe", cs_mm)
            p.barrier()
        p.dma("sp", gq_fm[mc * 128:(mc + 1) * 128, :], gout[:])
        p.barrier()
    wst.close()
    xst.close()
    for tc in range(NT):
        tsl = slice(tc * TCH, (tc + 1) * TCH)
        p.emit("act", lambda e, tsl=tsl, pb=banks[4 + tc]: e.activation(
            out=csr[0:1, tsl], in_=pb[0:1, :], func=AF.Copy, scale=128.0))
    p.barrier()
    p.emit("act", lambda e: e.activation(out=csrb[:], in_=csr[:], func=AF.Copy))
    p.barrier()
    for tc in range(NT):
        tsl = slice(tc * TCH, (tc + 1) * TCH)
        p.emit("pe", lambda e, tsl=tsl, pb=banks[tc]: e.matmul(
            pb[:], ones1[:], csrb[0:1, tsl], start=True, stop=True))
    p.barrier()
    for tc in range(NT):
        tsl = slice(tc * TCH, (tc + 1) * TCH)
        p.emit("act", lambda e, tsl=tsl, pb=banks[tc]: e.activation(
            out=CSB[:, tsl], in_=pb[:], func=AF.Copy))
    p.barrier()

    dst_ = p.phase_stack()
    TH = 1024
    gc = p.sbuf("l1f_gc", [128, NQ * TH], BF16, dst_)
    wdb = [p.sbuf(f"l1f_wdb{j}", [128, NQ * 128], BF16, dst_) for j in range(2)]
    t1 = p.sbuf("l1f_t1", [128, TCH], F32, dst_)
    ot = p.sbuf("l1f_ot", [128, TH], BF16, dst_)
    for th in range(2):
        for mc in range(NQ):
            p.dma("sp" if mc % 2 == 0 else "act", gc[:, mc * TH:(mc + 1) * TH],
                  gq_fm[mc * 128:(mc + 1) * 128, th * TH:(th + 1) * TH])
        p.barrier()
        for dk in range(ND):
            j = dk % 2
            load_block_kpj(p, wdb[j], p.dram["wqdT_b"][dk], NQ)
            p.barrier()
            for stc in range(2):
                pb = banks[(dk % 2) * 2 + stc]
                gsl = slice(th * TH + stc * TCH, th * TH + (stc + 1) * TCH)

                def dn_mm(e, j=j, stc=stc, pb=pb):
                    for mc in range(NQ):
                        instr = e.matmul(
                            pb[:], wdb[j][:, mc * 128:(mc + 1) * 128],
                            gc[:, mc * TH + stc * TCH: mc * TH + (stc + 1) * TCH],
                            start=(mc == 0), stop=(mc == NQ - 1))
                    return instr
                p.emit("pe", dn_mm)
                p.barrier()
                p.emit("dve", lambda e, pb=pb, gsl=gsl: e.tensor_tensor(
                    out=t1[:], in0=pb[:], in1=CSB[:, gsl], op=ALU.subtract))
                p.barrier()
                p.emit("dve", lambda e, stc=stc, dk=dk: e.tensor_scalar(
                    out=ot[:, stc * TCH:(stc + 1) * TCH], in0=t1[:],
                    scalar1=sd[:, dk:dk + 1], op0=ALU.mult, scalar2=None))
                p.barrier()
            p.dma("sp", out_fm[dk * 128:(dk + 1) * 128, th * TH:(th + 1) * TH], ot[:])
            p.barrier()
    dst_.close()
    st.close()


L0A_INPUTS = ["nu_log", "theta_log", "BreT", "BimT", "CreT", "CimT", "l0a_D"]
L0F_INPUTS = ["gwT", "moe_wg", "moe_wu_hc", "moe_wd", "shg", "shwg_b", "shwu_b", "shwd_b"]
L1A_INPUTS = ["n1w", "n2w", "leftW_b", "rightW_b", "gaW_b", "gxW_b", "outW_b",
              "left_b", "right_b", "conv_b", "ga_b", "gx_b", "out_b", "a_param",
              "conv_w", "up1_b", "up2_b", "down_b", "up1W_b", "up2W_b", "downW_b"]
L1F_INPUTS = ["wqgT_b", "wquT_b", "wqdT_b", "s_gate", "s_up", "s_down"]

SB_INPUTS = {"l0a": L0A_INPUTS, "l0f": L0F_INPUTS, "l1a": L1A_INPUTS,
             "l1f": L1F_INPUTS}


def build_subblock_program(name):
    p = Prog(name)
    r_in = p.dram_t("r_in", [T, D], F32, "ExternalInput")
    r_out = p.dram_t("r_out", [T, D], F32, "ExternalOutput")
    for n in CONST_NAMES:
        p.dram_t(n, list(np.shape(_HOST[n])), F32, "ExternalInput")
    declare_inputs(p, _HOST, SB_INPUTS[name])
    xn_tm = p.dram_t("XN_TM", [T, D], BF16, "Internal")
    out_fm = p.dram_t("OUT_FM", [D, T], BF16, "Internal")
    load_consts(p)
    emit_ln(p, r_in, xn_tm, name)
    if name == "l0a":
        emit_l0a(p, out_fm)
        adds = [out_fm]
    elif name == "l0f":
        p.dram_t("OUT_ACC", [D, T], F32, "Internal")
        p.dram_t("CBR_D", [8, T], F32, "Internal")
        emit_l0f(p, out_fm)
        adds = [out_fm]
    elif name == "l1a":
        x1_fm = p.dram_t("X1_FM", [D, T], BF16, "Internal")
        p.dram_t("G1_FM", [6144, T], BF16, "Internal")
        emit_l1a(p, x1_fm, out_fm)
        adds = [x1_fm, out_fm]
    elif name == "l1f":
        p.dram_t("GQ_FM", [5632, T], BF16, "Internal")
        emit_l1f(p, out_fm)
        adds = [out_fm]
    emit_resid(p, r_in, adds, r_out, name)
    return p.finalize()


_HOST = None
_PROG_CACHE = {}
SB_ORDER = ["l0a", "l0f", "l1a", "l1f"]
FUSED = True


def build_fused_program():
    p = Prog("fused")
    r0 = p.dram_t("r_in", [T, D], F32, "ExternalInput")
    r4 = p.dram_t("r_out", [T, D], F32, "ExternalOutput")
    r1 = p.dram_t("r1", [T, D], F32, "Internal")
    r2 = p.dram_t("r2", [T, D], F32, "Internal")
    r3 = p.dram_t("r3", [T, D], F32, "Internal")
    for n in CONST_NAMES:
        p.dram_t(n, list(np.shape(_HOST[n])), F32, "ExternalInput")
    for name in SB_ORDER:
        declare_inputs(p, _HOST, SB_INPUTS[name])
    xn_tm = p.dram_t("XN_TM", [T, D], BF16, "Internal")
    out_fm = p.dram_t("OUT_FM", [D, T], BF16, "Internal")
    p.dram_t("OUT_ACC", [D, T], F32, "Internal")
    p.dram_t("CBR_D", [8, T], F32, "Internal")
    x1_fm = p.dram_t("X1_FM", [D, T], BF16, "Internal")
    p.dram_t("G1_FM", [6144, T], BF16, "Internal")
    p.dram_t("GQ_FM", [5632, T], BF16, "Internal")
    load_consts(p)
    emit_ln(p, r0, xn_tm, "l0a")
    emit_l0a(p, out_fm)
    emit_resid(p, r0, [out_fm], r1, "l0a")
    emit_ln(p, r1, xn_tm, "l0f")
    emit_l0f(p, out_fm)
    emit_resid(p, r1, [out_fm], r2, "l0f")
    emit_ln(p, r2, xn_tm, "l1a")
    emit_l1a(p, x1_fm, out_fm)
    emit_resid(p, r2, [x1_fm, out_fm], r3, "l1a")
    emit_ln(p, r3, xn_tm, "l1f")
    emit_l1f(p, out_fm)
    emit_resid(p, r3, [out_fm], r4, "l1f")
    return p.finalize()


def kernel(**inputs):
    global _HOST
    _HOST = host_prep(inputs)
    x = inputs["L0_attn_x"]
    if FUSED:
        if "fused" not in _PROG_CACHE:
            _PROG_CACHE["fused"] = build_fused_program()
        nc = _PROG_CACHE["fused"]
        all_names = list(CONST_NAMES)
        for name in SB_ORDER:
            all_names += SB_INPUTS[name]
        in_maps = []
        for b in range(8):
            m = {"r_in": np.ascontiguousarray(x[b])}
            for n in all_names:
                m[n] = _HOST[n]
            in_maps.append(m)
        res = run_bass_kernel_spmd(nc, in_maps, core_ids=list(range(8)))
        return np.stack([res.results[b]["r_out"] for b in range(8)], axis=0)
    r = [np.ascontiguousarray(x[b]) for b in range(8)]
    for name in SB_ORDER:
        if name not in _PROG_CACHE:
            _PROG_CACHE[name] = build_subblock_program(name)
        nc = _PROG_CACHE[name]
        in_maps = []
        for b in range(8):
            m = {"r_in": r[b]}
            for n in CONST_NAMES + SB_INPUTS[name]:
                m[n] = _HOST[n]
            in_maps.append(m)
        res = run_bass_kernel_spmd(nc, in_maps, core_ids=list(range(8)))
        r = [res.results[b]["r_out"] for b in range(8)]
    return np.stack(r, axis=0)


# version 04
# speedup vs baseline: 1.8320x; 1.8320x over previous
import contextlib
import numpy as np
import concourse.bass as bass
import concourse.mybir as mybir
from concourse.bass_utils import run_bass_kernel_spmd

F32 = mybir.dt.float32
BF16 = mybir.dt.bfloat16
I32 = mybir.dt.int32
U8 = mybir.dt.uint8
U32 = mybir.dt.uint32
AF = mybir.ActivationFunctionType
ALU = mybir.AluOpType
AX = mybir.AxisListType

T = 2048
D = 2048
ND = D // 128
TCH = 512
NT = T // TCH
NTT = T // 128
EPS_LN = 1e-5
EPS_RMS = 1e-8
TWO_PI = 6.283185307179586
PI = 3.141592653589793

ENGS = ("pe", "act", "dve", "pool", "sp")
_BLKATTR = {"pe": "tensor", "act": "scalar", "dve": "vector", "pool": "gpsimd", "sp": "sync"}
_ENGSEM = {"pe": ("c_pe", 1), "act": ("c_act", 1), "dve": ("c_dve", 1),
           "pool": ("c_pool", 1), "sp": ("c_sp", 1)}
_DMASEM = {"sp": ("d_sp", 16), "act": ("d_act", 16), "pool": ("d_pool", 16)}


class Prog:
    def __init__(self, name="prog"):
        self.nc = bass.Bass("TRN2", target_bir_lowering=False)
        self.stack = contextlib.ExitStack()
        self.ops = {e: [] for e in ENGS}
        self.sem_count = {}
        self.frontier = {}
        self.eng_waited = {e: {} for e in ENGS}
        self.sems = {}
        for s in ("c_pe", "c_act", "c_dve", "c_pool", "c_sp", "d_sp", "d_act", "d_pool"):
            self.sems[s] = self.stack.enter_context(self.nc.semaphore(s))
            self.sem_count[s] = 0
        self.sb = {}
        self.ps = {}
        self.dram = {}
        self._finalized = False

    def dram_t(self, name, shape, dt, kind):
        ap = self.nc.dram_tensor(name, shape, dt, kind=kind).ap()
        self.dram[name] = ap
        return ap

    def sbuf(self, name, shape, dt, stack=None):
        st = stack if stack is not None else self.stack
        t = st.enter_context(self.nc.sbuf_tensor(name, shape, dt))
        self.sb[name] = t
        return t

    def psum(self, name, shape=(128, 512), dt=F32, stack=None):
        st = stack if stack is not None else self.stack
        t = st.enter_context(self.nc.psum_tensor(name, list(shape), dt))
        self.ps[name] = t
        return t

    def phase_stack(self):
        return contextlib.ExitStack()

    def _wait_closures(self, eng):
        out = []
        w = self.eng_waited[eng]
        for s, v in self.frontier.items():
            if v > 0 and w.get(s, 0) < v:
                sem = self.sems[s]
                out.append(lambda e, sem=sem, v=v: e.wait_ge(sem, v))
                w[s] = v
        return out

    def emit(self, eng, fn, dma=False, mark=True, waits=None):
        self.ops[eng].extend(self._wait_closures(eng))
        if waits:
            w = self.eng_waited[eng]
            for tok in waits:
                if tok is None:
                    continue
                s, v = tok
                if w.get(s, 0) < v:
                    sem = self.sems[s]
                    self.ops[eng].append(lambda e, sem=sem, v=v: e.wait_ge(sem, v))
                    w[s] = v
        if mark:
            sname, amt = _DMASEM[eng] if dma else _ENGSEM[eng]
            sem = self.sems[sname]
            self.sem_count[sname] += amt

            def wrapped(e, fn=fn, sem=sem, amt=amt):
                instr = fn(e)
                instr.then_inc(sem, amt)
            self.ops[eng].append(wrapped)
            return (sname, self.sem_count[sname])
        self.ops[eng].append(lambda e, fn=fn: fn(e))
        return None

    def barrier(self):
        for s, c in self.sem_count.items():
            self.frontier[s] = c

    def finalize(self):
        assert not self._finalized
        self._finalized = True
        self.barrier()
        for e in ENGS:
            if self.ops[e]:
                self.ops[e].extend(self._wait_closures(e))
        blk = self.stack.enter_context(self.nc.Block())
        for e in ENGS:
            if not self.ops[e]:
                continue
            oplist = self.ops[e]

            def body(engobj, oplist=oplist):
                for op in oplist:
                    op(engobj)
            getattr(blk, _BLKATTR[e])(body)
        self.stack.close()
        return self.nc

    def dma(self, queue, out, in_, waits=None, **kw):
        return self.emit(queue, lambda e: e.dma_start(out=out, in_=in_, **kw),
                         dma=True, waits=waits)

    def dma_t(self, out, in_, waits=None, queue="sp"):
        return self.emit(queue, lambda e: e.dma_start_transpose(out=out, in_=in_),
                         dma=True, waits=waits)



def _hl_col(v):
    k = v.shape[0] // 128
    return np.ascontiguousarray(v.reshape(k, 128).T)


def host_prep(inp):
    H = {}
    H["ones1x128"] = np.ones((1, 128), dtype=np.float32)
    H["ones128x1"] = np.ones((128, 1), dtype=np.float32)
    H["iota8"] = np.tile(np.arange(8, dtype=np.float32), (128, 1))
    H["iota128"] = np.tile(np.arange(128, dtype=np.float32), (128, 1))
    H["iota16"] = np.tile(np.arange(16, dtype=np.float32), (128, 1))
    H["identity128"] = np.eye(128, dtype=np.float32)
    H["c_eps5"] = np.full((128, 1), EPS_LN, dtype=np.float32)
    H["c_eps8"] = np.full((128, 1), EPS_RMS, dtype=np.float32)
    H["c_one"] = np.full((128, 1), 1.0, dtype=np.float32)
    H["nu_log"] = _hl_col(inp["L0_attn_nu_log"])
    H["theta_log"] = _hl_col(inp["L0_attn_theta_log"])
    H["l0a_D"] = _hl_col(inp["L0_attn_D"])
    H["BreT"] = np.ascontiguousarray(inp["L0_attn_B_re"].T)
    H["BimT"] = np.ascontiguousarray(inp["L0_attn_B_im"].T)
    H["CreT"] = np.ascontiguousarray(inp["L0_attn_C_re"].T)
    H["CimT"] = np.ascontiguousarray(inp["L0_attn_C_im"].T)
    H["gwT"] = np.ascontiguousarray(inp["L0_ffn_gate_w"].T)
    H["moe_wg_hc"] = np.ascontiguousarray(
        inp["L0_ffn_w_gate"].reshape(8, D, 11, 128).transpose(0, 2, 1, 3))
    H["moe_wu_hc"] = np.ascontiguousarray(
        inp["L0_ffn_w_up"].reshape(8, D, 11, 128).transpose(0, 2, 1, 3))
    H["moe_wd_dk"] = np.ascontiguousarray(
        inp["L0_ffn_w_down"].reshape(8, 1408, ND, 128).transpose(0, 2, 1, 3))
    H["shg"] = _hl_col(inp["L0_ffn_shgate_w"])
    HS = 2816
    H["shwg_b"] = np.ascontiguousarray(
        inp["L0_ffn_shw_gate"].reshape(D, HS // 128, 128).transpose(1, 0, 2))
    H["shwu_b"] = np.ascontiguousarray(
        inp["L0_ffn_shw_up"].reshape(D, HS // 128, 128).transpose(1, 0, 2))
    H["shwd_b"] = np.ascontiguousarray(
        inp["L0_ffn_shw_down"].reshape(22, 128, ND, 128).transpose(2, 0, 1, 3).reshape(ND, 2816, 128))
    H["n1w"] = _hl_col(inp["L1_attn_norm1_w"])
    H["n2w"] = _hl_col(inp["L1_attn_norm2_w"])
    def _mblk(w):
        m = w.shape[1] // 128
        return np.ascontiguousarray(w.reshape(D, m, 128).transpose(1, 0, 2))
    H["leftW_b"] = _mblk(inp["L1_attn_left_W"])
    H["rightW_b"] = _mblk(inp["L1_attn_right_W"])
    H["gaW_b"] = _mblk(inp["L1_attn_ga_W"])
    H["gxW_b"] = _mblk(inp["L1_attn_gx_W"])
    H["outW_b"] = _mblk(inp["L1_attn_out_W"])
    H["left_b"] = _hl_col(inp["L1_attn_left_b"])
    H["right_b"] = _hl_col(inp["L1_attn_right_b"])
    H["conv_b"] = _hl_col(inp["L1_attn_conv_b"])
    H["ga_b"] = _hl_col(inp["L1_attn_ga_b"])
    H["gx_b"] = _hl_col(inp["L1_attn_gx_b"])
    H["out_b"] = _hl_col(inp["L1_attn_out_b"])
    H["a_param"] = _hl_col(inp["L1_attn_a_param"])
    H["conv_w"] = np.ascontiguousarray(
        inp["L1_attn_conv_w"].T.reshape(16, 128, 4).transpose(1, 0, 2).reshape(128, 64))
    F = 6144
    H["up1_b"] = _hl_col(inp["L1_attn_up1_b"])
    H["up2_b"] = _hl_col(inp["L1_attn_up2_b"])
    H["down_b"] = _hl_col(inp["L1_attn_down_b"])
    H["up1W_b"] = _mblk(inp["L1_attn_up1_W"])
    H["up2W_b"] = _mblk(inp["L1_attn_up2_W"])
    H["downW_b"] = np.ascontiguousarray(
        inp["L1_attn_down_W"].reshape(F, ND, 128).transpose(1, 0, 2))
    HQ = 5632
    H["wqgT_b"] = np.ascontiguousarray(
        inp["L1_ffn_wq_gate"].T.reshape(D, HQ // 128, 128).transpose(1, 0, 2))
    H["wquT_b"] = np.ascontiguousarray(
        inp["L1_ffn_wq_up"].T.reshape(D, HQ // 128, 128).transpose(1, 0, 2))
    H["wqdT_b"] = np.ascontiguousarray(
        inp["L1_ffn_wq_down"].T.reshape(HQ, ND, 128).transpose(1, 0, 2))
    H["s_gate"] = _hl_col(inp["L1_ffn_s_gate"])
    H["s_up"] = _hl_col(inp["L1_ffn_s_up"])
    H["s_down"] = _hl_col(inp["L1_ffn_s_down"])
    return H


CONST_NAMES = ["ones1x128", "ones128x1", "iota8", "iota128", "iota16",
               "identity128", "c_eps5", "c_eps8", "c_one"]


def declare_inputs(p, H, names):
    for n in names:
        a = H[n]
        dt = I32 if a.dtype == np.int32 else F32
        p.dram_t(n, list(a.shape), dt, "ExternalInput")


def load_consts(p):
    c = {}
    c["ones1x128"] = p.sbuf("cst_ones1x128", [1, 128], BF16)
    c["ones128x1"] = p.sbuf("cst_ones128x1", [128, 1], BF16)
    for n, shape in [("iota8", [128, 8]), ("iota128", [128, 128]), ("iota16", [128, 16]),
                     ("identity128", [128, 128]), ("c_eps5", [128, 1]), ("c_eps8", [128, 1]),
                     ("c_one", [128, 1])]:
        c[n] = p.sbuf("cst_" + n, shape, F32)
        p.dma("sp", c[n][:], p.dram[n])
    p.dma("pool", c["ones1x128"][:], p.dram["ones1x128"])
    p.dma("pool", c["ones128x1"][:], p.dram["ones128x1"])
    p.barrier()
    return c



def emit_ln(p, r_src, xn_tm, tag):
    st = p.phase_stack()
    PAR = 8
    rt = [p.sbuf(f"{tag}_ln_r{j}", [128, D], F32, st) for j in range(PAR)]
    xt = [p.sbuf(f"{tag}_ln_x{j}", [128, D], BF16, st) for j in range(PAR)]
    stats = [p.sbuf(f"{tag}_ln_s{j}", [128, 24], F32, st) for j in range(PAR)]
    mv = [p.sbuf(f"{tag}_ln_mv{j}", [128, 2], F32, st) for j in range(PAR)]
    rstd = [p.sbuf(f"{tag}_ln_rs{j}", [128, 1], F32, st) for j in range(PAR)]
    eps = p.sb["cst_c_eps5"]
    for i0 in range(0, NTT, PAR):
        for j in range(PAR):
            i = i0 + j
            p.dma("sp", rt[j][:], r_src[i * 128:(i + 1) * 128, :])
        p.barrier()
        for j in range(PAR):
            for q in range(4):
                p.emit("dve", lambda e, j=j, q=q: e.bn_stats(
                    out=stats[j][:, q * 6:(q + 1) * 6], in_=rt[j][:, q * 512:(q + 1) * 512]))
        p.barrier()
        for j in range(PAR):
            p.emit("dve", lambda e, j=j: e.bn_aggr(out=mv[j][:], in_=stats[j][:]))
        p.barrier()
        for j in range(PAR):
            p.emit("act", lambda e, j=j: e.activation(
                out=rstd[j][:], in_=mv[j][:, 1:2], func=AF.Ln, bias=eps[:]))
        p.barrier()
        for j in range(PAR):
            p.emit("act", lambda e, j=j: e.activation(
                out=rstd[j][:], in_=rstd[j][:], func=AF.Exp, scale=-0.5))
        p.barrier()
        for j in range(PAR):
            p.emit("dve", lambda e, j=j: e.tensor_scalar(
                out=xt[j][:], in0=rt[j][:], scalar1=mv[j][:, 0:1], scalar2=rstd[j][:],
                op0=ALU.subtract, op1=ALU.mult))
        p.barrier()
        for j in range(PAR):
            i = i0 + j
            p.dma("sp", xn_tm[i * 128:(i + 1) * 128, :], xt[j][:])
        p.barrier()
    st.close()


def emit_load_xn_fm(p, xn_tm, xn_fm, tag):
    for k in range(ND):
        p.dma_t(xn_fm[k][:], xn_tm[0:T, k * 128:(k + 1) * 128])
    p.barrier()


def emit_resid(p, r_src, adds_fm, r_dst, tag):
    st = p.phase_stack()
    PAR = 4
    rt = [p.sbuf(f"{tag}_rs_r{j}", [128, D], F32, st) for j in range(PAR)]
    at = [[p.sbuf(f"{tag}_rs_a{j}_{m}", [128, D], BF16, st) for m in range(len(adds_fm))]
          for j in range(PAR)]
    for i0 in range(0, NTT, PAR):
        for j in range(PAR):
            i = i0 + j
            p.dma("sp", rt[j][:], r_src[i * 128:(i + 1) * 128, :])
            for m, a in enumerate(adds_fm):
                p.dma_t(at[j][m][:], a[0:D, i * 128:(i + 1) * 128])
        p.barrier()
        for m in range(len(adds_fm)):
            for j in range(PAR):
                p.emit("dve", lambda e, j=j, m=m: e.tensor_tensor(
                    out=rt[j][:], in0=rt[j][:], in1=at[j][m][:], op=ALU.add))
            p.barrier()
        for j in range(PAR):
            i = i0 + j
            p.dma("sp", r_dst[i * 128:(i + 1) * 128, :], rt[j][:])
        p.barrier()
    st.close()


def emit_softplus(p, out, in_, tmp, tag):
    one = p.sb["cst_c_one"]
    p.emit("dve", lambda e: e.tensor_scalar(out=tmp[:], in0=in_[:], scalar1=-1.0, op0=ALU.mult,
                                            scalar2=None))
    p.barrier()
    p.emit("dve", lambda e: e.tensor_tensor(out=tmp[:], in0=tmp[:], in1=in_[:], op=ALU.max))
    p.barrier()
    p.emit("dve", lambda e: e.tensor_scalar(out=tmp[:], in0=tmp[:], scalar1=-1.0, op0=ALU.mult,
                                            scalar2=None))
    p.barrier()
    p.emit("act", lambda e: e.activation(out=tmp[:], in_=tmp[:], func=AF.Exp))
    p.barrier()
    p.emit("act", lambda e: e.activation(out=tmp[:], in_=tmp[:], func=AF.Ln, bias=one[:]))
    p.barrier()
    p.emit("dve", lambda e: e.tensor_scalar(out=out[:], in0=in_[:], scalar1=0.0, op0=ALU.max,
                                            scalar2=None))
    p.barrier()
    p.emit("dve", lambda e: e.tensor_tensor(out=out[:], in0=out[:], in1=tmp[:], op=ALU.add))
    p.barrier()


def emit_sin_pm_pi(p, out, ang, tag):
    p.emit("act", lambda e: e.activation(out=out[:], in_=ang[:], func=AF.Sin))


def emit_mod_2pi(p, out, in_, itmp, ftmp, tag):
    p.emit("dve", lambda e: e.tensor_scalar(out=out[:], in0=in_[:], scalar1=1.0 / TWO_PI,
                                            op0=ALU.mult, scalar2=None))
    p.barrier()
    p.emit("dve", lambda e: e.tensor_copy(out=itmp[:], in_=out[:]))
    p.barrier()
    p.emit("dve", lambda e: e.tensor_copy(out=ftmp[:], in_=itmp[:]))
    p.barrier()
    p.emit("dve", lambda e: e.tensor_tensor(out=itmp[:], in0=ftmp[:], in1=out[:], op=ALU.is_gt))
    p.barrier()
    p.emit("dve", lambda e: e.scalar_tensor_tensor(out=ftmp[:], in0=itmp[:], scalar=-1.0,
                                                   op0=ALU.mult, op1=ALU.add, in1=ftmp[:]))
    p.barrier()
    p.emit("dve", lambda e: e.tensor_tensor(out=out[:], in0=out[:], in1=ftmp[:], op=ALU.subtract))
    p.barrier()
    p.emit("dve", lambda e: e.tensor_scalar(out=out[:], in0=out[:], scalar1=TWO_PI,
                                            op0=ALU.mult, scalar2=None))
    p.barrier()


def emit_fold_to_pi(p, ang, msk, tag):
    p.emit("dve", lambda e: e.tensor_scalar(out=msk[:], in0=ang[:], scalar1=PI, op0=ALU.is_gt,
                                            scalar2=None))
    p.barrier()
    p.emit("dve", lambda e: e.scalar_tensor_tensor(out=ang[:], in0=msk[:], scalar=-TWO_PI,
                                                   op0=ALU.mult, op1=ALU.add, in1=ang[:]))
    p.barrier()



def emit_l0a(p, out_fm):
    NS = 2
    st = p.phase_stack()
    nu_l = p.sbuf("l0a_nul", [128, 2], F32, st)
    th_l = p.sbuf("l0a_thl", [128, 2], F32, st)
    Dv = p.sbuf("l0a_Dv", [128, 16], F32, st)
    p.dma("sp", nu_l[:], p.dram["nu_log"])
    p.dma("sp", th_l[:], p.dram["theta_log"])
    p.dma("sp", Dv[:], p.dram["l0a_D"])
    p.barrier()
    nu = p.sbuf("l0a_nu", [128, 2], F32, st)
    tmp2 = p.sbuf("l0a_tmp2", [128, 2], F32, st)
    emit_softplus(p, nu, nu_l, tmp2, "l0a")
    mag = p.sbuf("l0a_mag", [128, 2], F32, st)
    theta = p.sbuf("l0a_theta", [128, 2], F32, st)
    gam = p.sbuf("l0a_gam", [128, 2], F32, st)
    p.emit("act", lambda e: e.activation(out=mag[:], in_=nu[:], func=AF.Exp))
    p.emit("act", lambda e: e.activation(out=theta[:], in_=th_l[:], func=AF.Exp))
    p.barrier()
    p.emit("act", lambda e: e.activation(out=mag[:], in_=mag[:], func=AF.Exp, scale=-1.0))
    p.barrier()
    p.emit("act", lambda e: e.activation(out=gam[:], in_=mag[:], func=AF.Square))
    p.barrier()
    p.emit("dve", lambda e: e.tensor_scalar(out=gam[:], in0=gam[:], scalar1=-1.0, scalar2=1.0,
                                            op0=ALU.mult, op1=ALU.add))
    p.barrier()
    p.emit("dve", lambda e: e.tensor_scalar(out=gam[:], in0=gam[:], scalar1=0.0, op0=ALU.max,
                                            scalar2=None))
    p.barrier()
    p.emit("act", lambda e: e.activation(out=gam[:], in_=gam[:], func=AF.Sqrt))
    p.barrier()

    iota128 = p.sb["cst_iota128"]
    iota16 = p.sb["cst_iota16"]
    ang_j = p.sbuf("l0a_angj", [128, 128], F32, st)
    itmp = p.sbuf("l0a_itmp", [128, 128], I32, st)
    ftmp = p.sbuf("l0a_ftmp", [128, 128], F32, st)
    cj = p.sbuf("l0a_cj", [128, 128], F32, st)
    sj = p.sbuf("l0a_sj", [128, 128], F32, st)
    ang_c = p.sbuf("l0a_angc", [128, 16], F32, st)
    itmp16 = p.sbuf("l0a_itmp16", [128, 16], I32, st)
    ftmp16 = p.sbuf("l0a_ftmp16", [128, 16], F32, st)
    cc = p.sbuf("l0a_cc", [128, 16], F32, st)
    sc = p.sbuf("l0a_sc", [128, 16], F32, st)
    th128 = p.sbuf("l0a_th128", [128, 2], F32, st)
    itmp2 = p.sbuf("l0a_itmp2", [128, 2], I32, st)
    p.emit("dve", lambda e: e.tensor_scalar(out=th128[:], in0=theta[:], scalar1=128.0,
                                            op0=ALU.mult, scalar2=None))
    p.barrier()
    emit_mod_2pi(p, th128, th128, itmp2, tmp2, "l0a")
    tabs = {}
    s1 = p.sbuf("l0a_s1", [128, T], F32, st)
    s2 = p.sbuf("l0a_s2", [128, T], F32, st)
    for ns in range(NS):
        for nm in ("re", "im"):
            tabs[(ns, nm)] = p.sbuf(f"l0a_tab{ns}{nm}", [128, T], F32, st)
        thc = theta[:, ns:ns + 1]
        p.emit("dve", lambda e, thc=thc: e.tensor_scalar(
            out=ang_j[:], in0=iota128[:], scalar1=thc, op0=ALU.mult, scalar2=None))
        p.barrier()
        emit_mod_2pi(p, ang_j, ang_j, itmp, ftmp, "l0a")
        p.emit("dve", lambda e: e.tensor_copy(out=sj[:], in_=ang_j[:]))
        p.barrier()
        emit_fold_to_pi(p, sj, ftmp, "l0a")
        p.emit("act", lambda e: e.activation(out=sj[:], in_=sj[:], func=AF.Sin))
        p.barrier()
        p.emit("dve", lambda e: e.tensor_scalar(out=cj[:], in0=ang_j[:], scalar1=-1.0,
                                                scalar2=PI / 2, op0=ALU.mult, op1=ALU.add))
        p.barrier()
        p.emit("dve", lambda e: e.tensor_scalar(out=ftmp[:], in0=cj[:], scalar1=-PI,
                                                op0=ALU.is_lt, scalar2=None))
        p.barrier()
        p.emit("dve", lambda e: e.scalar_tensor_tensor(out=cj[:], in0=ftmp[:], scalar=TWO_PI,
                                                       op0=ALU.mult, op1=ALU.add, in1=cj[:]))
        p.barrier()
        p.emit("act", lambda e: e.activation(out=cj[:], in_=cj[:], func=AF.Sin))
        p.barrier()
        th128c = th128[:, ns:ns + 1]
        p.emit("dve", lambda e, th128c=th128c: e.tensor_scalar(
            out=ang_c[:], in0=iota16[:], scalar1=th128c, op0=ALU.mult, scalar2=None))
        p.barrier()
        emit_mod_2pi(p, ang_c, ang_c, itmp16, ftmp16, "l0a")
        p.emit("dve", lambda e: e.tensor_copy(out=sc[:], in_=ang_c[:]))
        p.barrier()
        emit_fold_to_pi(p, sc, ftmp16, "l0a")
        p.emit("act", lambda e: e.activation(out=sc[:], in_=sc[:], func=AF.Sin))
        p.barrier()
        p.emit("dve", lambda e: e.tensor_scalar(out=cc[:], in0=ang_c[:], scalar1=-1.0,
                                                scalar2=PI / 2, op0=ALU.mult, op1=ALU.add))
        p.barrier()
        p.emit("dve", lambda e: e.tensor_scalar(out=ftmp16[:], in0=cc[:], scalar1=-PI,
                                                op0=ALU.is_lt, scalar2=None))
        p.barrier()
        p.emit("dve", lambda e: e.scalar_tensor_tensor(out=cc[:], in0=ftmp16[:], scalar=TWO_PI,
                                                       op0=ALU.mult, op1=ALU.add, in1=cc[:]))
        p.barrier()
        p.emit("act", lambda e: e.activation(out=cc[:], in_=cc[:], func=AF.Sin))
        p.barrier()
        Tre, Tim = tabs[(ns, "re")], tabs[(ns, "im")]
        for c in range(16):
            sl = slice(c * 128, (c + 1) * 128)
            ccc = cc[:, c:c + 1]
            scc = sc[:, c:c + 1]
            p.emit("dve", lambda e, sl=sl, ccc=ccc, Tre=Tre: e.tensor_scalar(
                out=Tre[:, sl], in0=cj[:], scalar1=ccc, op0=ALU.mult, scalar2=None))
            p.emit("dve", lambda e, sl=sl, ccc=ccc, Tim=Tim: e.tensor_scalar(
                out=Tim[:, sl], in0=sj[:], scalar1=ccc, op0=ALU.mult, scalar2=None))
            p.emit("dve", lambda e, sl=sl, scc=scc: e.tensor_scalar(
                out=s1[:, sl], in0=sj[:], scalar1=scc, op0=ALU.mult, scalar2=None))
            p.emit("dve", lambda e, sl=sl, scc=scc: e.tensor_scalar(
                out=s2[:, sl], in0=cj[:], scalar1=scc, op0=ALU.mult, scalar2=None))
        p.barrier()
        p.emit("dve", lambda e, Tre=Tre: e.tensor_tensor(out=Tre[:], in0=Tre[:], in1=s1[:],
                                                         op=ALU.subtract))
        p.emit("dve", lambda e, Tim=Tim: e.tensor_tensor(out=Tim[:], in0=Tim[:], in1=s2[:],
                                                         op=ALU.add))
        p.barrier()

    BU = {}
    for nm in ("re", "im"):
        for ns in range(NS):
            BU[(ns, nm)] = p.sbuf(f"l0a_bu{ns}{nm}", [128, T], F32, st)
    bst = p.phase_stack()
    bre = [p.sbuf(f"l0a_bre{k}", [128, 256], BF16, bst) for k in range(ND)]
    bim = [p.sbuf(f"l0a_bim{k}", [128, 256], BF16, bst) for k in range(ND)]
    for k in range(ND):
        p.dma("pool", bre[k][:], p.dram["BreT"][k * 128:(k + 1) * 128, :])
        p.dma("pool", bim[k][:], p.dram["BimT"][k * 128:(k + 1) * 128, :])
    xnt = [p.sbuf(f"l0a_xnt{j}", [128, TCH], BF16, bst) for j in range(2 * NT)]
    banks = [p.psum(f"l0a_psb{i}", stack=bst) for i in range(8)]
    p.barrier()
    xn_tm = p.dram["XN_TM"]
    for nm, btiles in (("re", bre), ("im", bim)):
        for k in range(ND):
            for tc in range(NT):
                j = (k % 2) * NT + tc
                p.dma_t(xnt[j][:], xn_tm[tc * TCH:(tc + 1) * TCH, k * 128:(k + 1) * 128])
            p.barrier()

            def mmk(e, btiles=btiles, k=k):
                for ns in range(NS):
                    for tc in range(NT):
                        j = (k % 2) * NT + tc
                        instr = e.matmul(banks[ns * NT + tc][:],
                                         btiles[k][:, ns * 128:(ns + 1) * 128],
                                         xnt[j][:], start=(k == 0), stop=(k == ND - 1))
                return instr
            p.emit("pe", mmk)
            p.barrier()
        for ns in range(NS):
            gc = gam[:, ns:ns + 1]
            bu = BU[(ns, nm)]
            for tc in range(NT):
                tsl = slice(tc * TCH, (tc + 1) * TCH)
                p.emit("dve", lambda e, bu=bu, tsl=tsl, gc=gc, b=banks[ns * NT + tc]:
                       e.tensor_scalar(out=bu[:, tsl], in0=b[:], scalar1=gc,
                                       op0=ALU.mult, scalar2=None))
        p.barrier()
    bst.close()

    for ns in range(NS):
        Tre, Tim = tabs[(ns, "re")], tabs[(ns, "im")]
        bre_, bim_ = BU[(ns, "re")], BU[(ns, "im")]
        p.emit("dve", lambda e, Tim=Tim, bim_=bim_: e.tensor_tensor(
            out=s1[:], in0=Tim[:], in1=bim_[:], op=ALU.mult))
        p.emit("dve", lambda e, Tim=Tim, bre_=bre_: e.tensor_tensor(
            out=s2[:], in0=Tim[:], in1=bre_[:], op=ALU.mult))
        p.barrier()
        p.emit("dve", lambda e, Tre=Tre, bre_=bre_: e.tensor_tensor(
            out=bre_[:], in0=Tre[:], in1=bre_[:], op=ALU.mult))
        p.barrier()
        p.emit("dve", lambda e, bre_=bre_: e.tensor_tensor(
            out=bre_[:], in0=bre_[:], in1=s1[:], op=ALU.add))
        p.emit("dve", lambda e, Tre=Tre, bim_=bim_: e.tensor_tensor(
            out=bim_[:], in0=Tre[:], in1=bim_[:], op=ALU.mult))
        p.barrier()
        p.emit("dve", lambda e, bim_=bim_: e.tensor_tensor(
            out=bim_[:], in0=bim_[:], in1=s2[:], op=ALU.subtract))
        p.barrier()

    mt = p.sbuf("l0a_mt", [128, T], F32, st)
    G = {}
    for nm in ("re", "im"):
        for ns in range(NS):
            G[(ns, nm)] = p.sbuf(f"l0a_g{ns}{nm}", [128, T], BF16, st)
    for ns in range(NS):
        p.emit("dve", lambda e: e.memset(mt[:], 1.0))
        p.barrier()
        mc = mag[:, ns:ns + 1]
        p.emit("dve", lambda e, mc=mc: e.tensor_scalar(
            out=mt[:], in0=mt[:], scalar1=mc, op0=ALU.mult, scalar2=None))
        p.barrier()
        for nm in ("re", "im"):
            g_ = G[(ns, nm)]
            v_ = BU[(ns, nm)]
            p.emit("dve", lambda e, g_=g_, v_=v_: e.tensor_tensor_scan(
                out=g_[:], data0=mt[:], data1=v_[:], initial=0.0,
                op0=ALU.mult, op1=ALU.add))
            p.barrier()

    H = {}
    for nm in ("re", "imN"):
        for ns in range(NS):
            H[(ns, nm)] = p.sbuf(f"l0a_h{ns}{nm}", [128, T], BF16, st)
    for ns in range(NS):
        Tre, Tim = tabs[(ns, "re")], tabs[(ns, "im")]
        gre, gim = G[(ns, "re")], G[(ns, "im")]
        p.emit("dve", lambda e, Tre=Tre, gre=gre: e.tensor_tensor(
            out=s1[:], in0=Tre[:], in1=gre[:], op=ALU.mult))
        p.emit("dve", lambda e, Tim=Tim, gim=gim: e.tensor_tensor(
            out=s2[:], in0=Tim[:], in1=gim[:], op=ALU.mult))
        p.barrier()
        hre = H[(ns, "re")]
        p.emit("dve", lambda e, hre=hre: e.tensor_tensor(
            out=hre[:], in0=s1[:], in1=s2[:], op=ALU.subtract))
        p.barrier()
        p.emit("dve", lambda e, Tre=Tre, gim=gim: e.tensor_tensor(
            out=s1[:], in0=Tre[:], in1=gim[:], op=ALU.mult))
        p.emit("dve", lambda e, Tim=Tim, gre=gre: e.tensor_tensor(
            out=s2[:], in0=Tim[:], in1=gre[:], op=ALU.mult))
        p.barrier()
        p.emit("dve", lambda e: e.tensor_tensor(
            out=s1[:], in0=s1[:], in1=s2[:], op=ALU.add))
        p.barrier()
        hin = H[(ns, "imN")]
        p.emit("dve", lambda e, hin=hin: e.tensor_scalar(
            out=hin[:], in0=s1[:], scalar1=-1.0, op0=ALU.mult, scalar2=None))
        p.barrier()

    yst = p.phase_stack()
    cre = [p.sbuf(f"l0a_cre{ns}", [128, D], BF16, yst) for ns in range(NS)]
    cim = [p.sbuf(f"l0a_cim{ns}", [128, D], BF16, yst) for ns in range(NS)]
    for ns in range(NS):
        p.dma("pool", cre[ns][:], p.dram["CreT"][ns * 128:(ns + 1) * 128, :])
        p.dma("pool", cim[ns][:], p.dram["CimT"][ns * 128:(ns + 1) * 128, :])
    ot = p.sbuf("l0a_ot", [128, T], BF16, yst)
    xnk = [p.sbuf(f"l0a_yxn{j}", [128, T], BF16, yst) for j in range(2)]
    banks2 = [p.psum(f"l0a_yps{i}", stack=yst) for i in range(4)]
    p.barrier()
    for k in range(ND):
        xb = xnk[k % 2]
        p.dma_t(xb[:], xn_tm[0:T, k * 128:(k + 1) * 128])
        for tc in range(NT):
            tsl = slice(tc * TCH, (tc + 1) * TCH)

            def ymm(e, k=k, tsl=tsl, pb=banks2[tc]):
                first = True
                for ns in range(NS):
                    e.matmul(pb[:], cre[ns][:, k * 128:(k + 1) * 128],
                             H[(ns, "re")][:, tsl], start=first, stop=False)
                    first = False
                    instr = e.matmul(pb[:], cim[ns][:, k * 128:(k + 1) * 128],
                                     H[(ns, "imN")][:, tsl], start=False, stop=(ns == NS - 1))
                return instr
            p.emit("pe", ymm)
        p.barrier()
        dc = Dv[:, k:k + 1]
        for tc in range(NT):
            tsl = slice(tc * TCH, (tc + 1) * TCH)
            p.emit("dve", lambda e, tsl=tsl, dc=dc, xb=xb, pb=banks2[tc]:
                   e.scalar_tensor_tensor(out=ot[:, tsl], in0=xb[:, tsl], scalar=dc,
                                          op0=ALU.mult, op1=ALU.add, in1=pb[:]))
        p.barrier()
        p.dma("sp", out_fm[k * 128:(k + 1) * 128, :], ot[:])
        p.barrier()
    yst.close()
    st.close()



def load_block_kpj(p, dst, src2d, K, waits=None, queue="pool"):
    return p.dma(queue, dst[:].rearrange("p (k j) -> p k j", j=128),
                 src2d.rearrange("(k p) j -> p k j", p=128), waits=waits)


def emit_l0f(p, out_fm):
    st = p.phase_stack()
    banks = [p.psum(f"l0f_ps{i}", stack=st) for i in range(8)]
    xn_fm = [p.sbuf(f"l0f_xn{k}", [128, T], BF16, st) for k in range(ND)]
    cb8 = p.sbuf("l0f_cb8", [8, 128], F32, st)
    SGRb = p.sbuf("l0f_sgr", [1, T], BF16, st)
    cbr_d = p.dram["CBR_D"]
    out_acc = p.dram["OUT_ACC"]
    emit_load_xn_fm(p, p.dram["XN_TM"], xn_fm, "l0f")

    rst = p.phase_stack()
    gwt = p.sbuf("l0f_gwt", [128, ND * 8], BF16, rst)
    shgb = p.sbuf("l0f_shgb", [128, 16], BF16, rst)
    probs = p.sbuf("l0f_probs", [128, 8], F32, rst)
    srt = p.sbuf("l0f_srt", [128, 8], F32, rst)
    idx = p.sbuf("l0f_idx", [128, 8], U32, rst)
    cmb = p.sbuf("l0f_cmb", [128, 8], F32, rst)
    msk = p.sbuf("l0f_msk", [128, 8], F32, rst)
    m1 = p.sbuf("l0f_m1", [128, 1], F32, rst)
    w1 = p.sbuf("l0f_w1", [128, 1], F32, rst)
    w2 = p.sbuf("l0f_w2", [128, 1], F32, rst)
    i1f = p.sbuf("l0f_i1f", [128, 1], F32, rst)
    i2f = p.sbuf("l0f_i2f", [128, 1], F32, rst)
    for k in range(ND):
        p.dma("pool", gwt[:, k * 8:(k + 1) * 8], p.dram["gwT"][k * 128:(k + 1) * 128, :])
    p.dma("pool", shgb[:], p.dram["shg"])
    iota8 = p.sb["cst_iota8"]
    ident = p.sb["cst_identity128"]
    p.barrier()
    for i in range(NTT):
        tsl128 = slice(i * 128, (i + 1) * 128)
        pb = banks[0]

        def logit_mm(e, tsl128=tsl128, pb=pb):
            for k in range(ND):
                instr = e.matmul(pb[:, 0:8], xn_fm[k][:, tsl128], gwt[:, k * 8:(k + 1) * 8],
                                 start=(k == 0), stop=(k == ND - 1))
            return instr
        p.emit("pe", logit_mm)
        p.barrier()
        p.emit("dve", lambda e, pb=pb: e.reduce_max(out=m1[:], in_=pb[:, 0:8], axis=AX.X))
        p.barrier()
        p.emit("dve", lambda e: e.tensor_scalar(out=m1[:], in0=m1[:], scalar1=-1.0,
                                                op0=ALU.mult, scalar2=None))
        p.barrier()
        p.emit("act", lambda e, pb=pb: e.activation(out=probs[:], in_=pb[:, 0:8],
                                                    func=AF.Exp, bias=m1[:]))
        p.barrier()
        p.emit("dve", lambda e: e.reduce_sum(out=w1[:], in_=probs[:], axis=AX.X))
        p.barrier()
        p.emit("dve", lambda e: e.reciprocal(out=w2[:], in_=w1[:]))
        p.barrier()
        p.emit("dve", lambda e: e.tensor_scalar(out=probs[:], in0=probs[:], scalar1=w2[:],
                                                op0=ALU.mult, scalar2=None))
        p.barrier()
        p.emit("dve", lambda e: e.max(out=srt[:], in_=probs[:]))
        p.barrier()
        p.emit("dve", lambda e: e.max_index(out=idx[:], in_max=srt[:], in_values=probs[:]))
        p.barrier()
        p.emit("dve", lambda e: e.tensor_tensor(out=w1[:], in0=srt[:, 0:1], in1=srt[:, 1:2],
                                                op=ALU.add))
        p.barrier()
        p.emit("dve", lambda e: e.reciprocal(out=w1[:], in_=w1[:]))
        p.barrier()
        p.emit("dve", lambda e: e.tensor_scalar(out=w2[:], in0=srt[:, 1:2], scalar1=w1[:],
                                                op0=ALU.mult, scalar2=None))
        p.barrier()
        p.emit("dve", lambda e: e.tensor_scalar(out=w1[:], in0=srt[:, 0:1], scalar1=w1[:],
                                                op0=ALU.mult, scalar2=None))
        p.barrier()
        p.emit("dve", lambda e: e.tensor_copy(out=i1f[:], in_=idx[:, 0:1]))
        p.emit("dve", lambda e: e.tensor_copy(out=i2f[:], in_=idx[:, 1:2]))
        p.barrier()
        p.emit("dve", lambda e: e.tensor_scalar(out=cmb[:], in0=iota8[:], scalar1=i1f[:],
                                                op0=ALU.is_equal, scalar2=None))
        p.emit("dve", lambda e: e.tensor_scalar(out=msk[:], in0=iota8[:], scalar1=i2f[:],
                                                op0=ALU.is_equal, scalar2=None))
        p.barrier()
        p.emit("dve", lambda e: e.tensor_scalar(out=cmb[:], in0=cmb[:], scalar1=w1[:],
                                                op0=ALU.mult, scalar2=None))
        p.barrier()
        p.emit("dve", lambda e: e.scalar_tensor_tensor(out=cmb[:], in0=msk[:], scalar=w2[:],
                                                       op0=ALU.mult, op1=ALU.add, in1=cmb[:]))
        p.barrier()
        p.emit("pe", lambda e, pb=banks[1]: e.transpose(pb[0:8, 0:128], cmb[:], ident[:]))
        p.barrier()
        p.emit("dve", lambda e, pb=banks[1]: e.tensor_copy(
            out=cb8[:], in_=pb[0:8, 0:128]))
        p.barrier()
        p.dma("sp", cbr_d[0:8, tsl128], cb8[:])
        p.barrier()
    for tc in range(NT):
        tsl = slice(tc * TCH, (tc + 1) * TCH)
        pb = banks[0]

        def sg_mm(e, tsl=tsl, pb=pb):
            for k in range(ND):
                instr = e.matmul(pb[0:1, :], shgb[:, k:k + 1], xn_fm[k][:, tsl],
                                 start=(k == 0), stop=(k == ND - 1))
            return instr
        p.emit("pe", sg_mm)
        p.barrier()
        p.emit("act", lambda e, tsl=tsl, pb=pb: e.activation(
            out=SGRb[0:1, tsl], in_=pb[0:1, :], func=AF.Sigmoid))
        p.barrier()
    rst.close()

    ones1 = p.sb["cst_ones1x128"]
    import os
    NE = int(os.environ.get("L0F_NE", "10"))
    NHC = 11
    est = p.phase_stack()
    HB = p.sbuf("l0f_hb", [128, NHC * T], BF16, est)
    cbt = [p.sbuf(f"l0f_cbt{j}", [128, T], BF16, est) for j in range(2)]
    crow = [p.sbuf(f"l0f_crow{j}", [1, T], BF16, est) for j in range(2)]
    wgb = [p.sbuf(f"l0f_wgb{j}", [128, ND * 128], BF16, est) for j in range(2)]
    wub = [p.sbuf(f"l0f_wub{j}", [128, ND * 128], BF16, est) for j in range(2)]
    wdb = [p.sbuf(f"l0f_wdb{j}", [128, NHC * 128], BF16, est) for j in range(2)]
    gs_t = [p.sbuf(f"l0f_gs{j}", [128, TCH], BF16, est) for j in range(2)]
    t1_t = [p.sbuf(f"l0f_t1{j}", [128, TCH], BF16, est) for j in range(2)]
    ot32 = [p.sbuf(f"l0f_ot{j}", [128, T], F32, est) for j in range(2)]
    pacc = [p.sbuf(f"l0f_pacc{j}", [128, T], F32, est) for j in range(2)]
    tok_pl = {}
    tok_reg = {}
    tok_wl = {}
    tok_mm = {}
    tok_gs = {}
    tok_t1 = {}
    tok_h = {}
    tok_cb = {}
    tok_cbrd = {}
    tok_dwl = {}
    tok_dmm = {}
    tok_ev = {}
    tok_acc = {}
    tok_dmm_last = {}

    def wload(e, hc):
        if e >= NE:
            return
        dep = [tok_mm.get((e, hc - 2, NT - 1)) if hc >= 2 else
               tok_mm.get((e - 1, NHC - 1 - hc, NT - 1))]
        if e >= 8:
            hb = (e - 8) * 11
            load_block_kpj(p, wgb[hc % 2], p.dram["shwg_b"][hb + hc], ND, waits=dep)
            tok_wl[(e, hc)] = load_block_kpj(p, wub[hc % 2], p.dram["shwu_b"][hb + hc], ND)
        else:
            load_block_kpj(p, wgb[hc % 2], p.dram["moe_wg_hc"][e, hc], ND, waits=dep)
            tok_wl[(e, hc)] = load_block_kpj(p, wub[hc % 2], p.dram["moe_wu_hc"][e, hc], ND)

    wload(0, 0)
    wload(0, 1)
    for e in range(NE):
        shared = (e >= 8)
        j2 = e % 2
        if shared:
            csrc = SGRb
            tok_crow = None
        else:
            tok_crow = p.dma("pool", crow[j2][:], cbr_d[e:e + 1, :],
                             waits=[tok_cb.get(e - 2, [None] * 4)[3]])
            csrc = crow[j2]
        cbtoks = []
        for tc in range(NT):
            tsl = slice(tc * TCH, (tc + 1) * TCH)
            pb = banks[4 + tc]
            bdep = tok_ev.get((e - 1, ND - 1, tc))
            tmm = p.emit("pe", lambda e_, tsl=tsl, pb=pb, csrc=csrc: e_.matmul(
                pb[:], ones1[:], csrc[0:1, tsl], start=True, stop=True),
                waits=[tok_crow, bdep])
            cbtoks.append(p.emit("act", lambda e_, tsl=tsl, pb=pb, j2=j2: e_.activation(
                out=cbt[j2][:, tsl], in_=pb[:], func=AF.Copy),
                waits=[tmm, tok_cbrd.get(e - 2)]))
        tok_cb[e] = cbtoks
        for hc in range(NHC):
            jh = hc % 2
            for tc in range(NT):
                jt = tc % 2
                bg = banks[jt * 2]
                bu = banks[jt * 2 + 1]
                deps = [tok_wl[(e, hc)], cbtoks[tc],
                        tok_gs.get((e, hc, tc - 2)), tok_t1.get((e, hc, tc - 2)),
                        tok_gs.get((e, hc - 1, tc + 2)), tok_t1.get((e, hc - 1, tc + 2))]
                if hc == 0:
                    deps += [tok_gs.get((e - 1, NHC - 1, NT - 1)),
                             tok_t1.get((e - 1, NHC - 1, NT - 1)),
                             tok_h.get((e - 1, NHC - 1, NT - 1))]

                def gu_mm(e_, jh=jh, bg=bg, bu=bu, tc=tc):
                    for k in range(ND):
                        e_.matmul(bg[:], wgb[jh][:, k * 128:(k + 1) * 128],
                                  xn_fm[k][:, tc * TCH:(tc + 1) * TCH],
                                  start=(k == 0), stop=(k == ND - 1))
                    for k in range(ND):
                        instr = e_.matmul(bu[:], wub[jh][:, k * 128:(k + 1) * 128],
                                          xn_fm[k][:, tc * TCH:(tc + 1) * TCH],
                                          start=(k == 0), stop=(k == ND - 1))
                    return instr
                tok_mm[(e, hc, tc)] = p.emit("pe", gu_mm, waits=deps)
                hwar = [tok_h.get((e, hc, tc - 2)), tok_h.get((e, hc - 1, tc + 2))]
                if hc == 0:
                    hwar.append(tok_h.get((e - 1, NHC - 1, NT - 1)))
                tok_gs[(e, hc, tc)] = p.emit("act", lambda e_, bg=bg, jt=jt:
                                             e_.activation(
                    out=gs_t[jt][:], in_=bg[:], func=AF.Silu),
                    waits=[tok_mm[(e, hc, tc)]] + hwar)
                tok_t1[(e, hc, tc)] = p.emit("dve", lambda e_, bu=bu, jt=jt, j2=j2, tc=tc:
                                             e_.tensor_tensor(
                    out=t1_t[jt][:], in0=bu[:], in1=cbt[j2][:, tc * TCH:(tc + 1) * TCH],
                    op=ALU.mult),
                    waits=[tok_mm[(e, hc, tc)]] + hwar)
                if not shared:
                    tok_cbrd[e] = tok_t1[(e, hc, tc)]
                hdeps = [tok_gs[(e, hc, tc)], tok_t1[(e, hc, tc)],
                         tok_dmm_last.get(e - 1)]
                tok_h[(e, hc, tc)] = p.emit("dve", lambda e_, hc=hc, tc=tc, jt=jt:
                                            e_.tensor_tensor(
                    out=HB[:, hc * T + tc * TCH: hc * T + (tc + 1) * TCH],
                    in0=t1_t[jt][:], in1=gs_t[jt][:], op=ALU.mult), waits=hdeps)
            if hc + 2 < NHC:
                wload(e, hc + 2)
            elif hc == NHC - 1:
                wload(e + 1, 0)
                wload(e + 1, 1)
        def wdload(e, dk):
            dep = [tok_dmm.get((e, dk - 2, NT - 1)) if dk >= 2 else
                   tok_dmm.get((e - 1, ND - 2 + dk, NT - 1))]
            if e >= 8:
                hb = (e - 8) * 11
                blk = p.dram["shwd_b"][dk][hb * 128:(hb + 11) * 128, :]
                tok_dwl[(e, dk)] = load_block_kpj(p, wdb[dk % 2], blk, NHC, waits=dep)
            else:
                tok_dwl[(e, dk)] = load_block_kpj(p, wdb[dk % 2],
                                                  p.dram["moe_wd_dk"][e, dk], NHC,
                                                  waits=dep)
        if e == 0:
            wdload(0, 0)
            wdload(0, 1)
        for dk in range(ND):
            jd = dk % 2
            if e > 0:
                tok_pl[(e, dk)] = p.dma(
                    "sp", pacc[jd][:], out_acc[dk * 128:(dk + 1) * 128, :],
                    waits=[tok_reg.get((e - 1, dk)),
                           tok_ev.get((e, dk - 2, NT - 1)) if dk >= 2
                           else tok_ev.get((e - 1, ND - 2 + dk, NT - 1))])
            for tc in range(NT):
                pb = banks[4 + tc]
                hdep = [tok_h[(e, hc, tc)] for hc in range(NHC)] if dk == 0 else []
                prev_ev = (tok_ev.get((e, dk - 1, tc)) if dk >= 1
                           else tok_ev.get((e - 1, ND - 1, tc)))
                deps = [tok_dwl[(e, dk)], prev_ev] + hdep
                if dk == 0:
                    deps.append(cbtoks[tc])

                def dn_mm(e_, jd=jd, tc=tc, pb=pb):
                    for hc in range(NHC):
                        instr = e_.matmul(
                            pb[:], wdb[jd][:, hc * 128:(hc + 1) * 128],
                            HB[:, hc * T + tc * TCH: hc * T + (tc + 1) * TCH],
                            start=(hc == 0), stop=(hc == NHC - 1))
                    return instr
                tok_dmm[(e, dk, tc)] = p.emit("pe", dn_mm, waits=deps)
                acc_dep = (tok_acc.get((e, dk - 2)) if dk >= 2
                           else tok_acc.get((e - 1, ND - 2 + dk)))
                if e == 0:
                    tok_ev[(e, dk, tc)] = p.emit(
                        "dve" if tc % 2 else "act",
                        (lambda e_, tc=tc, pb=pb, jd=jd: e_.tensor_copy(
                            out=ot32[jd][:, tc * TCH:(tc + 1) * TCH], in_=pb[:]))
                        if tc % 2 else
                        (lambda e_, tc=tc, pb=pb, jd=jd: e_.activation(
                            out=ot32[jd][:, tc * TCH:(tc + 1) * TCH], in_=pb[:],
                            func=AF.Copy)),
                        waits=[tok_dmm[(e, dk, tc)], acc_dep])
                else:
                    tok_ev[(e, dk, tc)] = p.emit(
                        "dve", lambda e_, tc=tc, pb=pb, jd=jd: e_.tensor_tensor(
                            out=ot32[jd][:, tc * TCH:(tc + 1) * TCH], in0=pb[:],
                            in1=pacc[jd][:, tc * TCH:(tc + 1) * TCH], op=ALU.add),
                        waits=[tok_dmm[(e, dk, tc)], acc_dep, tok_pl[(e, dk)]])
            ne, ndk = (e, dk + 2) if dk + 2 < ND else (e + 1, dk + 2 - ND)
            if ne < NE:
                wdload(ne, ndk)
            tok_acc[(e, dk)] = tok_reg[(e, dk)] = p.dma(
                "pool", out_acc[dk * 128:(dk + 1) * 128, :], ot32[dk % 2][:],
                waits=[tok_ev[(e, dk, tc)] for tc in range(NT)])
            tok_dmm_last[e] = tok_dmm[(e, dk, NT - 1)]
    p.barrier()
    est.close()

    cst_ = p.phase_stack()
    cv32 = [p.sbuf(f"l0f_cv32{j}", [128, T], F32, cst_) for j in range(2)]
    cv16 = [p.sbuf(f"l0f_cv16{j}", [128, T], BF16, cst_) for j in range(2)]
    for k in range(ND):
        j = k % 2
        p.dma("sp", cv32[j][:], out_acc[k * 128:(k + 1) * 128, :])
        p.barrier()
        p.emit("act", lambda e, j=j: e.activation(out=cv16[j][:], in_=cv32[j][:],
                                                  func=AF.Copy))
        p.barrier()
        p.dma("sp", out_fm[k * 128:(k + 1) * 128, :], cv16[j][:])
    p.barrier()
    cst_.close()
    st.close()




PADT = T + 4
NF = 48

def emit_l1a(p, x1_fm, mlp_fm):
    st = p.phase_stack()
    banks = [p.psum(f"l1a_ps{i}", stack=st) for i in range(8)]
    xn_tm = p.dram["XN_TM"]
    lf_fm = p.dram["LF_FM"]
    eps8 = p.sb["cst_c_eps8"]
    ones1 = p.sb["cst_ones1x128"]
    ones128 = p.sb["cst_ones128x1"]

    n1w = p.sbuf("l1a_n1w", [128, 16], F32, st)
    n2w = p.sbuf("l1a_n2w", [128, 16], F32, st)
    lb = p.sbuf("l1a_lb", [128, 16], F32, st)
    rb = p.sbuf("l1a_rb", [128, 16], F32, st)
    cb = p.sbuf("l1a_cb", [128, 16], F32, st)
    gab = p.sbuf("l1a_gab", [128, 16], F32, st)
    gxb = p.sbuf("l1a_gxb", [128, 16], F32, st)
    ob = p.sbuf("l1a_ob", [128, 16], F32, st)
    u1b = p.sbuf("l1a_u1b", [128, 48], F32, st)
    u2b = p.sbuf("l1a_u2b", [128, 48], F32, st)
    dnb = p.sbuf("l1a_dnb", [128, 16], F32, st)
    apar = p.sbuf("l1a_ap", [128, 16], F32, st)
    c8 = p.sbuf("l1a_c8", [128, 16], F32, st)
    cw = p.sbuf("l1a_cw", [128, 64], F32, st)
    for nm, tile in [("n1w", n1w), ("n2w", n2w), ("left_b", lb), ("right_b", rb),
                     ("conv_b", cb), ("ga_b", gab), ("gx_b", gxb), ("out_b", ob),
                     ("up1_b", u1b), ("up2_b", u2b), ("down_b", dnb), ("a_param", apar),
                     ("conv_w", cw)]:
        p.dma("sp", tile[:], p.dram[nm])
    p.barrier()
    tmp16 = p.sbuf("l1a_tmp16", [128, 16], F32, st)
    emit_softplus(p, c8, apar, tmp16, "l1a")
    p.emit("dve", lambda e: e.tensor_scalar(out=c8[:], in0=c8[:], scalar1=-8.0,
                                            op0=ALU.mult, scalar2=None))
    p.barrier()

    rr1b = p.sbuf("l1a_rr1b", [1, T], BF16, st)
    rr2b = p.sbuf("l1a_rr2b", [1, T], BF16, st)
    rrt_sh = p.sbuf("l1a_rrt", [128, T], BF16, st)
    rr1t = rrt_sh
    rr2t = rrt_sh

    rlst = p.phase_stack()
    TN = p.sbuf("l1a_tn", [128, ND * T], BF16, rlst)
    RT = p.sbuf("l1a_rt", [128, ND * PADT], BF16, rlst)
    p.emit("dve", lambda e: e.memset(RT[:], 0.0))
    sst = p.phase_stack()
    sq = [p.sbuf(f"l1a_sq{j}", [128, T], BF16, sst) for j in range(2)]
    tok_xl = {}
    tok_sq = {}
    for k in range(ND):
        tok_xl[k] = p.dma_t(TN[:, k * T:(k + 1) * T], xn_tm[0:T, k * 128:(k + 1) * 128])
    for k in range(ND):
        j = k % 2
        tok_sq[k] = p.emit("dve", lambda e, j=j, k=k: e.tensor_tensor(
            out=sq[j][:], in0=TN[:, k * T:(k + 1) * T], in1=TN[:, k * T:(k + 1) * T],
            op=ALU.mult), waits=[tok_xl[k], tok_sq.get((k - 2))])

        def sqmm(e, j=j, k=k):
            for tc in range(NT):
                instr = e.matmul(banks[4 + tc][0:1, :], ones128[:],
                                 sq[j][:, tc * TCH:(tc + 1) * TCH],
                                 start=(k == 0), stop=(k == ND - 1))
            return instr
        tok_sq[k] = p.emit("pe", sqmm, waits=[tok_sq[k]])
    p.barrier()
    sst.close()
    for tc in range(NT):
        p.emit("act", lambda e, tc=tc, pb=banks[4 + tc]: e.activation(
            out=banks[tc][0:1, 0:TCH], in_=pb[0:1, :], func=AF.Ln, scale=1.0 / D,
            bias=eps8[0:1, :]))
    p.barrier()
    for tc in range(NT):
        tsl = slice(tc * TCH, (tc + 1) * TCH)
        p.emit("act", lambda e, tsl=tsl, tc=tc: e.activation(
            out=rr1b[0:1, tsl], in_=banks[tc][0:1, 0:TCH], func=AF.Exp, scale=-0.5))
    p.barrier()
    for tc in range(NT):
        tsl = slice(tc * TCH, (tc + 1) * TCH)
        p.emit("pe", lambda e, tsl=tsl, pb=banks[tc]: e.matmul(
            pb[:], ones1[:], rr1b[0:1, tsl], start=True, stop=True))
    p.barrier()
    for tc in range(NT):
        tsl = slice(tc * TCH, (tc + 1) * TCH)
        p.emit("act", lambda e, tsl=tsl, pb=banks[tc]: e.activation(
            out=rr1t[:, tsl], in_=pb[:], func=AF.Copy))
    p.barrier()
    for k in range(ND):
        for tc in range(NT):
            p.emit("dve", lambda e, k=k, tc=tc: e.scalar_tensor_tensor(
                out=TN[:, k * T + tc * TCH: k * T + (tc + 1) * TCH],
                in0=TN[:, k * T + tc * TCH: k * T + (tc + 1) * TCH],
                scalar=n1w[:, k:k + 1], op0=ALU.mult, op1=ALU.mult,
                in1=rr1t[:, tc * TCH:(tc + 1) * TCH]))
    p.barrier()

    s1st = p.phase_stack()
    wrb = [p.sbuf(f"l1a_wrb{j}", [128, ND * 128], BF16, s1st) for j in range(2)]
    acc1 = p.sbuf("l1a_acc1", [128, T], F32, s1st)
    acc2 = p.sbuf("l1a_acc2", [128, T], F32, s1st)
    tok_wl = {}
    tok_mm = {}
    tok_ev = {}
    tok_conv = {}

    def wload_r(m):
        tok_wl[m] = load_block_kpj(p, wrb[m % 2], p.dram["rightW_b"][m], ND,
                                   waits=[tok_mm.get((m - 2, NT - 1))])
    wload_r(0)
    wload_r(1)
    for m in range(ND):
        j = m % 2
        for tc in range(NT):
            pb = banks[tc % 4]
            deps = [tok_wl[m], tok_ev.get((m - 1, tc))]

            def r_mm(e, j=j, tc=tc, pb=pb):
                for k in range(ND):
                    instr = e.matmul(pb[:], wrb[j][:, k * 128:(k + 1) * 128],
                                     TN[:, k * T + tc * TCH: k * T + (tc + 1) * TCH],
                                     start=(k == 0), stop=(k == ND - 1))
                return instr
            tok_mm[(m, tc)] = p.emit("pe", r_mm, waits=deps)
            rt_off = m * PADT + 4 + tc * TCH
            tok_ev[(m, tc)] = p.emit("dve", lambda e, pb=pb, m=m, rt_off=rt_off:
                                     e.tensor_scalar(
                out=RT[:, rt_off:rt_off + TCH], in0=pb[:], scalar1=rb[:, m:m + 1],
                op0=ALU.add, scalar2=None),
                waits=[tok_mm[(m, tc)], tok_conv.get(m - 1)])
        if m + 2 < ND:
            wload_r(m + 2)
        base = m * PADT
        evd = [tok_ev[(m, tc)] for tc in range(NT)]
        t1_ = p.emit("dve", lambda e, base=base, m=m: e.tensor_scalar(
            out=acc1[:], in0=RT[:, base + 1: base + 1 + T], scalar1=cw[:, m * 4:m * 4 + 1],
            op0=ALU.mult, scalar2=None), waits=evd + [tok_conv.get(m - 1)])
        t2_ = p.emit("dve", lambda e, base=base, m=m: e.scalar_tensor_tensor(
            out=acc2[:], in0=RT[:, base + 2: base + 2 + T],
            scalar=cw[:, m * 4 + 1:m * 4 + 2], op0=ALU.mult, op1=ALU.add, in1=acc1[:]),
            waits=[t1_])
        t3_ = p.emit("dve", lambda e, base=base, m=m: e.scalar_tensor_tensor(
            out=acc1[:], in0=RT[:, base + 3: base + 3 + T],
            scalar=cw[:, m * 4 + 2:m * 4 + 3], op0=ALU.mult, op1=ALU.add, in1=acc2[:]),
            waits=[t2_])
        t4_ = p.emit("dve", lambda e, base=base, m=m: e.scalar_tensor_tensor(
            out=acc2[:], in0=RT[:, base + 4: base + 4 + T],
            scalar=cw[:, m * 4 + 3:m * 4 + 4], op0=ALU.mult, op1=ALU.add, in1=acc1[:]),
            waits=[t3_])
        tok_conv[m] = p.emit("dve", lambda e, base=base, m=m: e.tensor_scalar(
            out=RT[:, base + 4: base + 4 + T], in0=acc2[:], scalar1=cb[:, m:m + 1],
            op0=ALU.add, scalar2=None), waits=[t4_])
    p.barrier()
    s1st.close()

    s2st = p.phase_stack()
    wlb = [p.sbuf(f"l1a_wlb{j}", [128, ND * 128], BF16, s2st) for j in range(2)]
    lf_t = [p.sbuf(f"l1a_lft{j}", [128, T], BF16, s2st) for j in range(2)]
    tok_wl = {}
    tok_mm = {}
    tok_ev = {}
    tok_sto = {}

    def wload_l(m):
        tok_wl[m] = load_block_kpj(p, wlb[m % 2], p.dram["leftW_b"][m], ND,
                                   waits=[tok_mm.get((m - 2, NT - 1))])
    wload_l(0)
    wload_l(1)
    for m in range(ND):
        j = m % 2
        for tc in range(NT):
            pb = banks[tc % 4]
            deps = [tok_wl[m], tok_ev.get((m - 1, tc))]

            def l_mm(e, j=j, tc=tc, pb=pb):
                for k in range(ND):
                    instr = e.matmul(pb[:], wlb[j][:, k * 128:(k + 1) * 128],
                                     TN[:, k * T + tc * TCH: k * T + (tc + 1) * TCH],
                                     start=(k == 0), stop=(k == ND - 1))
                return instr
            tok_mm[(m, tc)] = p.emit("pe", l_mm, waits=deps)
            tok_ev[(m, tc)] = p.emit("act", lambda e, pb=pb, m=m, tc=tc, j=j:
                                     e.activation(
                out=lf_t[j][:, tc * TCH:(tc + 1) * TCH], in_=pb[:],
                func=AF.Gelu, bias=lb[:, m:m + 1]),
                waits=[tok_mm[(m, tc)], tok_sto.get(m - 2)])
        if m + 2 < ND:
            wload_l(m + 2)
        tok_sto[m] = p.dma("sp", lf_fm[m * 128:(m + 1) * 128, :], lf_t[m % 2][:],
                           waits=[tok_ev[(m, tc)] for tc in range(NT)])
    p.barrier()
    s2st.close()

    LR = TN
    gst = p.phase_stack()
    At = p.sbuf("l1a_At", [128, T], F32, gst)
    Bt = p.sbuf("l1a_Bt", [128, T], F32, gst)
    rt_g = [p.sbuf(f"l1a_rtg{j}", [128, TCH], BF16, gst) for j in range(2)]
    it_g = [p.sbuf(f"l1a_itg{j}", [128, TCH], BF16, gst) for j in range(2)]
    la_g = [p.sbuf(f"l1a_lag{j}", [128, TCH], F32, gst) for j in range(2)]
    mu_g = [p.sbuf(f"l1a_mug{j}", [128, TCH], F32, gst) for j in range(2)]
    t4_g = [p.sbuf(f"l1a_t4g{j}", [128, TCH], BF16, gst) for j in range(2)]
    rsc = p.sbuf("l1a_rsc", [128, T], BF16, gst)
    lfl = [p.sbuf(f"l1a_lfl{j}", [128, T], BF16, gst) for j in range(2)]
    wab = [p.sbuf(f"l1a_wab{j}", [128, ND * 128], BF16, gst) for j in range(2)]
    wxb = [p.sbuf(f"l1a_wxb{j}", [128, ND * 128], BF16, gst) for j in range(2)]
    tok_wl = {}
    tok_mm = {}
    tok_r = {}
    tok_i = {}
    tok_la = {}
    tok_A = {}
    tok_mu = {}
    tok_B = {}
    tok_scan = {}
    tok_lr = {}
    tok_lfl = {}

    def wload_g(m):
        dep = [tok_mm.get((m - 2, NT - 1))]
        load_block_kpj(p, wab[m % 2], p.dram["gaW_b"][m], ND, waits=dep)
        tok_wl[m] = load_block_kpj(p, wxb[m % 2], p.dram["gxW_b"][m], ND)
    wload_g(0)
    wload_g(1)
    for m in range(ND):
        j = m % 2
        tok_lfl[m] = p.dma("act", lfl[j][:], lf_fm[m * 128:(m + 1) * 128, :],
                           waits=[tok_lr.get(m - 2)])
        for tc in range(NT):
            jt = tc % 2
            ba = banks[jt * 2]
            bx = banks[jt * 2 + 1]
            deps = [tok_wl[m], tok_r.get((m, tc - 2)), tok_i.get((m, tc - 2)),
                    tok_r.get((m - 1, tc + 2)), tok_i.get((m - 1, tc + 2))]

            def g_mm(e, j=j, ba=ba, bx=bx, tc=tc):
                for k in range(ND):
                    e.matmul(ba[:], wab[j][:, k * 128:(k + 1) * 128],
                             RT[:, k * PADT + 4 + tc * TCH: k * PADT + 4 + (tc + 1) * TCH],
                             start=(k == 0), stop=(k == ND - 1))
                for k in range(ND):
                    instr = e.matmul(bx[:], wxb[j][:, k * 128:(k + 1) * 128],
                                     RT[:, k * PADT + 4 + tc * TCH:
                                        k * PADT + 4 + (tc + 1) * TCH],
                                     start=(k == 0), stop=(k == ND - 1))
                return instr
            tok_mm[(m, tc)] = p.emit("pe", g_mm, waits=deps)
            tok_r[(m, tc)] = p.emit("act", lambda e, ba=ba, m=m, jt=jt: e.activation(
                out=rt_g[jt][:], in_=ba[:], func=AF.Sigmoid, bias=gab[:, m:m + 1]),
                waits=[tok_mm[(m, tc)], tok_la.get((m, tc - 2)),
                       tok_la.get((m - 1, tc + 2))])
            tok_i[(m, tc)] = p.emit("act", lambda e, bx=bx, m=m, jt=jt: e.activation(
                out=it_g[jt][:], in_=bx[:], func=AF.Sigmoid, bias=gxb[:, m:m + 1]),
                waits=[tok_mm[(m, tc)], tok_B.get((m, tc - 2)),
                       tok_B.get((m - 1, tc + 2))])
            tok_la[(m, tc)] = p.emit("dve", lambda e, m=m, jt=jt: e.tensor_scalar(
                out=la_g[jt][:], in0=rt_g[jt][:], scalar1=c8[:, m:m + 1], op0=ALU.mult,
                scalar2=None),
                waits=[tok_r[(m, tc)], tok_A.get((m, tc - 2)), tok_mu.get((m, tc - 2)),
                       tok_A.get((m - 1, tc + 2)), tok_mu.get((m - 1, tc + 2))])
            tok_A[(m, tc)] = p.emit("act", lambda e, tc=tc, jt=jt: e.activation(
                out=At[:, tc * TCH:(tc + 1) * TCH], in_=la_g[jt][:], func=AF.Exp),
                waits=[tok_la[(m, tc)], tok_scan.get(m - 1)])
            tmu = p.emit("act", lambda e, jt=jt: e.activation(
                out=mu_g[jt][:], in_=la_g[jt][:], func=AF.Exp, scale=2.0),
                waits=[tok_la[(m, tc)], tok_B.get((m, tc - 2)),
                       tok_B.get((m - 1, tc + 2))])
            tok_mu[(m, tc)] = p.emit("act", lambda e, jt=jt: e.activation(
                out=mu_g[jt][:], in_=mu_g[jt][:], func=AF.Sqrt, scale=-1.0,
                bias=p.sb["cst_c_one"][:]), waits=[tmu])
            tk4 = p.emit("dve", lambda e, m=m, tc=tc, jt=jt: e.tensor_tensor(
                out=t4_g[jt][:], in0=it_g[jt][:],
                in1=RT[:, m * PADT + 4 + tc * TCH: m * PADT + 4 + (tc + 1) * TCH],
                op=ALU.mult),
                waits=[tok_i[(m, tc)], tok_B.get((m, tc - 2)),
                       tok_B.get((m - 1, tc + 2))])
            tok_B[(m, tc)] = p.emit("dve", lambda e, tc=tc, jt=jt: e.tensor_tensor(
                out=Bt[:, tc * TCH:(tc + 1) * TCH], in0=t4_g[jt][:], in1=mu_g[jt][:],
                op=ALU.mult),
                waits=[tk4, tok_mu[(m, tc)], tok_scan.get(m - 1)])
        if m + 2 < ND:
            wload_g(m + 2)
        tok_scan[m] = p.emit("dve", lambda e: e.tensor_tensor_scan(
            out=rsc[:], data0=At[:], data1=Bt[:], initial=0.0,
            op0=ALU.mult, op1=ALU.add),
            waits=[tok_A[(m, tc)] for tc in range(NT)] +
                  [tok_B[(m, tc)] for tc in range(NT)] + [tok_lr.get(m - 1)])
        tok_lr[m] = p.emit("dve", lambda e, m=m, j=j: e.tensor_tensor(
            out=LR[:, m * T:(m + 1) * T], in0=lfl[j][:], in1=rsc[:], op=ALU.mult),
            waits=[tok_scan[m], tok_lfl[m]])
    p.barrier()
    gst.close()

    ost = p.phase_stack()
    wob = [p.sbuf(f"l1a_wob{j}", [128, ND * 128], BF16, ost) for j in range(2)]
    xnk = [p.sbuf(f"l1a_oxn{j}", [128, T], BF16, ost) for j in range(2)]
    tev = [p.sbuf(f"l1a_tev{j}", [128, TCH], F32, ost) for j in range(2)]
    sq2 = [p.sbuf(f"l1a_sq2{j}", [128, T], BF16, ost) for j in range(2)]
    tok_wl = {}
    tok_mm = {}
    tok_x1 = {}
    tok_xn = {}
    tok_sq = {}
    tok_sqm = {}
    tok_sto = {}

    def wload_o(m):
        tok_wl[m] = load_block_kpj(p, wob[m % 2], p.dram["outW_b"][m], ND,
                                   waits=[tok_mm.get((m - 2, NT - 1))])
    wload_o(0)
    wload_o(1)
    tok_xn[0] = p.dma_t(xnk[0][:], xn_tm[0:T, 0:128])
    tok_xn[1] = p.dma_t(xnk[1][:], xn_tm[0:T, 128:256])
    for m2 in range(ND):
        j = m2 % 2
        for tc in range(NT):
            jt = tc % 2
            pb = banks[4 + jt]
            deps = [tok_wl[m2], tok_x1.get((m2, tc - 2)), tok_x1.get((m2 - 1, tc + 2))]

            def o_mm(e, j=j, tc=tc, pb=pb):
                for k in range(ND):
                    instr = e.matmul(pb[:], wob[j][:, k * 128:(k + 1) * 128],
                                     LR[:, k * T + tc * TCH: k * T + (tc + 1) * TCH],
                                     start=(k == 0), stop=(k == ND - 1))
                return instr
            tok_mm[(m2, tc)] = p.emit("pe", o_mm, waits=deps)
            te = p.emit("dve", lambda e, pb=pb, m2=m2, jt=jt: e.tensor_scalar(
                out=tev[jt][:], in0=pb[:], scalar1=ob[:, m2:m2 + 1], op0=ALU.add,
                scalar2=None),
                waits=[tok_mm[(m2, tc)], tok_x1.get((m2, tc - 2)),
                       tok_x1.get((m2 - 1, tc + 2))])
            rt_off = m2 * PADT + 4 + tc * TCH
            tok_x1[(m2, tc)] = p.emit("dve", lambda e, rt_off=rt_off, tc=tc, j=j, jt=jt:
                                      e.tensor_tensor(
                out=RT[:, rt_off:rt_off + TCH], in0=tev[jt][:],
                in1=xnk[j][:, tc * TCH:(tc + 1) * TCH], op=ALU.add),
                waits=[te, tok_xn[m2], tok_sto.get(m2 - 2), tok_sq.get(m2 - 2)])
        if m2 + 2 < ND:
            wload_o(m2 + 2)
            tok_xn[m2 + 2] = p.dma_t(
                xnk[m2 % 2][:], xn_tm[0:T, (m2 + 2) * 128:(m2 + 3) * 128],
                waits=[tok_x1[(m2, tc)] for tc in range(NT)])
        tok_sto[m2] = p.dma("sp", x1_fm[m2 * 128:(m2 + 1) * 128, :],
                            RT[:, m2 * PADT + 4: m2 * PADT + 4 + T],
                            waits=[tok_x1[(m2, tc)] for tc in range(NT)])
        tok_sq[m2] = p.emit("dve", lambda e, m2=m2, j=j: e.tensor_tensor(
            out=sq2[j][:], in0=RT[:, m2 * PADT + 4: m2 * PADT + 4 + T],
            in1=RT[:, m2 * PADT + 4: m2 * PADT + 4 + T], op=ALU.mult),
            waits=[tok_x1[(m2, tc)] for tc in range(NT)] + [tok_sqm.get(m2 - 2)])

        def sq2mm(e, m2=m2, j=j):
            for tc in range(NT):
                instr = e.matmul(banks[tc][0:1, :], ones128[:],
                                 sq2[j][:, tc * TCH:(tc + 1) * TCH],
                                 start=(m2 == 0), stop=(m2 == ND - 1))
            return instr
        tok_sqm[m2] = p.emit("pe", sq2mm, waits=[tok_sq[m2]])
    p.barrier()
    ost.close()
    for tc in range(NT):
        p.emit("act", lambda e, tc=tc, pb=banks[tc]: e.activation(
            out=banks[4 + tc][0:1, 0:TCH], in_=pb[0:1, :], func=AF.Ln, scale=1.0 / D,
            bias=eps8[0:1, :]))
    p.barrier()
    for tc in range(NT):
        tsl = slice(tc * TCH, (tc + 1) * TCH)
        p.emit("act", lambda e, tsl=tsl, tc=tc: e.activation(
            out=rr2b[0:1, tsl], in_=banks[4 + tc][0:1, 0:TCH], func=AF.Exp, scale=-0.5))
    p.barrier()
    for tc in range(NT):
        tsl = slice(tc * TCH, (tc + 1) * TCH)
        p.emit("pe", lambda e, tsl=tsl, pb=banks[tc]: e.matmul(
            pb[:], ones1[:], rr2b[0:1, tsl], start=True, stop=True))
    p.barrier()
    for tc in range(NT):
        tsl = slice(tc * TCH, (tc + 1) * TCH)
        p.emit("act", lambda e, tsl=tsl, pb=banks[tc]: e.activation(
            out=rr2t[:, tsl], in_=pb[:], func=AF.Copy))
    p.barrier()
    X1N = LR
    for k in range(ND):
        for tc in range(NT):
            p.emit("dve", lambda e, k=k, tc=tc: e.scalar_tensor_tensor(
                out=X1N[:, k * T + tc * TCH: k * T + (tc + 1) * TCH],
                in0=RT[:, k * PADT + 4 + tc * TCH: k * PADT + 4 + (tc + 1) * TCH],
                scalar=n2w[:, k:k + 1], op0=ALU.mult, op1=ALU.mult,
                in1=rr2t[:, tc * TCH:(tc + 1) * TCH]))
    p.barrier()

    g1_fm = p.dram["G1_FM"]
    ust = p.phase_stack()
    w1b = [p.sbuf(f"l1a_w1b{j}", [128, ND * 128], BF16, ust) for j in range(2)]
    w2b = [p.sbuf(f"l1a_w2b{j}", [128, ND * 128], BF16, ust) for j in range(2)]
    g1t = [p.sbuf(f"l1a_g1t{j}", [128, TCH], BF16, ust) for j in range(2)]
    gout = [p.sbuf(f"l1a_gout{j}", [128, T], BF16, ust) for j in range(2)]
    tok_wl = {}
    tok_mm = {}
    tok_g1 = {}
    tok_go = {}
    tok_sto = {}

    def wload_u(mc):
        dep = [tok_mm.get((mc - 2, NT - 1))]
        load_block_kpj(p, w1b[mc % 2], p.dram["up1W_b"][mc], ND, waits=dep)
        tok_wl[mc] = load_block_kpj(p, w2b[mc % 2], p.dram["up2W_b"][mc], ND)
    wload_u(0)
    wload_u(1)
    for mc in range(NF):
        j = mc % 2
        for tc in range(NT):
            jt = tc % 2
            b1 = banks[jt * 2]
            b2 = banks[jt * 2 + 1]
            deps = [tok_wl[mc], tok_g1.get((mc, tc - 2)), tok_go.get((mc, tc - 2)),
                    tok_g1.get((mc - 1, tc + 2)), tok_go.get((mc - 1, tc + 2))]

            def up_mm(e, j=j, b1=b1, b2=b2, tc=tc):
                for k in range(ND):
                    e.matmul(b1[:], w1b[j][:, k * 128:(k + 1) * 128],
                             X1N[:, k * T + tc * TCH: k * T + (tc + 1) * TCH],
                             start=(k == 0), stop=(k == ND - 1))
                for k in range(ND):
                    instr = e.matmul(b2[:], w2b[j][:, k * 128:(k + 1) * 128],
                                     X1N[:, k * T + tc * TCH: k * T + (tc + 1) * TCH],
                                     start=(k == 0), stop=(k == ND - 1))
                return instr
            tok_mm[(mc, tc)] = p.emit("pe", up_mm, waits=deps)
            tok_g1[(mc, tc)] = p.emit("act", lambda e, b1=b1, mc=mc, jt=jt: e.activation(
                out=g1t[jt][:], in_=b1[:], func=AF.Gelu, bias=u1b[:, mc:mc + 1]),
                waits=[tok_mm[(mc, tc)], tok_go.get((mc, tc - 2)),
                       tok_go.get((mc - 1, tc + 2))])
            tok_go[(mc, tc)] = p.emit("dve", lambda e, b2=b2, mc=mc, tc=tc, jt=jt:
                                      e.scalar_tensor_tensor(
                out=gout[mc % 2][:, tc * TCH:(tc + 1) * TCH], in0=b2[:],
                scalar=u2b[:, mc:mc + 1], op0=ALU.add, op1=ALU.mult, in1=g1t[jt][:]),
                waits=[tok_mm[(mc, tc)], tok_g1[(mc, tc)], tok_sto.get(mc - 2)])
        if mc + 2 < NF:
            wload_u(mc + 2)
        tok_sto[mc] = p.dma("sp", g1_fm[mc * 128:(mc + 1) * 128, :], gout[mc % 2][:],
                            waits=[tok_go[(mc, tc)] for tc in range(NT)])
    p.barrier()
    ust.close()
    rlst.close()

    dst_ = p.phase_stack()
    TH = 1024
    gc = p.sbuf("l1a_gc", [128, NF * TH], BF16, dst_)
    wdb = [p.sbuf(f"l1a_wdb{j}", [128, NF * 128], BF16, dst_) for j in range(2)]
    mt_ = [p.sbuf(f"l1a_mt{j}", [128, TH], BF16, dst_) for j in range(2)]
    tok_dmm = {}
    tok_ot = {}
    tok_ost = {}
    for th in range(2):
        gdep = [tok_dmm.get((th - 1, ND - 1, 1))]
        for mc in range(NF):
            p.dma("sp" if mc % 2 == 0 else "act", gc[:, mc * TH:(mc + 1) * TH],
                  g1_fm[mc * 128:(mc + 1) * 128, th * TH:(th + 1) * TH],
                  waits=gdep if mc < 2 else None)
        tok_wd = {}

        def wdload(dk, th=th):
            dep = [tok_dmm.get((th, dk - 2, 1)) if dk >= 2
                   else tok_dmm.get((th - 1, ND - 2 + dk, 1))]
            tok_wd[dk] = load_block_kpj(p, wdb[dk % 2], p.dram["downW_b"][dk], NF,
                                        waits=dep)
        wdload(0)
        wdload(1)
        gc_sp = ("d_sp", p.sem_count["d_sp"])
        gc_act = ("d_act", p.sem_count["d_act"])
        for dk in range(ND):
            j = dk % 2
            for stc in range(2):
                pb = banks[(dk % 2) * 2 + stc]
                prev_ot = (tok_ot.get((th, dk - 2, stc)) if dk >= 2
                           else tok_ot.get((th - 1, ND - 2 + dk, stc)))
                deps = [tok_wd[dk], gc_sp, gc_act, prev_ot]

                def dn_mm(e, j=j, stc=stc, pb=pb):
                    for mc in range(NF):
                        instr = e.matmul(
                            pb[:], wdb[j][:, mc * 128:(mc + 1) * 128],
                            gc[:, mc * TH + stc * TCH: mc * TH + (stc + 1) * TCH],
                            start=(mc == 0), stop=(mc == NF - 1))
                    return instr
                tok_dmm[(th, dk, stc)] = p.emit("pe", dn_mm, waits=deps)
                prev_st = (tok_ost.get((th, dk - 2)) if dk >= 2
                           else tok_ost.get((th - 1, ND - 2 + dk)))
                tok_ot[(th, dk, stc)] = p.emit(
                    "dve", lambda e, stc=stc, pb=pb, dk=dk, j=j: e.tensor_scalar(
                        out=mt_[j][:, stc * TCH:(stc + 1) * TCH], in0=pb[:],
                        scalar1=dnb[:, dk:dk + 1], op0=ALU.add, scalar2=None),
                    waits=[tok_dmm[(th, dk, stc)], prev_st])
            if dk + 2 < ND:
                wdload(dk + 2)
            tok_ost[(th, dk)] = p.dma(
                "sp", mlp_fm[dk * 128:(dk + 1) * 128, th * TH:(th + 1) * TH],
                mt_[dk % 2][:],
                waits=[tok_ot[(th, dk, 0)], tok_ot[(th, dk, 1)]])
    p.barrier()
    dst_.close()
    st.close()



NQ = 44

def emit_l1f(p, out_fm):
    st = p.phase_stack()
    banks = [p.psum(f"l1f_ps{i}", stack=st) for i in range(8)]
    xn_tm = p.dram["XN_TM"]
    gq_fm = p.dram["GQ_FM"]
    ones1 = p.sb["cst_ones1x128"]
    ones128 = p.sb["cst_ones128x1"]
    sg = p.sbuf("l1f_sg", [128, NQ], F32, st)
    su = p.sbuf("l1f_su", [128, NQ], F32, st)
    sd = p.sbuf("l1f_sd", [128, 16], F32, st)
    p.dma("sp", sg[:], p.dram["s_gate"])
    p.dma("sp", su[:], p.dram["s_up"])
    p.dma("sp", sd[:], p.dram["s_down"])
    csr = p.sbuf("l1f_csr", [1, T], F32, st)
    csrb = p.sbuf("l1f_csrb", [1, T], BF16, st)
    CSB = p.sbuf("l1f_csb", [128, T], F32, st)
    p.barrier()

    xst = p.phase_stack()
    xn_fm = [p.sbuf(f"l1f_xn{k}", [128, T], BF16, xst) for k in range(ND)]
    emit_load_xn_fm(p, xn_tm, xn_fm, "l1f")
    wst = p.phase_stack()
    wgb = [p.sbuf(f"l1f_wgb{j}", [128, ND * 128], BF16, wst) for j in range(2)]
    wub = [p.sbuf(f"l1f_wub{j}", [128, ND * 128], BF16, wst) for j in range(2)]
    gs_t = [p.sbuf(f"l1f_gst{j}", [128, TCH], BF16, wst) for j in range(2)]
    gout = [p.sbuf(f"l1f_gout{j}", [128, T], BF16, wst) for j in range(2)]
    tok_wl = {}
    tok_mm = {}
    tok_gs = {}
    tok_go = {}
    tok_store = {}

    def wload(mc):
        dep = [tok_mm.get((mc - 2, NT - 1))]
        load_block_kpj(p, wgb[mc % 2], p.dram["wqgT_b"][mc], ND, waits=dep)
        tok_wl[mc] = load_block_kpj(p, wub[mc % 2], p.dram["wquT_b"][mc], ND)
    wload(0)
    wload(1)
    for mc in range(NQ):
        j = mc % 2
        for tc in range(NT):
            bg = banks[(tc % 2) * 2]
            bu = banks[(tc % 2) * 2 + 1]
            deps = [tok_wl[mc], tok_gs.get((mc, tc - 2)), tok_go.get((mc, tc - 2)),
                    tok_gs.get((mc - 1, tc + 2)), tok_go.get((mc - 1, tc + 2))]

            def gu_mm(e, j=j, bg=bg, bu=bu, tc=tc):
                for k in range(ND):
                    e.matmul(bg[:], wgb[j][:, k * 128:(k + 1) * 128],
                             xn_fm[k][:, tc * TCH:(tc + 1) * TCH],
                             start=(k == 0), stop=(k == ND - 1))
                for k in range(ND):
                    instr = e.matmul(bu[:], wub[j][:, k * 128:(k + 1) * 128],
                                     xn_fm[k][:, tc * TCH:(tc + 1) * TCH],
                                     start=(k == 0), stop=(k == ND - 1))
                return instr
            tok_mm[(mc, tc)] = p.emit("pe", gu_mm, waits=deps)
            tok_gs[(mc, tc)] = p.emit("act", lambda e, bg=bg, mc=mc, tc=tc: e.activation(
                out=gs_t[tc % 2][:], in_=bg[:], func=AF.Silu, scale=sg[:, mc:mc + 1]),
                waits=[tok_mm[(mc, tc)], tok_go.get((mc, tc - 2)),
                       tok_go.get((mc - 1, tc + 2))])
            tok_go[(mc, tc)] = p.emit("dve", lambda e, bu=bu, mc=mc, tc=tc:
                                      e.scalar_tensor_tensor(
                out=gout[mc % 2][:, tc * TCH:(tc + 1) * TCH], in0=bu[:],
                scalar=su[:, mc:mc + 1], op0=ALU.mult, op1=ALU.mult,
                in1=gs_t[tc % 2][:]),
                waits=[tok_mm[(mc, tc)], tok_gs[(mc, tc)], tok_store.get(mc - 2)])
        if mc + 2 < NQ:
            wload(mc + 2)
        if mc >= 1:
            def cs_mm(e, mcl=mc - 1):
                for tc in range(NT):
                    instr = e.matmul(banks[4 + tc][0:1, :], ones128[:],
                                     gout[mcl % 2][:, tc * TCH:(tc + 1) * TCH],
                                     start=(mcl == 0), stop=(mcl == NQ - 1))
                return instr
            p.emit("pe", cs_mm, waits=[tok_go[(mc - 1, tc)] for tc in range(NT)])
            tok_store[mc - 1] = p.dma(
                "sp", gq_fm[(mc - 1) * 128:mc * 128, :], gout[(mc - 1) % 2][:],
                waits=[tok_go[(mc - 1, tc)] for tc in range(NT)])
    mcl = NQ - 1

    def cs_mm_last(e, mcl=mcl):
        for tc in range(NT):
            instr = e.matmul(banks[4 + tc][0:1, :], ones128[:],
                             gout[mcl % 2][:, tc * TCH:(tc + 1) * TCH],
                             start=False, stop=(tc == NT - 1) or True)
        return instr
    p.emit("pe", cs_mm_last, waits=[tok_go[(mcl, tc)] for tc in range(NT)])
    tok_store[mcl] = p.dma("sp", gq_fm[mcl * 128:(mcl + 1) * 128, :], gout[mcl % 2][:],
                           waits=[tok_go[(mcl, tc)] for tc in range(NT)])
    p.barrier()
    wst.close()
    xst.close()
    for tc in range(NT):
        tsl = slice(tc * TCH, (tc + 1) * TCH)
        p.emit("act", lambda e, tsl=tsl, pb=banks[4 + tc]: e.activation(
            out=csr[0:1, tsl], in_=pb[0:1, :], func=AF.Copy, scale=128.0))
    p.barrier()
    p.emit("act", lambda e: e.activation(out=csrb[:], in_=csr[:], func=AF.Copy))
    p.barrier()
    for tc in range(NT):
        tsl = slice(tc * TCH, (tc + 1) * TCH)
        p.emit("pe", lambda e, tsl=tsl, pb=banks[tc]: e.matmul(
            pb[:], ones1[:], csrb[0:1, tsl], start=True, stop=True))
    p.barrier()
    for tc in range(NT):
        tsl = slice(tc * TCH, (tc + 1) * TCH)
        p.emit("act", lambda e, tsl=tsl, pb=banks[tc]: e.activation(
            out=CSB[:, tsl], in_=pb[:], func=AF.Copy))
    p.barrier()

    dst_ = p.phase_stack()
    TH = 1024
    gc = p.sbuf("l1f_gc", [128, NQ * TH], BF16, dst_)
    wdb = [p.sbuf(f"l1f_wdb{j}", [128, NQ * 128], BF16, dst_) for j in range(2)]
    t1 = [p.sbuf(f"l1f_t1{j}", [128, TCH], F32, dst_) for j in range(2)]
    ot = [p.sbuf(f"l1f_ot{j}", [128, TH], BF16, dst_) for j in range(2)]
    tok_dmm = {}
    tok_ot = {}
    tok_ost = {}
    for th in range(2):
        gdep = [tok_dmm.get((th - 1, dk, stc)) for dk in (ND - 1, ND - 2)
                for stc in range(2)]
        tok_gc = None
        for mc in range(NQ):
            tok_gc = p.dma("sp" if mc % 2 == 0 else "act",
                           gc[:, mc * TH:(mc + 1) * TH],
                           gq_fm[mc * 128:(mc + 1) * 128, th * TH:(th + 1) * TH],
                           waits=gdep if mc < 2 else None)
        tok_gc2 = tok_gc
        tok_wd = {}

        def wdload(dk, th=th):
            dep = [tok_dmm.get((th, dk - 2, 1)) if dk >= 2
                   else tok_dmm.get((th - 1, ND - 2 + dk, 1))]
            tok_wd[dk] = load_block_kpj(p, wdb[dk % 2], p.dram["wqdT_b"][dk], NQ,
                                        waits=dep)
        wdload(0)
        wdload(1)
        gc_sp = ("d_sp", p.sem_count["d_sp"])
        gc_act = ("d_act", p.sem_count["d_act"])
        for dk in range(ND):
            j = dk % 2
            for stc in range(2):
                pb = banks[(dk % 2) * 2 + stc]
                gsl = slice(th * TH + stc * TCH, th * TH + (stc + 1) * TCH)
                prev_ot = (tok_ot.get((th, dk - 2, stc)) if dk >= 2
                           else tok_ot.get((th - 1, ND - 2 + dk, stc)))
                deps = [tok_wd[dk], gc_sp, gc_act, prev_ot]

                def dn_mm(e, j=j, stc=stc, pb=pb):
                    for mc in range(NQ):
                        instr = e.matmul(
                            pb[:], wdb[j][:, mc * 128:(mc + 1) * 128],
                            gc[:, mc * TH + stc * TCH: mc * TH + (stc + 1) * TCH],
                            start=(mc == 0), stop=(mc == NQ - 1))
                    return instr
                tok_dmm[(th, dk, stc)] = p.emit("pe", dn_mm, waits=deps)
                tk1 = p.emit("dve", lambda e, pb=pb, gsl=gsl, j=j: e.tensor_tensor(
                    out=t1[j][:], in0=pb[:], in1=CSB[:, gsl], op=ALU.subtract),
                    waits=[tok_dmm[(th, dk, stc)], prev_ot])
                prev_st = (tok_ost.get((th, dk - 2)) if dk >= 2
                           else tok_ost.get((th - 1, ND - 2 + dk)))
                tok_ot[(th, dk, stc)] = p.emit(
                    "dve", lambda e, stc=stc, dk=dk, j=j: e.tensor_scalar(
                        out=ot[j][:, stc * TCH:(stc + 1) * TCH], in0=t1[j][:],
                        scalar1=sd[:, dk:dk + 1], op0=ALU.mult, scalar2=None),
                    waits=[tk1, prev_st])
            if dk + 2 < ND:
                wdload(dk + 2)
            tok_ost[(th, dk)] = p.dma(
                "sp", out_fm[dk * 128:(dk + 1) * 128, th * TH:(th + 1) * TH], ot[dk % 2][:],
                waits=[tok_ot[(th, dk, 0)], tok_ot[(th, dk, 1)]])
    p.barrier()
    dst_.close()
    st.close()


L0A_INPUTS = ["nu_log", "theta_log", "BreT", "BimT", "CreT", "CimT", "l0a_D"]
L0F_INPUTS = ["gwT", "moe_wg_hc", "moe_wu_hc", "moe_wd_dk", "shg", "shwg_b", "shwu_b", "shwd_b"]
L1A_INPUTS = ["n1w", "n2w", "leftW_b", "rightW_b", "gaW_b", "gxW_b", "outW_b",
              "left_b", "right_b", "conv_b", "ga_b", "gx_b", "out_b", "a_param",
              "conv_w", "up1_b", "up2_b", "down_b", "up1W_b", "up2W_b", "downW_b"]
L1F_INPUTS = ["wqgT_b", "wquT_b", "wqdT_b", "s_gate", "s_up", "s_down"]

SB_INPUTS = {"l0a": L0A_INPUTS, "l0f": L0F_INPUTS, "l1a": L1A_INPUTS,
             "l1f": L1F_INPUTS}


def build_subblock_program(name):
    p = Prog(name)
    r_in = p.dram_t("r_in", [T, D], F32, "ExternalInput")
    r_out = p.dram_t("r_out", [T, D], F32, "ExternalOutput")
    for n in CONST_NAMES:
        p.dram_t(n, list(np.shape(_HOST[n])), F32, "ExternalInput")
    declare_inputs(p, _HOST, SB_INPUTS[name])
    xn_tm = p.dram_t("XN_TM", [T, D], BF16, "Internal")
    out_fm = p.dram_t("OUT_FM", [D, T], BF16, "Internal")
    load_consts(p)
    emit_ln(p, r_in, xn_tm, name)
    if name == "l0a":
        emit_l0a(p, out_fm)
        adds = [out_fm]
    elif name == "l0f":
        p.dram_t("OUT_ACC", [D, T], F32, "Internal")
        p.dram_t("CBR_D", [8, T], F32, "Internal")
        emit_l0f(p, out_fm)
        adds = [out_fm]
    elif name == "l1a":
        x1_fm = p.dram_t("X1_FM", [D, T], BF16, "Internal")
        p.dram_t("G1_FM", [6144, T], BF16, "Internal")
        p.dram_t("LF_FM", [D, T], BF16, "Internal")
        emit_l1a(p, x1_fm, out_fm)
        adds = [x1_fm, out_fm]
    elif name == "l1f":
        p.dram_t("GQ_FM", [5632, T], BF16, "Internal")
        emit_l1f(p, out_fm)
        adds = [out_fm]
    emit_resid(p, r_in, adds, r_out, name)
    return p.finalize()


_HOST = None
_PROG_CACHE = {}
SB_ORDER = ["l0a", "l0f", "l1a", "l1f"]
FUSED = True


def build_fused_program(blocks=None):
    import os
    blocks = blocks or os.environ.get("FUSE_BLOCKS", "l0a,l0f,l1a,l1f").split(",")
    p = Prog("fused")
    r0 = p.dram_t("r_in", [T, D], F32, "ExternalInput")
    rout = p.dram_t("r_out", [T, D], F32, "ExternalOutput")
    rmid = [p.dram_t(f"rm{i}", [T, D], F32, "Internal") for i in range(len(blocks) - 1)]
    chain_in = [r0] + rmid
    chain_out = rmid + [rout]
    for n in CONST_NAMES:
        p.dram_t(n, list(np.shape(_HOST[n])), F32, "ExternalInput")
    for name in blocks:
        declare_inputs(p, _HOST, SB_INPUTS[name])
    xn_tm = p.dram_t("XN_TM", [T, D], BF16, "Internal")
    out_fm = p.dram_t("OUT_FM", [D, T], BF16, "Internal")
    if "l0f" in blocks:
        p.dram_t("OUT_ACC", [D, T], F32, "Internal")
        p.dram_t("CBR_D", [8, T], F32, "Internal")
    if "l1a" in blocks:
        x1_fm = p.dram_t("X1_FM", [D, T], BF16, "Internal")
        p.dram_t("G1_FM", [6144, T], BF16, "Internal")
        p.dram_t("LF_FM", [D, T], BF16, "Internal")
    if "l1f" in blocks:
        p.dram_t("GQ_FM", [5632, T], BF16, "Internal")
    load_consts(p)
    for i, name in enumerate(blocks):
        rin, rdst = chain_in[i], chain_out[i]
        emit_ln(p, rin, xn_tm, name)
        if name == "l0a":
            emit_l0a(p, out_fm)
            adds = [out_fm]
        elif name == "l0f":
            emit_l0f(p, out_fm)
            adds = [out_fm]
        elif name == "l1a":
            emit_l1a(p, x1_fm, out_fm)
            adds = [x1_fm, out_fm]
        elif name == "l1f":
            emit_l1f(p, out_fm)
            adds = [out_fm]
        emit_resid(p, rin, adds, rdst, name)
    return p.finalize()


def kernel(**inputs):
    global _HOST
    _HOST = host_prep(inputs)
    x = inputs["L0_attn_x"]
    if FUSED:
        if "fused" not in _PROG_CACHE:
            _PROG_CACHE["fused"] = build_fused_program()
        nc = _PROG_CACHE["fused"]
        import os
        blocks = os.environ.get("FUSE_BLOCKS", "l0a,l0f,l1a,l1f").split(",")
        all_names = list(CONST_NAMES)
        for name in blocks:
            all_names += SB_INPUTS[name]
        in_maps = []
        for b in range(8):
            m = {"r_in": np.ascontiguousarray(x[b])}
            for n in all_names:
                m[n] = _HOST[n]
            in_maps.append(m)
        res = run_bass_kernel_spmd(nc, in_maps, core_ids=list(range(8)))
        return np.stack([res.results[b]["r_out"] for b in range(8)], axis=0)
    r = [np.ascontiguousarray(x[b]) for b in range(8)]
    for name in SB_ORDER:
        if name not in _PROG_CACHE:
            _PROG_CACHE[name] = build_subblock_program(name)
        nc = _PROG_CACHE[name]
        in_maps = []
        for b in range(8):
            m = {"r_in": r[b]}
            for n in CONST_NAMES + SB_INPUTS[name]:
                m[n] = _HOST[n]
            in_maps.append(m)
        res = run_bass_kernel_spmd(nc, in_maps, core_ids=list(range(8)))
        r = [res.results[b]["r_out"] for b in range(8)]
    return np.stack(r, axis=0)


# version 05 (final)
# speedup vs baseline: 1.8721x; 1.0219x over previous
import contextlib
import numpy as np
import concourse.bass as bass
import concourse.mybir as mybir
from concourse.bass_utils import run_bass_kernel_spmd

F32 = mybir.dt.float32
BF16 = mybir.dt.bfloat16
I32 = mybir.dt.int32
U8 = mybir.dt.uint8
U32 = mybir.dt.uint32
AF = mybir.ActivationFunctionType
ALU = mybir.AluOpType
AX = mybir.AxisListType

T = 2048
D = 2048
ND = D // 128
TCH = 512
NT = T // TCH
NTT = T // 128
EPS_LN = 1e-5
EPS_RMS = 1e-8
TWO_PI = 6.283185307179586
PI = 3.141592653589793

ENGS = ("pe", "act", "dve", "pool", "sp")
_BLKATTR = {"pe": "tensor", "act": "scalar", "dve": "vector", "pool": "gpsimd", "sp": "sync"}
_ENGSEM = {"pe": ("c_pe", 1), "act": ("c_act", 1), "dve": ("c_dve", 1),
           "pool": ("c_pool", 1), "sp": ("c_sp", 1)}
_DMASEM = {"sp": ("d_sp", 16), "act": ("d_act", 16), "pool": ("d_pool", 16)}


class Prog:
    def __init__(self, name="prog"):
        self.nc = bass.Bass("TRN2", target_bir_lowering=False)
        self.stack = contextlib.ExitStack()
        self.ops = {e: [] for e in ENGS}
        self.sem_count = {}
        self.frontier = {}
        self.eng_waited = {e: {} for e in ENGS}
        self.sems = {}
        for s in ("c_pe", "c_act", "c_dve", "c_pool", "c_sp", "d_sp", "d_act", "d_pool"):
            self.sems[s] = self.stack.enter_context(self.nc.semaphore(s))
            self.sem_count[s] = 0
        self.sb = {}
        self.ps = {}
        self.dram = {}
        self._finalized = False

    def dram_t(self, name, shape, dt, kind):
        ap = self.nc.dram_tensor(name, shape, dt, kind=kind).ap()
        self.dram[name] = ap
        return ap

    def sbuf(self, name, shape, dt, stack=None):
        st = stack if stack is not None else self.stack
        t = st.enter_context(self.nc.sbuf_tensor(name, shape, dt))
        self.sb[name] = t
        return t

    def psum(self, name, shape=(128, 512), dt=F32, stack=None):
        st = stack if stack is not None else self.stack
        t = st.enter_context(self.nc.psum_tensor(name, list(shape), dt))
        self.ps[name] = t
        return t

    def phase_stack(self):
        return contextlib.ExitStack()

    def _wait_closures(self, eng):
        out = []
        w = self.eng_waited[eng]
        for s, v in self.frontier.items():
            if v > 0 and w.get(s, 0) < v:
                sem = self.sems[s]
                out.append(lambda e, sem=sem, v=v: e.wait_ge(sem, v))
                w[s] = v
        return out

    def emit(self, eng, fn, dma=False, mark=True, waits=None):
        self.ops[eng].extend(self._wait_closures(eng))
        if waits:
            w = self.eng_waited[eng]
            for tok in waits:
                if tok is None:
                    continue
                s, v = tok
                if w.get(s, 0) < v:
                    sem = self.sems[s]
                    self.ops[eng].append(lambda e, sem=sem, v=v: e.wait_ge(sem, v))
                    w[s] = v
        if mark:
            sname, amt = _DMASEM[eng] if dma else _ENGSEM[eng]
            sem = self.sems[sname]
            self.sem_count[sname] += amt

            def wrapped(e, fn=fn, sem=sem, amt=amt):
                instr = fn(e)
                instr.then_inc(sem, amt)
            self.ops[eng].append(wrapped)
            return (sname, self.sem_count[sname])
        self.ops[eng].append(lambda e, fn=fn: fn(e))
        return None

    def barrier(self):
        for s, c in self.sem_count.items():
            self.frontier[s] = c

    def finalize(self):
        assert not self._finalized
        self._finalized = True
        self.barrier()
        for e in ENGS:
            if self.ops[e]:
                self.ops[e].extend(self._wait_closures(e))
        blk = self.stack.enter_context(self.nc.Block())
        for e in ENGS:
            if not self.ops[e]:
                continue
            oplist = self.ops[e]

            def body(engobj, oplist=oplist):
                for op in oplist:
                    op(engobj)
            getattr(blk, _BLKATTR[e])(body)
        self.stack.close()
        return self.nc

    def dma(self, queue, out, in_, waits=None, **kw):
        return self.emit(queue, lambda e: e.dma_start(out=out, in_=in_, **kw),
                         dma=True, waits=waits)

    def dma_t(self, out, in_, waits=None, queue="sp"):
        return self.emit(queue, lambda e: e.dma_start_transpose(out=out, in_=in_),
                         dma=True, waits=waits)



def _hl_col(v):
    k = v.shape[0] // 128
    return np.ascontiguousarray(v.reshape(k, 128).T)


def host_prep(inp):
    H = {}
    H["ones1x128"] = np.ones((1, 128), dtype=np.float32)
    H["ones128x1"] = np.ones((128, 1), dtype=np.float32)
    H["iota8"] = np.tile(np.arange(8, dtype=np.float32), (128, 1))
    H["iota128"] = np.tile(np.arange(128, dtype=np.float32), (128, 1))
    H["iota16"] = np.tile(np.arange(16, dtype=np.float32), (128, 1))
    H["identity128"] = np.eye(128, dtype=np.float32)
    H["c_eps5"] = np.full((128, 1), EPS_LN, dtype=np.float32)
    H["c_eps8"] = np.full((128, 1), EPS_RMS, dtype=np.float32)
    H["c_one"] = np.full((128, 1), 1.0, dtype=np.float32)
    H["nu_log"] = _hl_col(inp["L0_attn_nu_log"])
    H["theta_log"] = _hl_col(inp["L0_attn_theta_log"])
    H["l0a_D"] = _hl_col(inp["L0_attn_D"])
    H["BreT"] = np.ascontiguousarray(inp["L0_attn_B_re"].T)
    H["BimT"] = np.ascontiguousarray(inp["L0_attn_B_im"].T)
    H["CreT"] = np.ascontiguousarray(inp["L0_attn_C_re"].T)
    H["CimT"] = np.ascontiguousarray(inp["L0_attn_C_im"].T)
    H["gwT"] = np.ascontiguousarray(inp["L0_ffn_gate_w"].T)
    H["moe_wg_hc"] = np.ascontiguousarray(
        inp["L0_ffn_w_gate"].reshape(8, D, 11, 128).transpose(0, 2, 1, 3))
    H["moe_wu_hc"] = np.ascontiguousarray(
        inp["L0_ffn_w_up"].reshape(8, D, 11, 128).transpose(0, 2, 1, 3))
    H["moe_wd_dk"] = np.ascontiguousarray(
        inp["L0_ffn_w_down"].reshape(8, 1408, ND, 128).transpose(0, 2, 1, 3))
    H["shg"] = _hl_col(inp["L0_ffn_shgate_w"])
    HS = 2816
    H["shwg_b"] = np.ascontiguousarray(
        inp["L0_ffn_shw_gate"].reshape(D, HS // 128, 128).transpose(1, 0, 2))
    H["shwu_b"] = np.ascontiguousarray(
        inp["L0_ffn_shw_up"].reshape(D, HS // 128, 128).transpose(1, 0, 2))
    H["shwd_b"] = np.ascontiguousarray(
        inp["L0_ffn_shw_down"].reshape(22, 128, ND, 128).transpose(2, 0, 1, 3).reshape(ND, 2816, 128))
    H["n1w"] = _hl_col(inp["L1_attn_norm1_w"])
    H["n2w"] = _hl_col(inp["L1_attn_norm2_w"])
    def _mblk(w):
        m = w.shape[1] // 128
        return np.ascontiguousarray(w.reshape(D, m, 128).transpose(1, 0, 2))
    H["leftW_b"] = _mblk(inp["L1_attn_left_W"])
    H["rightW_b"] = _mblk(inp["L1_attn_right_W"])
    H["gaW_b"] = _mblk(inp["L1_attn_ga_W"])
    H["gxW_b"] = _mblk(inp["L1_attn_gx_W"])
    H["outW_b"] = _mblk(inp["L1_attn_out_W"])
    H["left_b"] = _hl_col(inp["L1_attn_left_b"])
    H["right_b"] = _hl_col(inp["L1_attn_right_b"])
    H["conv_b"] = _hl_col(inp["L1_attn_conv_b"])
    H["ga_b"] = _hl_col(inp["L1_attn_ga_b"])
    H["gx_b"] = _hl_col(inp["L1_attn_gx_b"])
    H["out_b"] = _hl_col(inp["L1_attn_out_b"])
    H["a_param"] = _hl_col(inp["L1_attn_a_param"])
    H["conv_w"] = np.ascontiguousarray(
        inp["L1_attn_conv_w"].T.reshape(16, 128, 4).transpose(1, 0, 2).reshape(128, 64))
    F = 6144
    H["up1_b"] = _hl_col(inp["L1_attn_up1_b"])
    H["up2_b"] = _hl_col(inp["L1_attn_up2_b"])
    H["down_b"] = _hl_col(inp["L1_attn_down_b"])
    H["up1W_b"] = _mblk(inp["L1_attn_up1_W"])
    H["up2W_b"] = _mblk(inp["L1_attn_up2_W"])
    H["downW_b"] = np.ascontiguousarray(
        inp["L1_attn_down_W"].reshape(F, ND, 128).transpose(1, 0, 2))
    HQ = 5632
    H["wqgT_b"] = np.ascontiguousarray(
        inp["L1_ffn_wq_gate"].T.reshape(D, HQ // 128, 128).transpose(1, 0, 2))
    H["wquT_b"] = np.ascontiguousarray(
        inp["L1_ffn_wq_up"].T.reshape(D, HQ // 128, 128).transpose(1, 0, 2))
    H["wqdT_b"] = np.ascontiguousarray(
        inp["L1_ffn_wq_down"].T.reshape(HQ, ND, 128).transpose(1, 0, 2))
    H["s_gate"] = _hl_col(inp["L1_ffn_s_gate"])
    H["s_up"] = _hl_col(inp["L1_ffn_s_up"])
    H["s_down"] = _hl_col(inp["L1_ffn_s_down"])
    return H


CONST_NAMES = ["ones1x128", "ones128x1", "iota8", "iota128", "iota16",
               "identity128", "c_eps5", "c_eps8", "c_one"]


def declare_inputs(p, H, names):
    for n in names:
        a = H[n]
        dt = I32 if a.dtype == np.int32 else F32
        p.dram_t(n, list(a.shape), dt, "ExternalInput")


def load_consts(p):
    c = {}
    c["ones1x128"] = p.sbuf("cst_ones1x128", [1, 128], BF16)
    c["ones128x1"] = p.sbuf("cst_ones128x1", [128, 1], BF16)
    for n, shape in [("iota8", [128, 8]), ("iota128", [128, 128]), ("iota16", [128, 16]),
                     ("identity128", [128, 128]), ("c_eps5", [128, 1]), ("c_eps8", [128, 1]),
                     ("c_one", [128, 1])]:
        c[n] = p.sbuf("cst_" + n, shape, F32)
        p.dma("sp", c[n][:], p.dram[n])
    p.dma("pool", c["ones1x128"][:], p.dram["ones1x128"])
    p.dma("pool", c["ones128x1"][:], p.dram["ones128x1"])
    p.barrier()
    return c



def emit_ln(p, r_src, xn_tm, tag):
    st = p.phase_stack()
    PAR = 4
    rt = [p.sbuf(f"{tag}_ln_r{j}", [128, D], F32, st) for j in range(PAR)]
    xt = [p.sbuf(f"{tag}_ln_x{j}", [128, D], BF16, st) for j in range(PAR)]
    stats = [p.sbuf(f"{tag}_ln_s{j}", [128, 24], F32, st) for j in range(PAR)]
    mv = [p.sbuf(f"{tag}_ln_mv{j}", [128, 2], F32, st) for j in range(PAR)]
    rstd = [p.sbuf(f"{tag}_ln_rs{j}", [128, 1], F32, st) for j in range(PAR)]
    eps = p.sb["cst_c_eps5"]
    for i0 in range(0, NTT, PAR):
        for j in range(PAR):
            i = i0 + j
            p.dma("sp", rt[j][:], r_src[i * 128:(i + 1) * 128, :])
        p.barrier()
        for j in range(PAR):
            for q in range(4):
                p.emit("dve", lambda e, j=j, q=q: e.bn_stats(
                    out=stats[j][:, q * 6:(q + 1) * 6], in_=rt[j][:, q * 512:(q + 1) * 512]))
        p.barrier()
        for j in range(PAR):
            p.emit("dve", lambda e, j=j: e.bn_aggr(out=mv[j][:], in_=stats[j][:]))
        p.barrier()
        for j in range(PAR):
            p.emit("act", lambda e, j=j: e.activation(
                out=rstd[j][:], in_=mv[j][:, 1:2], func=AF.Ln, bias=eps[:]))
        p.barrier()
        for j in range(PAR):
            p.emit("act", lambda e, j=j: e.activation(
                out=rstd[j][:], in_=rstd[j][:], func=AF.Exp, scale=-0.5))
        p.barrier()
        for j in range(PAR):
            p.emit("dve", lambda e, j=j: e.tensor_scalar(
                out=xt[j][:], in0=rt[j][:], scalar1=mv[j][:, 0:1], scalar2=rstd[j][:],
                op0=ALU.subtract, op1=ALU.mult))
        p.barrier()
        for j in range(PAR):
            i = i0 + j
            p.dma("sp", xn_tm[i * 128:(i + 1) * 128, :], xt[j][:])
        p.barrier()
    st.close()


def emit_load_xn_fm(p, xn_tm, xn_fm, tag):
    for k in range(ND):
        p.dma_t(xn_fm[k][:], xn_tm[0:T, k * 128:(k + 1) * 128])
    p.barrier()


def emit_resid(p, r_src, adds_fm, r_dst, tag):
    st = p.phase_stack()
    PAR = 2
    rt = [p.sbuf(f"{tag}_rs_r{j}", [128, D], F32, st) for j in range(PAR)]
    at = [[p.sbuf(f"{tag}_rs_a{j}_{m}", [128, D], BF16, st) for m in range(len(adds_fm))]
          for j in range(PAR)]
    for i0 in range(0, NTT, PAR):
        for j in range(PAR):
            i = i0 + j
            p.dma("sp", rt[j][:], r_src[i * 128:(i + 1) * 128, :])
            for m, a in enumerate(adds_fm):
                p.dma_t(at[j][m][:], a[0:D, i * 128:(i + 1) * 128])
        p.barrier()
        for m in range(len(adds_fm)):
            for j in range(PAR):
                p.emit("dve", lambda e, j=j, m=m: e.tensor_tensor(
                    out=rt[j][:], in0=rt[j][:], in1=at[j][m][:], op=ALU.add))
            p.barrier()
        for j in range(PAR):
            i = i0 + j
            p.dma("sp", r_dst[i * 128:(i + 1) * 128, :], rt[j][:])
        p.barrier()
    st.close()


def emit_resid_ln(p, r_src, adds_fm, r_dst, xn_tm, tag):
    st = p.phase_stack()
    PAR = 2
    rt = [p.sbuf(f"{tag}_rl_r{j}", [128, D], F32, st) for j in range(PAR)]
    at = [[p.sbuf(f"{tag}_rl_a{j}_{m}", [128, D], BF16, st) for m in range(len(adds_fm))]
          for j in range(PAR)]
    xt = [p.sbuf(f"{tag}_rl_x{j}", [128, D], BF16, st) for j in range(PAR)]
    stats = [p.sbuf(f"{tag}_rl_s{j}", [128, 24], F32, st) for j in range(PAR)]
    mv = [p.sbuf(f"{tag}_rl_mv{j}", [128, 2], F32, st) for j in range(PAR)]
    rstd = [p.sbuf(f"{tag}_rl_rs{j}", [128, 1], F32, st) for j in range(PAR)]
    eps = p.sb["cst_c_eps5"]
    for i0 in range(0, NTT, PAR):
        for j in range(PAR):
            i = i0 + j
            p.dma("sp", rt[j][:], r_src[i * 128:(i + 1) * 128, :])
            for m, a in enumerate(adds_fm):
                p.dma_t(at[j][m][:], a[0:D, i * 128:(i + 1) * 128])
        p.barrier()
        for m in range(len(adds_fm)):
            for j in range(PAR):
                p.emit("dve", lambda e, j=j, m=m: e.tensor_tensor(
                    out=rt[j][:], in0=rt[j][:], in1=at[j][m][:], op=ALU.add))
            p.barrier()
        for j in range(PAR):
            i = i0 + j
            p.dma("sp", r_dst[i * 128:(i + 1) * 128, :], rt[j][:])
            for q in range(4):
                p.emit("dve", lambda e, j=j, q=q: e.bn_stats(
                    out=stats[j][:, q * 6:(q + 1) * 6], in_=rt[j][:, q * 512:(q + 1) * 512]))
        p.barrier()
        for j in range(PAR):
            p.emit("dve", lambda e, j=j: e.bn_aggr(out=mv[j][:], in_=stats[j][:]))
        p.barrier()
        for j in range(PAR):
            p.emit("act", lambda e, j=j: e.activation(
                out=rstd[j][:], in_=mv[j][:, 1:2], func=AF.Ln, bias=eps[:]))
        p.barrier()
        for j in range(PAR):
            p.emit("act", lambda e, j=j: e.activation(
                out=rstd[j][:], in_=rstd[j][:], func=AF.Exp, scale=-0.5))
        p.barrier()
        for j in range(PAR):
            p.emit("dve", lambda e, j=j: e.tensor_scalar(
                out=xt[j][:], in0=rt[j][:], scalar1=mv[j][:, 0:1], scalar2=rstd[j][:],
                op0=ALU.subtract, op1=ALU.mult))
        p.barrier()
        for j in range(PAR):
            i = i0 + j
            p.dma("sp", xn_tm[i * 128:(i + 1) * 128, :], xt[j][:])
        p.barrier()
    st.close()


def emit_softplus(p, out, in_, tmp, tag):
    one = p.sb["cst_c_one"]
    p.emit("dve", lambda e: e.tensor_scalar(out=tmp[:], in0=in_[:], scalar1=-1.0, op0=ALU.mult,
                                            scalar2=None))
    p.barrier()
    p.emit("dve", lambda e: e.tensor_tensor(out=tmp[:], in0=tmp[:], in1=in_[:], op=ALU.max))
    p.barrier()
    p.emit("dve", lambda e: e.tensor_scalar(out=tmp[:], in0=tmp[:], scalar1=-1.0, op0=ALU.mult,
                                            scalar2=None))
    p.barrier()
    p.emit("act", lambda e: e.activation(out=tmp[:], in_=tmp[:], func=AF.Exp))
    p.barrier()
    p.emit("act", lambda e: e.activation(out=tmp[:], in_=tmp[:], func=AF.Ln, bias=one[:]))
    p.barrier()
    p.emit("dve", lambda e: e.tensor_scalar(out=out[:], in0=in_[:], scalar1=0.0, op0=ALU.max,
                                            scalar2=None))
    p.barrier()
    p.emit("dve", lambda e: e.tensor_tensor(out=out[:], in0=out[:], in1=tmp[:], op=ALU.add))
    p.barrier()


def emit_sin_pm_pi(p, out, ang, tag):
    p.emit("act", lambda e: e.activation(out=out[:], in_=ang[:], func=AF.Sin))


def emit_mod_2pi(p, out, in_, itmp, ftmp, tag):
    p.emit("dve", lambda e: e.tensor_scalar(out=out[:], in0=in_[:], scalar1=1.0 / TWO_PI,
                                            op0=ALU.mult, scalar2=None))
    p.barrier()
    p.emit("dve", lambda e: e.tensor_copy(out=itmp[:], in_=out[:]))
    p.barrier()
    p.emit("dve", lambda e: e.tensor_copy(out=ftmp[:], in_=itmp[:]))
    p.barrier()
    p.emit("dve", lambda e: e.tensor_tensor(out=itmp[:], in0=ftmp[:], in1=out[:], op=ALU.is_gt))
    p.barrier()
    p.emit("dve", lambda e: e.scalar_tensor_tensor(out=ftmp[:], in0=itmp[:], scalar=-1.0,
                                                   op0=ALU.mult, op1=ALU.add, in1=ftmp[:]))
    p.barrier()
    p.emit("dve", lambda e: e.tensor_tensor(out=out[:], in0=out[:], in1=ftmp[:], op=ALU.subtract))
    p.barrier()
    p.emit("dve", lambda e: e.tensor_scalar(out=out[:], in0=out[:], scalar1=TWO_PI,
                                            op0=ALU.mult, scalar2=None))
    p.barrier()


def emit_fold_to_pi(p, ang, msk, tag):
    p.emit("dve", lambda e: e.tensor_scalar(out=msk[:], in0=ang[:], scalar1=PI, op0=ALU.is_gt,
                                            scalar2=None))
    p.barrier()
    p.emit("dve", lambda e: e.scalar_tensor_tensor(out=ang[:], in0=msk[:], scalar=-TWO_PI,
                                                   op0=ALU.mult, op1=ALU.add, in1=ang[:]))
    p.barrier()



def emit_l0a(p, out_fm):
    NS = 2
    st = p.phase_stack()
    nu_l = p.sbuf("l0a_nul", [128, 2], F32, st)
    th_l = p.sbuf("l0a_thl", [128, 2], F32, st)
    Dv = p.sbuf("l0a_Dv", [128, 16], F32, st)
    p.dma("sp", nu_l[:], p.dram["nu_log"])
    p.dma("sp", th_l[:], p.dram["theta_log"])
    p.dma("sp", Dv[:], p.dram["l0a_D"])
    p.barrier()
    nu = p.sbuf("l0a_nu", [128, 2], F32, st)
    tmp2 = p.sbuf("l0a_tmp2", [128, 2], F32, st)
    emit_softplus(p, nu, nu_l, tmp2, "l0a")
    mag = p.sbuf("l0a_mag", [128, 2], F32, st)
    theta = p.sbuf("l0a_theta", [128, 2], F32, st)
    gam = p.sbuf("l0a_gam", [128, 2], F32, st)
    p.emit("act", lambda e: e.activation(out=mag[:], in_=nu[:], func=AF.Exp))
    p.emit("act", lambda e: e.activation(out=theta[:], in_=th_l[:], func=AF.Exp))
    p.barrier()
    p.emit("act", lambda e: e.activation(out=mag[:], in_=mag[:], func=AF.Exp, scale=-1.0))
    p.barrier()
    p.emit("act", lambda e: e.activation(out=gam[:], in_=mag[:], func=AF.Square))
    p.barrier()
    p.emit("dve", lambda e: e.tensor_scalar(out=gam[:], in0=gam[:], scalar1=-1.0, scalar2=1.0,
                                            op0=ALU.mult, op1=ALU.add))
    p.barrier()
    p.emit("dve", lambda e: e.tensor_scalar(out=gam[:], in0=gam[:], scalar1=0.0, op0=ALU.max,
                                            scalar2=None))
    p.barrier()
    p.emit("act", lambda e: e.activation(out=gam[:], in_=gam[:], func=AF.Sqrt))
    p.barrier()

    iota128 = p.sb["cst_iota128"]
    iota16 = p.sb["cst_iota16"]
    ang_j = p.sbuf("l0a_angj", [128, 128], F32, st)
    itmp = p.sbuf("l0a_itmp", [128, 128], I32, st)
    ftmp = p.sbuf("l0a_ftmp", [128, 128], F32, st)
    cj = p.sbuf("l0a_cj", [128, 128], F32, st)
    sj = p.sbuf("l0a_sj", [128, 128], F32, st)
    ang_c = p.sbuf("l0a_angc", [128, 16], F32, st)
    itmp16 = p.sbuf("l0a_itmp16", [128, 16], I32, st)
    ftmp16 = p.sbuf("l0a_ftmp16", [128, 16], F32, st)
    cc = p.sbuf("l0a_cc", [128, 16], F32, st)
    sc = p.sbuf("l0a_sc", [128, 16], F32, st)
    th128 = p.sbuf("l0a_th128", [128, 2], F32, st)
    itmp2 = p.sbuf("l0a_itmp2", [128, 2], I32, st)
    p.emit("dve", lambda e: e.tensor_scalar(out=th128[:], in0=theta[:], scalar1=128.0,
                                            op0=ALU.mult, scalar2=None))
    p.barrier()
    emit_mod_2pi(p, th128, th128, itmp2, tmp2, "l0a")
    tabs = {}
    s1 = p.sbuf("l0a_s1", [128, T], F32, st)
    s2 = p.sbuf("l0a_s2", [128, T], F32, st)
    for ns in range(NS):
        for nm in ("re", "im"):
            tabs[(ns, nm)] = p.sbuf(f"l0a_tab{ns}{nm}", [128, T], F32, st)
        thc = theta[:, ns:ns + 1]
        p.emit("dve", lambda e, thc=thc: e.tensor_scalar(
            out=ang_j[:], in0=iota128[:], scalar1=thc, op0=ALU.mult, scalar2=None))
        p.barrier()
        emit_mod_2pi(p, ang_j, ang_j, itmp, ftmp, "l0a")
        p.emit("dve", lambda e: e.tensor_copy(out=sj[:], in_=ang_j[:]))
        p.barrier()
        emit_fold_to_pi(p, sj, ftmp, "l0a")
        p.emit("act", lambda e: e.activation(out=sj[:], in_=sj[:], func=AF.Sin))
        p.barrier()
        p.emit("dve", lambda e: e.tensor_scalar(out=cj[:], in0=ang_j[:], scalar1=-1.0,
                                                scalar2=PI / 2, op0=ALU.mult, op1=ALU.add))
        p.barrier()
        p.emit("dve", lambda e: e.tensor_scalar(out=ftmp[:], in0=cj[:], scalar1=-PI,
                                                op0=ALU.is_lt, scalar2=None))
        p.barrier()
        p.emit("dve", lambda e: e.scalar_tensor_tensor(out=cj[:], in0=ftmp[:], scalar=TWO_PI,
                                                       op0=ALU.mult, op1=ALU.add, in1=cj[:]))
        p.barrier()
        p.emit("act", lambda e: e.activation(out=cj[:], in_=cj[:], func=AF.Sin))
        p.barrier()
        th128c = th128[:, ns:ns + 1]
        p.emit("dve", lambda e, th128c=th128c: e.tensor_scalar(
            out=ang_c[:], in0=iota16[:], scalar1=th128c, op0=ALU.mult, scalar2=None))
        p.barrier()
        emit_mod_2pi(p, ang_c, ang_c, itmp16, ftmp16, "l0a")
        p.emit("dve", lambda e: e.tensor_copy(out=sc[:], in_=ang_c[:]))
        p.barrier()
        emit_fold_to_pi(p, sc, ftmp16, "l0a")
        p.emit("act", lambda e: e.activation(out=sc[:], in_=sc[:], func=AF.Sin))
        p.barrier()
        p.emit("dve", lambda e: e.tensor_scalar(out=cc[:], in0=ang_c[:], scalar1=-1.0,
                                                scalar2=PI / 2, op0=ALU.mult, op1=ALU.add))
        p.barrier()
        p.emit("dve", lambda e: e.tensor_scalar(out=ftmp16[:], in0=cc[:], scalar1=-PI,
                                                op0=ALU.is_lt, scalar2=None))
        p.barrier()
        p.emit("dve", lambda e: e.scalar_tensor_tensor(out=cc[:], in0=ftmp16[:], scalar=TWO_PI,
                                                       op0=ALU.mult, op1=ALU.add, in1=cc[:]))
        p.barrier()
        p.emit("act", lambda e: e.activation(out=cc[:], in_=cc[:], func=AF.Sin))
        p.barrier()
        Tre, Tim = tabs[(ns, "re")], tabs[(ns, "im")]
        for c in range(16):
            sl = slice(c * 128, (c + 1) * 128)
            ccc = cc[:, c:c + 1]
            scc = sc[:, c:c + 1]
            p.emit("dve", lambda e, sl=sl, ccc=ccc, Tre=Tre: e.tensor_scalar(
                out=Tre[:, sl], in0=cj[:], scalar1=ccc, op0=ALU.mult, scalar2=None))
            p.emit("dve", lambda e, sl=sl, ccc=ccc, Tim=Tim: e.tensor_scalar(
                out=Tim[:, sl], in0=sj[:], scalar1=ccc, op0=ALU.mult, scalar2=None))
            p.emit("dve", lambda e, sl=sl, scc=scc: e.tensor_scalar(
                out=s1[:, sl], in0=sj[:], scalar1=scc, op0=ALU.mult, scalar2=None))
            p.emit("dve", lambda e, sl=sl, scc=scc: e.tensor_scalar(
                out=s2[:, sl], in0=cj[:], scalar1=scc, op0=ALU.mult, scalar2=None))
        p.barrier()
        p.emit("dve", lambda e, Tre=Tre: e.tensor_tensor(out=Tre[:], in0=Tre[:], in1=s1[:],
                                                         op=ALU.subtract))
        p.emit("dve", lambda e, Tim=Tim: e.tensor_tensor(out=Tim[:], in0=Tim[:], in1=s2[:],
                                                         op=ALU.add))
        p.barrier()

    BU = {}
    for nm in ("re", "im"):
        for ns in range(NS):
            BU[(ns, nm)] = p.sbuf(f"l0a_bu{ns}{nm}", [128, T], F32, st)
    bst = p.phase_stack()
    bre = [p.sbuf(f"l0a_bre{k}", [128, 256], BF16, bst) for k in range(ND)]
    bim = [p.sbuf(f"l0a_bim{k}", [128, 256], BF16, bst) for k in range(ND)]
    for k in range(ND):
        p.dma("pool", bre[k][:], p.dram["BreT"][k * 128:(k + 1) * 128, :])
        p.dma("pool", bim[k][:], p.dram["BimT"][k * 128:(k + 1) * 128, :])
    xnt = [p.sbuf(f"l0a_xnt{j}", [128, TCH], BF16, bst) for j in range(2 * NT)]
    banks = [p.psum(f"l0a_psb{i}", stack=bst) for i in range(8)]
    p.barrier()
    xn_tm = p.dram["XN_TM"]
    for nm, btiles in (("re", bre), ("im", bim)):
        for k in range(ND):
            for tc in range(NT):
                j = (k % 2) * NT + tc
                p.dma_t(xnt[j][:], xn_tm[tc * TCH:(tc + 1) * TCH, k * 128:(k + 1) * 128])
            p.barrier()

            def mmk(e, btiles=btiles, k=k):
                for ns in range(NS):
                    for tc in range(NT):
                        j = (k % 2) * NT + tc
                        instr = e.matmul(banks[ns * NT + tc][:],
                                         btiles[k][:, ns * 128:(ns + 1) * 128],
                                         xnt[j][:], start=(k == 0), stop=(k == ND - 1))
                return instr
            p.emit("pe", mmk)
            p.barrier()
        for ns in range(NS):
            gc = gam[:, ns:ns + 1]
            bu = BU[(ns, nm)]
            for tc in range(NT):
                tsl = slice(tc * TCH, (tc + 1) * TCH)
                p.emit("dve", lambda e, bu=bu, tsl=tsl, gc=gc, b=banks[ns * NT + tc]:
                       e.tensor_scalar(out=bu[:, tsl], in0=b[:], scalar1=gc,
                                       op0=ALU.mult, scalar2=None))
        p.barrier()
    bst.close()

    for ns in range(NS):
        Tre, Tim = tabs[(ns, "re")], tabs[(ns, "im")]
        bre_, bim_ = BU[(ns, "re")], BU[(ns, "im")]
        p.emit("dve", lambda e, Tim=Tim, bim_=bim_: e.tensor_tensor(
            out=s1[:], in0=Tim[:], in1=bim_[:], op=ALU.mult))
        p.emit("dve", lambda e, Tim=Tim, bre_=bre_: e.tensor_tensor(
            out=s2[:], in0=Tim[:], in1=bre_[:], op=ALU.mult))
        p.barrier()
        p.emit("dve", lambda e, Tre=Tre, bre_=bre_: e.tensor_tensor(
            out=bre_[:], in0=Tre[:], in1=bre_[:], op=ALU.mult))
        p.barrier()
        p.emit("dve", lambda e, bre_=bre_: e.tensor_tensor(
            out=bre_[:], in0=bre_[:], in1=s1[:], op=ALU.add))
        p.emit("dve", lambda e, Tre=Tre, bim_=bim_: e.tensor_tensor(
            out=bim_[:], in0=Tre[:], in1=bim_[:], op=ALU.mult))
        p.barrier()
        p.emit("dve", lambda e, bim_=bim_: e.tensor_tensor(
            out=bim_[:], in0=bim_[:], in1=s2[:], op=ALU.subtract))
        p.barrier()

    mt = p.sbuf("l0a_mt", [128, T], F32, st)
    G = {}
    for nm in ("re", "im"):
        for ns in range(NS):
            G[(ns, nm)] = p.sbuf(f"l0a_g{ns}{nm}", [128, T], BF16, st)
    for ns in range(NS):
        p.emit("dve", lambda e: e.memset(mt[:], 1.0))
        p.barrier()
        mc = mag[:, ns:ns + 1]
        p.emit("dve", lambda e, mc=mc: e.tensor_scalar(
            out=mt[:], in0=mt[:], scalar1=mc, op0=ALU.mult, scalar2=None))
        p.barrier()
        for nm in ("re", "im"):
            g_ = G[(ns, nm)]
            v_ = BU[(ns, nm)]
            p.emit("dve", lambda e, g_=g_, v_=v_: e.tensor_tensor_scan(
                out=g_[:], data0=mt[:], data1=v_[:], initial=0.0,
                op0=ALU.mult, op1=ALU.add))
            p.barrier()

    H = {}
    for nm in ("re", "imN"):
        for ns in range(NS):
            H[(ns, nm)] = p.sbuf(f"l0a_h{ns}{nm}", [128, T], BF16, st)
    for ns in range(NS):
        Tre, Tim = tabs[(ns, "re")], tabs[(ns, "im")]
        gre, gim = G[(ns, "re")], G[(ns, "im")]
        p.emit("dve", lambda e, Tre=Tre, gre=gre: e.tensor_tensor(
            out=s1[:], in0=Tre[:], in1=gre[:], op=ALU.mult))
        p.emit("dve", lambda e, Tim=Tim, gim=gim: e.tensor_tensor(
            out=s2[:], in0=Tim[:], in1=gim[:], op=ALU.mult))
        p.barrier()
        hre = H[(ns, "re")]
        p.emit("dve", lambda e, hre=hre: e.tensor_tensor(
            out=hre[:], in0=s1[:], in1=s2[:], op=ALU.subtract))
        p.barrier()
        p.emit("dve", lambda e, Tre=Tre, gim=gim: e.tensor_tensor(
            out=s1[:], in0=Tre[:], in1=gim[:], op=ALU.mult))
        p.emit("dve", lambda e, Tim=Tim, gre=gre: e.tensor_tensor(
            out=s2[:], in0=Tim[:], in1=gre[:], op=ALU.mult))
        p.barrier()
        p.emit("dve", lambda e: e.tensor_tensor(
            out=s1[:], in0=s1[:], in1=s2[:], op=ALU.add))
        p.barrier()
        hin = H[(ns, "imN")]
        p.emit("dve", lambda e, hin=hin: e.tensor_scalar(
            out=hin[:], in0=s1[:], scalar1=-1.0, op0=ALU.mult, scalar2=None))
        p.barrier()

    yst = p.phase_stack()
    cre = [p.sbuf(f"l0a_cre{ns}", [128, D], BF16, yst) for ns in range(NS)]
    cim = [p.sbuf(f"l0a_cim{ns}", [128, D], BF16, yst) for ns in range(NS)]
    for ns in range(NS):
        p.dma("pool", cre[ns][:], p.dram["CreT"][ns * 128:(ns + 1) * 128, :])
        p.dma("pool", cim[ns][:], p.dram["CimT"][ns * 128:(ns + 1) * 128, :])
    ot = p.sbuf("l0a_ot", [128, T], BF16, yst)
    xnk = [p.sbuf(f"l0a_yxn{j}", [128, T], BF16, yst) for j in range(2)]
    banks2 = [p.psum(f"l0a_yps{i}", stack=yst) for i in range(4)]
    p.barrier()
    for k in range(ND):
        xb = xnk[k % 2]
        p.dma_t(xb[:], xn_tm[0:T, k * 128:(k + 1) * 128])
        for tc in range(NT):
            tsl = slice(tc * TCH, (tc + 1) * TCH)

            def ymm(e, k=k, tsl=tsl, pb=banks2[tc]):
                first = True
                for ns in range(NS):
                    e.matmul(pb[:], cre[ns][:, k * 128:(k + 1) * 128],
                             H[(ns, "re")][:, tsl], start=first, stop=False)
                    first = False
                    instr = e.matmul(pb[:], cim[ns][:, k * 128:(k + 1) * 128],
                                     H[(ns, "imN")][:, tsl], start=False, stop=(ns == NS - 1))
                return instr
            p.emit("pe", ymm)
        p.barrier()
        dc = Dv[:, k:k + 1]
        for tc in range(NT):
            tsl = slice(tc * TCH, (tc + 1) * TCH)
            p.emit("dve", lambda e, tsl=tsl, dc=dc, xb=xb, pb=banks2[tc]:
                   e.scalar_tensor_tensor(out=ot[:, tsl], in0=xb[:, tsl], scalar=dc,
                                          op0=ALU.mult, op1=ALU.add, in1=pb[:]))
        p.barrier()
        p.dma("sp", out_fm[k * 128:(k + 1) * 128, :], ot[:])
        p.barrier()
    yst.close()
    st.close()



def load_block_kpj(p, dst, src2d, K, waits=None, queue="pool"):
    return p.dma(queue, dst[:].rearrange("p (k j) -> p k j", j=128),
                 src2d.rearrange("(k p) j -> p k j", p=128), waits=waits)


def emit_l0f(p, out_fm):
    st = p.phase_stack()
    banks = [p.psum(f"l0f_ps{i}", stack=st) for i in range(8)]
    xn_fm = [p.sbuf(f"l0f_xn{k}", [128, T], BF16, st) for k in range(ND)]
    cb8 = p.sbuf("l0f_cb8", [8, 128], F32, st)
    SGRb = p.sbuf("l0f_sgr", [1, T], BF16, st)
    cbr_d = p.dram["CBR_D"]
    out_acc = p.dram["OUT_ACC"]
    emit_load_xn_fm(p, p.dram["XN_TM"], xn_fm, "l0f")

    rst = p.phase_stack()
    gwt = p.sbuf("l0f_gwt", [128, ND * 8], BF16, rst)
    shgb = p.sbuf("l0f_shgb", [128, 16], BF16, rst)
    probs = p.sbuf("l0f_probs", [128, 8], F32, rst)
    srt = p.sbuf("l0f_srt", [128, 8], F32, rst)
    idx = p.sbuf("l0f_idx", [128, 8], U32, rst)
    cmb = p.sbuf("l0f_cmb", [128, 8], F32, rst)
    msk = p.sbuf("l0f_msk", [128, 8], F32, rst)
    m1 = p.sbuf("l0f_m1", [128, 1], F32, rst)
    w1 = p.sbuf("l0f_w1", [128, 1], F32, rst)
    w2 = p.sbuf("l0f_w2", [128, 1], F32, rst)
    i1f = p.sbuf("l0f_i1f", [128, 1], F32, rst)
    i2f = p.sbuf("l0f_i2f", [128, 1], F32, rst)
    for k in range(ND):
        p.dma("pool", gwt[:, k * 8:(k + 1) * 8], p.dram["gwT"][k * 128:(k + 1) * 128, :])
    p.dma("pool", shgb[:], p.dram["shg"])
    iota8 = p.sb["cst_iota8"]
    ident = p.sb["cst_identity128"]
    p.barrier()
    for i in range(NTT):
        tsl128 = slice(i * 128, (i + 1) * 128)
        pb = banks[0]

        def logit_mm(e, tsl128=tsl128, pb=pb):
            for k in range(ND):
                instr = e.matmul(pb[:, 0:8], xn_fm[k][:, tsl128], gwt[:, k * 8:(k + 1) * 8],
                                 start=(k == 0), stop=(k == ND - 1))
            return instr
        p.emit("pe", logit_mm)
        p.barrier()
        p.emit("dve", lambda e, pb=pb: e.reduce_max(out=m1[:], in_=pb[:, 0:8], axis=AX.X))
        p.barrier()
        p.emit("dve", lambda e: e.tensor_scalar(out=m1[:], in0=m1[:], scalar1=-1.0,
                                                op0=ALU.mult, scalar2=None))
        p.barrier()
        p.emit("act", lambda e, pb=pb: e.activation(out=probs[:], in_=pb[:, 0:8],
                                                    func=AF.Exp, bias=m1[:]))
        p.barrier()
        p.emit("dve", lambda e: e.reduce_sum(out=w1[:], in_=probs[:], axis=AX.X))
        p.barrier()
        p.emit("dve", lambda e: e.reciprocal(out=w2[:], in_=w1[:]))
        p.barrier()
        p.emit("dve", lambda e: e.tensor_scalar(out=probs[:], in0=probs[:], scalar1=w2[:],
                                                op0=ALU.mult, scalar2=None))
        p.barrier()
        p.emit("dve", lambda e: e.max(out=srt[:], in_=probs[:]))
        p.barrier()
        p.emit("dve", lambda e: e.max_index(out=idx[:], in_max=srt[:], in_values=probs[:]))
        p.barrier()
        p.emit("dve", lambda e: e.tensor_tensor(out=w1[:], in0=srt[:, 0:1], in1=srt[:, 1:2],
                                                op=ALU.add))
        p.barrier()
        p.emit("dve", lambda e: e.reciprocal(out=w1[:], in_=w1[:]))
        p.barrier()
        p.emit("dve", lambda e: e.tensor_scalar(out=w2[:], in0=srt[:, 1:2], scalar1=w1[:],
                                                op0=ALU.mult, scalar2=None))
        p.barrier()
        p.emit("dve", lambda e: e.tensor_scalar(out=w1[:], in0=srt[:, 0:1], scalar1=w1[:],
                                                op0=ALU.mult, scalar2=None))
        p.barrier()
        p.emit("dve", lambda e: e.tensor_copy(out=i1f[:], in_=idx[:, 0:1]))
        p.emit("dve", lambda e: e.tensor_copy(out=i2f[:], in_=idx[:, 1:2]))
        p.barrier()
        p.emit("dve", lambda e: e.tensor_scalar(out=cmb[:], in0=iota8[:], scalar1=i1f[:],
                                                op0=ALU.is_equal, scalar2=None))
        p.emit("dve", lambda e: e.tensor_scalar(out=msk[:], in0=iota8[:], scalar1=i2f[:],
                                                op0=ALU.is_equal, scalar2=None))
        p.barrier()
        p.emit("dve", lambda e: e.tensor_scalar(out=cmb[:], in0=cmb[:], scalar1=w1[:],
                                                op0=ALU.mult, scalar2=None))
        p.barrier()
        p.emit("dve", lambda e: e.scalar_tensor_tensor(out=cmb[:], in0=msk[:], scalar=w2[:],
                                                       op0=ALU.mult, op1=ALU.add, in1=cmb[:]))
        p.barrier()
        p.emit("pe", lambda e, pb=banks[1]: e.transpose(pb[0:8, 0:128], cmb[:], ident[:]))
        p.barrier()
        p.emit("dve", lambda e, pb=banks[1]: e.tensor_copy(
            out=cb8[:], in_=pb[0:8, 0:128]))
        p.barrier()
        p.dma("sp", cbr_d[0:8, tsl128], cb8[:])
        p.barrier()
    for tc in range(NT):
        tsl = slice(tc * TCH, (tc + 1) * TCH)
        pb = banks[0]

        def sg_mm(e, tsl=tsl, pb=pb):
            for k in range(ND):
                instr = e.matmul(pb[0:1, :], shgb[:, k:k + 1], xn_fm[k][:, tsl],
                                 start=(k == 0), stop=(k == ND - 1))
            return instr
        p.emit("pe", sg_mm)
        p.barrier()
        p.emit("act", lambda e, tsl=tsl, pb=pb: e.activation(
            out=SGRb[0:1, tsl], in_=pb[0:1, :], func=AF.Sigmoid))
        p.barrier()
    rst.close()

    ones1 = p.sb["cst_ones1x128"]
    import os
    NE = int(os.environ.get("L0F_NE", "10"))
    NHC = 11
    est = p.phase_stack()
    HB = p.sbuf("l0f_hb", [128, NHC * T], BF16, est)
    cbt = [p.sbuf(f"l0f_cbt{j}", [128, T], BF16, est) for j in range(2)]
    crow = [p.sbuf(f"l0f_crow{j}", [1, T], BF16, est) for j in range(2)]
    wgb = [p.sbuf(f"l0f_wgb{j}", [128, ND * 128], BF16, est) for j in range(2)]
    wub = [p.sbuf(f"l0f_wub{j}", [128, ND * 128], BF16, est) for j in range(2)]
    wdb = [p.sbuf(f"l0f_wdb{j}", [128, NHC * 128], BF16, est) for j in range(2)]
    gs_t = [p.sbuf(f"l0f_gs{j}", [128, TCH], BF16, est) for j in range(2)]
    t1_t = [p.sbuf(f"l0f_t1{j}", [128, TCH], BF16, est) for j in range(2)]
    ot32 = [p.sbuf(f"l0f_ot{j}", [128, T], F32, est) for j in range(2)]
    pacc = [p.sbuf(f"l0f_pacc{j}", [128, T], F32, est) for j in range(2)]
    tok_pl = {}
    tok_reg = {}
    tok_wl = {}
    tok_mm = {}
    tok_gs = {}
    tok_t1 = {}
    tok_h = {}
    tok_cb = {}
    tok_cbrd = {}
    tok_dwl = {}
    tok_dmm = {}
    tok_ev = {}
    tok_acc = {}
    tok_dmm_last = {}

    def wload(e, hc):
        if e >= NE:
            return
        dep = [tok_mm.get((e, hc - 2, NT - 1)) if hc >= 2 else
               tok_mm.get((e - 1, NHC - 1 - hc, NT - 1))]
        if e >= 8:
            hb = (e - 8) * 11
            load_block_kpj(p, wgb[hc % 2], p.dram["shwg_b"][hb + hc], ND, waits=dep)
            tok_wl[(e, hc)] = load_block_kpj(p, wub[hc % 2], p.dram["shwu_b"][hb + hc], ND)
        else:
            load_block_kpj(p, wgb[hc % 2], p.dram["moe_wg_hc"][e, hc], ND, waits=dep)
            tok_wl[(e, hc)] = load_block_kpj(p, wub[hc % 2], p.dram["moe_wu_hc"][e, hc], ND)

    wload(0, 0)
    wload(0, 1)
    for e in range(NE):
        shared = (e >= 8)
        j2 = e % 2
        if shared:
            csrc = SGRb
            tok_crow = None
        else:
            tok_crow = p.dma("pool", crow[j2][:], cbr_d[e:e + 1, :],
                             waits=[tok_cb.get(e - 2, [None] * 4)[3]])
            csrc = crow[j2]
        cbtoks = []
        for tc in range(NT):
            tsl = slice(tc * TCH, (tc + 1) * TCH)
            pb = banks[4 + tc]
            bdep = tok_ev.get((e - 1, ND - 1, tc))
            tmm = p.emit("pe", lambda e_, tsl=tsl, pb=pb, csrc=csrc: e_.matmul(
                pb[:], ones1[:], csrc[0:1, tsl], start=True, stop=True),
                waits=[tok_crow, bdep])
            cbtoks.append(p.emit("act", lambda e_, tsl=tsl, pb=pb, j2=j2: e_.activation(
                out=cbt[j2][:, tsl], in_=pb[:], func=AF.Copy),
                waits=[tmm, tok_cbrd.get(e - 2)]))
        tok_cb[e] = cbtoks
        for hc in range(NHC):
            jh = hc % 2
            for tc in range(NT):
                jt = tc % 2
                bg = banks[jt * 2]
                bu = banks[jt * 2 + 1]
                deps = [tok_wl[(e, hc)], cbtoks[tc],
                        tok_gs.get((e, hc, tc - 2)), tok_t1.get((e, hc, tc - 2)),
                        tok_gs.get((e, hc - 1, tc + 2)), tok_t1.get((e, hc - 1, tc + 2))]
                if hc == 0:
                    deps += [tok_gs.get((e - 1, NHC - 1, NT - 1)),
                             tok_t1.get((e - 1, NHC - 1, NT - 1)),
                             tok_h.get((e - 1, NHC - 1, NT - 1))]

                def gu_mm(e_, jh=jh, bg=bg, bu=bu, tc=tc):
                    for k in range(ND):
                        e_.matmul(bg[:], wgb[jh][:, k * 128:(k + 1) * 128],
                                  xn_fm[k][:, tc * TCH:(tc + 1) * TCH],
                                  start=(k == 0), stop=(k == ND - 1))
                    for k in range(ND):
                        instr = e_.matmul(bu[:], wub[jh][:, k * 128:(k + 1) * 128],
                                          xn_fm[k][:, tc * TCH:(tc + 1) * TCH],
                                          start=(k == 0), stop=(k == ND - 1))
                    return instr
                tok_mm[(e, hc, tc)] = p.emit("pe", gu_mm, waits=deps)
                hwar = [tok_h.get((e, hc, tc - 2)), tok_h.get((e, hc - 1, tc + 2))]
                if hc == 0:
                    hwar.append(tok_h.get((e - 1, NHC - 1, NT - 1)))
                tok_gs[(e, hc, tc)] = p.emit("act", lambda e_, bg=bg, jt=jt:
                                             e_.activation(
                    out=gs_t[jt][:], in_=bg[:], func=AF.Silu),
                    waits=[tok_mm[(e, hc, tc)]] + hwar)
                tok_t1[(e, hc, tc)] = p.emit("dve", lambda e_, bu=bu, jt=jt, j2=j2, tc=tc:
                                             e_.tensor_tensor(
                    out=t1_t[jt][:], in0=bu[:], in1=cbt[j2][:, tc * TCH:(tc + 1) * TCH],
                    op=ALU.mult),
                    waits=[tok_mm[(e, hc, tc)]] + hwar)
                if not shared:
                    tok_cbrd[e] = tok_t1[(e, hc, tc)]
                hdeps = [tok_gs[(e, hc, tc)], tok_t1[(e, hc, tc)],
                         tok_dmm_last.get(e - 1)]
                tok_h[(e, hc, tc)] = p.emit("dve", lambda e_, hc=hc, tc=tc, jt=jt:
                                            e_.tensor_tensor(
                    out=HB[:, hc * T + tc * TCH: hc * T + (tc + 1) * TCH],
                    in0=t1_t[jt][:], in1=gs_t[jt][:], op=ALU.mult), waits=hdeps)
            if hc + 2 < NHC:
                wload(e, hc + 2)
            elif hc == NHC - 1:
                wload(e + 1, 0)
                wload(e + 1, 1)
        def wdload(e, dk):
            dep = [tok_dmm.get((e, dk - 2, NT - 1)) if dk >= 2 else
                   tok_dmm.get((e - 1, ND - 2 + dk, NT - 1))]
            if e >= 8:
                hb = (e - 8) * 11
                blk = p.dram["shwd_b"][dk][hb * 128:(hb + 11) * 128, :]
                tok_dwl[(e, dk)] = load_block_kpj(p, wdb[dk % 2], blk, NHC, waits=dep)
            else:
                tok_dwl[(e, dk)] = load_block_kpj(p, wdb[dk % 2],
                                                  p.dram["moe_wd_dk"][e, dk], NHC,
                                                  waits=dep)
        if e == 0:
            wdload(0, 0)
            wdload(0, 1)
        for dk in range(ND):
            jd = dk % 2
            if e > 0:
                tok_pl[(e, dk)] = p.dma(
                    "sp", pacc[jd][:], out_acc[dk * 128:(dk + 1) * 128, :],
                    waits=[tok_reg.get((e - 1, dk)),
                           tok_ev.get((e, dk - 2, NT - 1)) if dk >= 2
                           else tok_ev.get((e - 1, ND - 2 + dk, NT - 1))])
            for tc in range(NT):
                pb = banks[4 + tc]
                hdep = [tok_h[(e, hc, tc)] for hc in range(NHC)] if dk == 0 else []
                prev_ev = (tok_ev.get((e, dk - 1, tc)) if dk >= 1
                           else tok_ev.get((e - 1, ND - 1, tc)))
                deps = [tok_dwl[(e, dk)], prev_ev] + hdep
                if dk == 0:
                    deps.append(cbtoks[tc])

                def dn_mm(e_, jd=jd, tc=tc, pb=pb):
                    for hc in range(NHC):
                        instr = e_.matmul(
                            pb[:], wdb[jd][:, hc * 128:(hc + 1) * 128],
                            HB[:, hc * T + tc * TCH: hc * T + (tc + 1) * TCH],
                            start=(hc == 0), stop=(hc == NHC - 1))
                    return instr
                tok_dmm[(e, dk, tc)] = p.emit("pe", dn_mm, waits=deps)
                acc_dep = (tok_acc.get((e, dk - 2)) if dk >= 2
                           else tok_acc.get((e - 1, ND - 2 + dk)))
                if e == 0:
                    tok_ev[(e, dk, tc)] = p.emit(
                        "dve" if tc % 2 else "act",
                        (lambda e_, tc=tc, pb=pb, jd=jd: e_.tensor_copy(
                            out=ot32[jd][:, tc * TCH:(tc + 1) * TCH], in_=pb[:]))
                        if tc % 2 else
                        (lambda e_, tc=tc, pb=pb, jd=jd: e_.activation(
                            out=ot32[jd][:, tc * TCH:(tc + 1) * TCH], in_=pb[:],
                            func=AF.Copy)),
                        waits=[tok_dmm[(e, dk, tc)], acc_dep])
                else:
                    tok_ev[(e, dk, tc)] = p.emit(
                        "dve", lambda e_, tc=tc, pb=pb, jd=jd: e_.tensor_tensor(
                            out=ot32[jd][:, tc * TCH:(tc + 1) * TCH], in0=pb[:],
                            in1=pacc[jd][:, tc * TCH:(tc + 1) * TCH], op=ALU.add),
                        waits=[tok_dmm[(e, dk, tc)], acc_dep, tok_pl[(e, dk)]])
            ne, ndk = (e, dk + 2) if dk + 2 < ND else (e + 1, dk + 2 - ND)
            if ne < NE:
                wdload(ne, ndk)
            tok_acc[(e, dk)] = tok_reg[(e, dk)] = p.dma(
                "pool", out_acc[dk * 128:(dk + 1) * 128, :], ot32[dk % 2][:],
                waits=[tok_ev[(e, dk, tc)] for tc in range(NT)])
            tok_dmm_last[e] = tok_dmm[(e, dk, NT - 1)]
    p.barrier()
    est.close()

    cst_ = p.phase_stack()
    cv32 = [p.sbuf(f"l0f_cv32{j}", [128, T], F32, cst_) for j in range(2)]
    cv16 = [p.sbuf(f"l0f_cv16{j}", [128, T], BF16, cst_) for j in range(2)]
    for k in range(ND):
        j = k % 2
        p.dma("sp", cv32[j][:], out_acc[k * 128:(k + 1) * 128, :])
        p.barrier()
        p.emit("act", lambda e, j=j: e.activation(out=cv16[j][:], in_=cv32[j][:],
                                                  func=AF.Copy))
        p.barrier()
        p.dma("sp", out_fm[k * 128:(k + 1) * 128, :], cv16[j][:])
    p.barrier()
    cst_.close()
    st.close()




PADT = T + 4
NF = 48

def emit_l1a(p, x1_fm, mlp_fm):
    st = p.phase_stack()
    banks = [p.psum(f"l1a_ps{i}", stack=st) for i in range(8)]
    xn_tm = p.dram["XN_TM"]
    lf_fm = p.dram["LF_FM"]
    eps8 = p.sb["cst_c_eps8"]
    ones1 = p.sb["cst_ones1x128"]
    ones128 = p.sb["cst_ones128x1"]

    n1w = p.sbuf("l1a_n1w", [128, 16], F32, st)
    n2w = p.sbuf("l1a_n2w", [128, 16], F32, st)
    lb = p.sbuf("l1a_lb", [128, 16], F32, st)
    rb = p.sbuf("l1a_rb", [128, 16], F32, st)
    cb = p.sbuf("l1a_cb", [128, 16], F32, st)
    gab = p.sbuf("l1a_gab", [128, 16], F32, st)
    gxb = p.sbuf("l1a_gxb", [128, 16], F32, st)
    ob = p.sbuf("l1a_ob", [128, 16], F32, st)
    u1b = p.sbuf("l1a_u1b", [128, 48], F32, st)
    u2b = p.sbuf("l1a_u2b", [128, 48], F32, st)
    dnb = p.sbuf("l1a_dnb", [128, 16], F32, st)
    apar = p.sbuf("l1a_ap", [128, 16], F32, st)
    c8 = p.sbuf("l1a_c8", [128, 16], F32, st)
    cw = p.sbuf("l1a_cw", [128, 64], F32, st)
    for nm, tile in [("n1w", n1w), ("n2w", n2w), ("left_b", lb), ("right_b", rb),
                     ("conv_b", cb), ("ga_b", gab), ("gx_b", gxb), ("out_b", ob),
                     ("up1_b", u1b), ("up2_b", u2b), ("down_b", dnb), ("a_param", apar),
                     ("conv_w", cw)]:
        p.dma("sp", tile[:], p.dram[nm])
    p.barrier()
    tmp16 = p.sbuf("l1a_tmp16", [128, 16], F32, st)
    emit_softplus(p, c8, apar, tmp16, "l1a")
    p.emit("dve", lambda e: e.tensor_scalar(out=c8[:], in0=c8[:], scalar1=-8.0,
                                            op0=ALU.mult, scalar2=None))
    p.barrier()

    rr1b = p.sbuf("l1a_rr1b", [1, T], BF16, st)
    rr2b = p.sbuf("l1a_rr2b", [1, T], BF16, st)
    rrt_sh = p.sbuf("l1a_rrt", [128, T], BF16, st)
    rr1t = rrt_sh
    rr2t = rrt_sh

    rlst = p.phase_stack()
    TN = p.sbuf("l1a_tn", [128, ND * T], BF16, rlst)
    RT = p.sbuf("l1a_rt", [128, ND * PADT], BF16, rlst)
    p.emit("dve", lambda e: e.memset(RT[:], 0.0))
    sst = p.phase_stack()
    sq = [p.sbuf(f"l1a_sq{j}", [128, T], BF16, sst) for j in range(2)]
    tok_xl = {}
    tok_sq = {}
    for k in range(ND):
        tok_xl[k] = p.dma_t(TN[:, k * T:(k + 1) * T], xn_tm[0:T, k * 128:(k + 1) * 128])
    for k in range(ND):
        j = k % 2
        tok_sq[k] = p.emit("dve", lambda e, j=j, k=k: e.tensor_tensor(
            out=sq[j][:], in0=TN[:, k * T:(k + 1) * T], in1=TN[:, k * T:(k + 1) * T],
            op=ALU.mult), waits=[tok_xl[k], tok_sq.get((k - 2))])

        def sqmm(e, j=j, k=k):
            for tc in range(NT):
                instr = e.matmul(banks[4 + tc][0:1, :], ones128[:],
                                 sq[j][:, tc * TCH:(tc + 1) * TCH],
                                 start=(k == 0), stop=(k == ND - 1))
            return instr
        tok_sq[k] = p.emit("pe", sqmm, waits=[tok_sq[k]])
    p.barrier()
    sst.close()
    for tc in range(NT):
        p.emit("act", lambda e, tc=tc, pb=banks[4 + tc]: e.activation(
            out=banks[tc][0:1, 0:TCH], in_=pb[0:1, :], func=AF.Ln, scale=1.0 / D,
            bias=eps8[0:1, :]))
    p.barrier()
    for tc in range(NT):
        tsl = slice(tc * TCH, (tc + 1) * TCH)
        p.emit("act", lambda e, tsl=tsl, tc=tc: e.activation(
            out=rr1b[0:1, tsl], in_=banks[tc][0:1, 0:TCH], func=AF.Exp, scale=-0.5))
    p.barrier()
    for tc in range(NT):
        tsl = slice(tc * TCH, (tc + 1) * TCH)
        p.emit("pe", lambda e, tsl=tsl, pb=banks[tc]: e.matmul(
            pb[:], ones1[:], rr1b[0:1, tsl], start=True, stop=True))
    p.barrier()
    for tc in range(NT):
        tsl = slice(tc * TCH, (tc + 1) * TCH)
        p.emit("act", lambda e, tsl=tsl, pb=banks[tc]: e.activation(
            out=rr1t[:, tsl], in_=pb[:], func=AF.Copy))
    p.barrier()
    for k in range(ND):
        for tc in range(NT):
            p.emit("dve", lambda e, k=k, tc=tc: e.scalar_tensor_tensor(
                out=TN[:, k * T + tc * TCH: k * T + (tc + 1) * TCH],
                in0=TN[:, k * T + tc * TCH: k * T + (tc + 1) * TCH],
                scalar=n1w[:, k:k + 1], op0=ALU.mult, op1=ALU.mult,
                in1=rr1t[:, tc * TCH:(tc + 1) * TCH]))
    p.barrier()

    s1st = p.phase_stack()
    wrb = [p.sbuf(f"l1a_wrb{j}", [128, ND * 128], BF16, s1st) for j in range(2)]
    acc1 = p.sbuf("l1a_acc1", [128, T], F32, s1st)
    acc2 = p.sbuf("l1a_acc2", [128, T], F32, s1st)
    tok_wl = {}
    tok_mm = {}
    tok_ev = {}
    tok_conv = {}

    def wload_r(m):
        tok_wl[m] = load_block_kpj(p, wrb[m % 2], p.dram["rightW_b"][m], ND,
                                   waits=[tok_mm.get((m - 2, NT - 1))])
    wload_r(0)
    wload_r(1)
    for m in range(ND):
        j = m % 2
        for tc in range(NT):
            pb = banks[tc % 4]
            deps = [tok_wl[m], tok_ev.get((m - 1, tc))]

            def r_mm(e, j=j, tc=tc, pb=pb):
                for k in range(ND):
                    instr = e.matmul(pb[:], wrb[j][:, k * 128:(k + 1) * 128],
                                     TN[:, k * T + tc * TCH: k * T + (tc + 1) * TCH],
                                     start=(k == 0), stop=(k == ND - 1))
                return instr
            tok_mm[(m, tc)] = p.emit("pe", r_mm, waits=deps)
            rt_off = m * PADT + 4 + tc * TCH
            tok_ev[(m, tc)] = p.emit("dve", lambda e, pb=pb, m=m, rt_off=rt_off:
                                     e.tensor_scalar(
                out=RT[:, rt_off:rt_off + TCH], in0=pb[:], scalar1=rb[:, m:m + 1],
                op0=ALU.add, scalar2=None),
                waits=[tok_mm[(m, tc)], tok_conv.get(m - 1)])
        if m + 2 < ND:
            wload_r(m + 2)
        base = m * PADT
        evd = [tok_ev[(m, tc)] for tc in range(NT)]
        t1_ = p.emit("dve", lambda e, base=base, m=m: e.tensor_scalar(
            out=acc1[:], in0=RT[:, base + 1: base + 1 + T], scalar1=cw[:, m * 4:m * 4 + 1],
            op0=ALU.mult, scalar2=None), waits=evd + [tok_conv.get(m - 1)])
        t2_ = p.emit("dve", lambda e, base=base, m=m: e.scalar_tensor_tensor(
            out=acc2[:], in0=RT[:, base + 2: base + 2 + T],
            scalar=cw[:, m * 4 + 1:m * 4 + 2], op0=ALU.mult, op1=ALU.add, in1=acc1[:]),
            waits=[t1_])
        t3_ = p.emit("dve", lambda e, base=base, m=m: e.scalar_tensor_tensor(
            out=acc1[:], in0=RT[:, base + 3: base + 3 + T],
            scalar=cw[:, m * 4 + 2:m * 4 + 3], op0=ALU.mult, op1=ALU.add, in1=acc2[:]),
            waits=[t2_])
        t4_ = p.emit("dve", lambda e, base=base, m=m: e.scalar_tensor_tensor(
            out=acc2[:], in0=RT[:, base + 4: base + 4 + T],
            scalar=cw[:, m * 4 + 3:m * 4 + 4], op0=ALU.mult, op1=ALU.add, in1=acc1[:]),
            waits=[t3_])
        tok_conv[m] = p.emit("dve", lambda e, base=base, m=m: e.tensor_scalar(
            out=RT[:, base + 4: base + 4 + T], in0=acc2[:], scalar1=cb[:, m:m + 1],
            op0=ALU.add, scalar2=None), waits=[t4_])
    p.barrier()
    s1st.close()

    s2st = p.phase_stack()
    wlb = [p.sbuf(f"l1a_wlb{j}", [128, ND * 128], BF16, s2st) for j in range(2)]
    lf_t = [p.sbuf(f"l1a_lft{j}", [128, T], BF16, s2st) for j in range(2)]
    tok_wl = {}
    tok_mm = {}
    tok_ev = {}
    tok_sto = {}

    def wload_l(m):
        tok_wl[m] = load_block_kpj(p, wlb[m % 2], p.dram["leftW_b"][m], ND,
                                   waits=[tok_mm.get((m - 2, NT - 1))])
    wload_l(0)
    wload_l(1)
    for m in range(ND):
        j = m % 2
        for tc in range(NT):
            pb = banks[tc % 4]
            deps = [tok_wl[m], tok_ev.get((m - 1, tc))]

            def l_mm(e, j=j, tc=tc, pb=pb):
                for k in range(ND):
                    instr = e.matmul(pb[:], wlb[j][:, k * 128:(k + 1) * 128],
                                     TN[:, k * T + tc * TCH: k * T + (tc + 1) * TCH],
                                     start=(k == 0), stop=(k == ND - 1))
                return instr
            tok_mm[(m, tc)] = p.emit("pe", l_mm, waits=deps)
            tok_ev[(m, tc)] = p.emit("act", lambda e, pb=pb, m=m, tc=tc, j=j:
                                     e.activation(
                out=lf_t[j][:, tc * TCH:(tc + 1) * TCH], in_=pb[:],
                func=AF.Gelu, bias=lb[:, m:m + 1]),
                waits=[tok_mm[(m, tc)], tok_sto.get(m - 2)])
        if m + 2 < ND:
            wload_l(m + 2)
        tok_sto[m] = p.dma("sp", lf_fm[m * 128:(m + 1) * 128, :], lf_t[m % 2][:],
                           waits=[tok_ev[(m, tc)] for tc in range(NT)])
    p.barrier()
    s2st.close()

    LR = TN
    gst = p.phase_stack()
    At = p.sbuf("l1a_At", [128, T], F32, gst)
    Bt = p.sbuf("l1a_Bt", [128, T], F32, gst)
    rt_g = [p.sbuf(f"l1a_rtg{j}", [128, TCH], BF16, gst) for j in range(2)]
    it_g = [p.sbuf(f"l1a_itg{j}", [128, TCH], BF16, gst) for j in range(2)]
    la_g = [p.sbuf(f"l1a_lag{j}", [128, TCH], F32, gst) for j in range(2)]
    mu_g = [p.sbuf(f"l1a_mug{j}", [128, TCH], F32, gst) for j in range(2)]
    t4_g = [p.sbuf(f"l1a_t4g{j}", [128, TCH], BF16, gst) for j in range(2)]
    rsc = p.sbuf("l1a_rsc", [128, T], BF16, gst)
    lfl = [p.sbuf(f"l1a_lfl{j}", [128, T], BF16, gst) for j in range(2)]
    wab = [p.sbuf(f"l1a_wab{j}", [128, ND * 128], BF16, gst) for j in range(2)]
    wxb = [p.sbuf(f"l1a_wxb{j}", [128, ND * 128], BF16, gst) for j in range(2)]
    tok_wl = {}
    tok_mm = {}
    tok_r = {}
    tok_i = {}
    tok_la = {}
    tok_A = {}
    tok_mu = {}
    tok_B = {}
    tok_scan = {}
    tok_lr = {}
    tok_lfl = {}

    def wload_g(m):
        dep = [tok_mm.get((m - 2, NT - 1))]
        load_block_kpj(p, wab[m % 2], p.dram["gaW_b"][m], ND, waits=dep)
        tok_wl[m] = load_block_kpj(p, wxb[m % 2], p.dram["gxW_b"][m], ND)
    wload_g(0)
    wload_g(1)
    for m in range(ND):
        j = m % 2
        tok_lfl[m] = p.dma("act", lfl[j][:], lf_fm[m * 128:(m + 1) * 128, :],
                           waits=[tok_lr.get(m - 2)])
        for tc in range(NT):
            jt = tc % 2
            ba = banks[jt * 2]
            bx = banks[jt * 2 + 1]
            deps = [tok_wl[m], tok_r.get((m, tc - 2)), tok_i.get((m, tc - 2)),
                    tok_r.get((m - 1, tc + 2)), tok_i.get((m - 1, tc + 2))]

            def g_mm(e, j=j, ba=ba, bx=bx, tc=tc):
                for k in range(ND):
                    e.matmul(ba[:], wab[j][:, k * 128:(k + 1) * 128],
                             RT[:, k * PADT + 4 + tc * TCH: k * PADT + 4 + (tc + 1) * TCH],
                             start=(k == 0), stop=(k == ND - 1))
                for k in range(ND):
                    instr = e.matmul(bx[:], wxb[j][:, k * 128:(k + 1) * 128],
                                     RT[:, k * PADT + 4 + tc * TCH:
                                        k * PADT + 4 + (tc + 1) * TCH],
                                     start=(k == 0), stop=(k == ND - 1))
                return instr
            tok_mm[(m, tc)] = p.emit("pe", g_mm, waits=deps)
            tok_r[(m, tc)] = p.emit("act", lambda e, ba=ba, m=m, jt=jt: e.activation(
                out=rt_g[jt][:], in_=ba[:], func=AF.Sigmoid, bias=gab[:, m:m + 1]),
                waits=[tok_mm[(m, tc)], tok_la.get((m, tc - 2)),
                       tok_la.get((m - 1, tc + 2))])
            tok_i[(m, tc)] = p.emit("act", lambda e, bx=bx, m=m, jt=jt: e.activation(
                out=it_g[jt][:], in_=bx[:], func=AF.Sigmoid, bias=gxb[:, m:m + 1]),
                waits=[tok_mm[(m, tc)], tok_B.get((m, tc - 2)),
                       tok_B.get((m - 1, tc + 2))])
            tok_la[(m, tc)] = p.emit("dve", lambda e, m=m, jt=jt: e.tensor_scalar(
                out=la_g[jt][:], in0=rt_g[jt][:], scalar1=c8[:, m:m + 1], op0=ALU.mult,
                scalar2=None),
                waits=[tok_r[(m, tc)], tok_A.get((m, tc - 2)), tok_mu.get((m, tc - 2)),
                       tok_A.get((m - 1, tc + 2)), tok_mu.get((m - 1, tc + 2))])
            tok_A[(m, tc)] = p.emit("act", lambda e, tc=tc, jt=jt: e.activation(
                out=At[:, tc * TCH:(tc + 1) * TCH], in_=la_g[jt][:], func=AF.Exp),
                waits=[tok_la[(m, tc)], tok_scan.get(m - 1)])
            tmu = p.emit("act", lambda e, jt=jt: e.activation(
                out=mu_g[jt][:], in_=la_g[jt][:], func=AF.Exp, scale=2.0),
                waits=[tok_la[(m, tc)], tok_B.get((m, tc - 2)),
                       tok_B.get((m - 1, tc + 2))])
            tok_mu[(m, tc)] = p.emit("act", lambda e, jt=jt: e.activation(
                out=mu_g[jt][:], in_=mu_g[jt][:], func=AF.Sqrt, scale=-1.0,
                bias=p.sb["cst_c_one"][:]), waits=[tmu])
            tk4 = p.emit("dve", lambda e, m=m, tc=tc, jt=jt: e.tensor_tensor(
                out=t4_g[jt][:], in0=it_g[jt][:],
                in1=RT[:, m * PADT + 4 + tc * TCH: m * PADT + 4 + (tc + 1) * TCH],
                op=ALU.mult),
                waits=[tok_i[(m, tc)], tok_B.get((m, tc - 2)),
                       tok_B.get((m - 1, tc + 2))])
            tok_B[(m, tc)] = p.emit("dve", lambda e, tc=tc, jt=jt: e.tensor_tensor(
                out=Bt[:, tc * TCH:(tc + 1) * TCH], in0=t4_g[jt][:], in1=mu_g[jt][:],
                op=ALU.mult),
                waits=[tk4, tok_mu[(m, tc)], tok_scan.get(m - 1)])
        if m + 2 < ND:
            wload_g(m + 2)
        tok_scan[m] = p.emit("dve", lambda e: e.tensor_tensor_scan(
            out=rsc[:], data0=At[:], data1=Bt[:], initial=0.0,
            op0=ALU.mult, op1=ALU.add),
            waits=[tok_A[(m, tc)] for tc in range(NT)] +
                  [tok_B[(m, tc)] for tc in range(NT)] + [tok_lr.get(m - 1)])
        tok_lr[m] = p.emit("dve", lambda e, m=m, j=j: e.tensor_tensor(
            out=LR[:, m * T:(m + 1) * T], in0=lfl[j][:], in1=rsc[:], op=ALU.mult),
            waits=[tok_scan[m], tok_lfl[m]])
    p.barrier()
    gst.close()

    ost = p.phase_stack()
    wob = [p.sbuf(f"l1a_wob{j}", [128, ND * 128], BF16, ost) for j in range(2)]
    xnk = [p.sbuf(f"l1a_oxn{j}", [128, T], BF16, ost) for j in range(2)]
    tev = [p.sbuf(f"l1a_tev{j}", [128, TCH], F32, ost) for j in range(2)]
    sq2 = [p.sbuf(f"l1a_sq2{j}", [128, T], BF16, ost) for j in range(2)]
    tok_wl = {}
    tok_mm = {}
    tok_x1 = {}
    tok_xn = {}
    tok_sq = {}
    tok_sqm = {}
    tok_sto = {}

    def wload_o(m):
        tok_wl[m] = load_block_kpj(p, wob[m % 2], p.dram["outW_b"][m], ND,
                                   waits=[tok_mm.get((m - 2, NT - 1))])
    wload_o(0)
    wload_o(1)
    tok_xn[0] = p.dma_t(xnk[0][:], xn_tm[0:T, 0:128])
    tok_xn[1] = p.dma_t(xnk[1][:], xn_tm[0:T, 128:256])
    for m2 in range(ND):
        j = m2 % 2
        for tc in range(NT):
            jt = tc % 2
            pb = banks[4 + jt]
            deps = [tok_wl[m2], tok_x1.get((m2, tc - 2)), tok_x1.get((m2 - 1, tc + 2))]

            def o_mm(e, j=j, tc=tc, pb=pb):
                for k in range(ND):
                    instr = e.matmul(pb[:], wob[j][:, k * 128:(k + 1) * 128],
                                     LR[:, k * T + tc * TCH: k * T + (tc + 1) * TCH],
                                     start=(k == 0), stop=(k == ND - 1))
                return instr
            tok_mm[(m2, tc)] = p.emit("pe", o_mm, waits=deps)
            te = p.emit("dve", lambda e, pb=pb, m2=m2, jt=jt: e.tensor_scalar(
                out=tev[jt][:], in0=pb[:], scalar1=ob[:, m2:m2 + 1], op0=ALU.add,
                scalar2=None),
                waits=[tok_mm[(m2, tc)], tok_x1.get((m2, tc - 2)),
                       tok_x1.get((m2 - 1, tc + 2))])
            rt_off = m2 * PADT + 4 + tc * TCH
            tok_x1[(m2, tc)] = p.emit("dve", lambda e, rt_off=rt_off, tc=tc, j=j, jt=jt:
                                      e.tensor_tensor(
                out=RT[:, rt_off:rt_off + TCH], in0=tev[jt][:],
                in1=xnk[j][:, tc * TCH:(tc + 1) * TCH], op=ALU.add),
                waits=[te, tok_xn[m2], tok_sto.get(m2 - 2), tok_sq.get(m2 - 2)])
        if m2 + 2 < ND:
            wload_o(m2 + 2)
            tok_xn[m2 + 2] = p.dma_t(
                xnk[m2 % 2][:], xn_tm[0:T, (m2 + 2) * 128:(m2 + 3) * 128],
                waits=[tok_x1[(m2, tc)] for tc in range(NT)])
        tok_sto[m2] = p.dma("sp", x1_fm[m2 * 128:(m2 + 1) * 128, :],
                            RT[:, m2 * PADT + 4: m2 * PADT + 4 + T],
                            waits=[tok_x1[(m2, tc)] for tc in range(NT)])
        tok_sq[m2] = p.emit("dve", lambda e, m2=m2, j=j: e.tensor_tensor(
            out=sq2[j][:], in0=RT[:, m2 * PADT + 4: m2 * PADT + 4 + T],
            in1=RT[:, m2 * PADT + 4: m2 * PADT + 4 + T], op=ALU.mult),
            waits=[tok_x1[(m2, tc)] for tc in range(NT)] + [tok_sqm.get(m2 - 2)])

        def sq2mm(e, m2=m2, j=j):
            for tc in range(NT):
                instr = e.matmul(banks[tc][0:1, :], ones128[:],
                                 sq2[j][:, tc * TCH:(tc + 1) * TCH],
                                 start=(m2 == 0), stop=(m2 == ND - 1))
            return instr
        tok_sqm[m2] = p.emit("pe", sq2mm, waits=[tok_sq[m2]])
    p.barrier()
    ost.close()
    for tc in range(NT):
        p.emit("act", lambda e, tc=tc, pb=banks[tc]: e.activation(
            out=banks[4 + tc][0:1, 0:TCH], in_=pb[0:1, :], func=AF.Ln, scale=1.0 / D,
            bias=eps8[0:1, :]))
    p.barrier()
    for tc in range(NT):
        tsl = slice(tc * TCH, (tc + 1) * TCH)
        p.emit("act", lambda e, tsl=tsl, tc=tc: e.activation(
            out=rr2b[0:1, tsl], in_=banks[4 + tc][0:1, 0:TCH], func=AF.Exp, scale=-0.5))
    p.barrier()
    for tc in range(NT):
        tsl = slice(tc * TCH, (tc + 1) * TCH)
        p.emit("pe", lambda e, tsl=tsl, pb=banks[tc]: e.matmul(
            pb[:], ones1[:], rr2b[0:1, tsl], start=True, stop=True))
    p.barrier()
    for tc in range(NT):
        tsl = slice(tc * TCH, (tc + 1) * TCH)
        p.emit("act", lambda e, tsl=tsl, pb=banks[tc]: e.activation(
            out=rr2t[:, tsl], in_=pb[:], func=AF.Copy))
    p.barrier()
    X1N = LR
    for k in range(ND):
        for tc in range(NT):
            p.emit("dve", lambda e, k=k, tc=tc: e.scalar_tensor_tensor(
                out=X1N[:, k * T + tc * TCH: k * T + (tc + 1) * TCH],
                in0=RT[:, k * PADT + 4 + tc * TCH: k * PADT + 4 + (tc + 1) * TCH],
                scalar=n2w[:, k:k + 1], op0=ALU.mult, op1=ALU.mult,
                in1=rr2t[:, tc * TCH:(tc + 1) * TCH]))
    p.barrier()

    g1_fm = p.dram["G1_FM"]
    ust = p.phase_stack()
    w1b = [p.sbuf(f"l1a_w1b{j}", [128, ND * 128], BF16, ust) for j in range(2)]
    w2b = [p.sbuf(f"l1a_w2b{j}", [128, ND * 128], BF16, ust) for j in range(2)]
    g1t = [p.sbuf(f"l1a_g1t{j}", [128, TCH], BF16, ust) for j in range(2)]
    gout = [p.sbuf(f"l1a_gout{j}", [128, T], BF16, ust) for j in range(2)]
    tok_wl = {}
    tok_mm = {}
    tok_g1 = {}
    tok_go = {}
    tok_sto = {}

    def wload_u(mc):
        dep = [tok_mm.get((mc - 2, NT - 1))]
        load_block_kpj(p, w1b[mc % 2], p.dram["up1W_b"][mc], ND, waits=dep)
        tok_wl[mc] = load_block_kpj(p, w2b[mc % 2], p.dram["up2W_b"][mc], ND)
    wload_u(0)
    wload_u(1)
    for mc in range(NF):
        j = mc % 2
        for tc in range(NT):
            jt = tc % 2
            b1 = banks[jt * 2]
            b2 = banks[jt * 2 + 1]
            deps = [tok_wl[mc], tok_g1.get((mc, tc - 2)), tok_go.get((mc, tc - 2)),
                    tok_g1.get((mc - 1, tc + 2)), tok_go.get((mc - 1, tc + 2))]

            def up_mm(e, j=j, b1=b1, b2=b2, tc=tc):
                for k in range(ND):
                    e.matmul(b1[:], w1b[j][:, k * 128:(k + 1) * 128],
                             X1N[:, k * T + tc * TCH: k * T + (tc + 1) * TCH],
                             start=(k == 0), stop=(k == ND - 1))
                for k in range(ND):
                    instr = e.matmul(b2[:], w2b[j][:, k * 128:(k + 1) * 128],
                                     X1N[:, k * T + tc * TCH: k * T + (tc + 1) * TCH],
                                     start=(k == 0), stop=(k == ND - 1))
                return instr
            tok_mm[(mc, tc)] = p.emit("pe", up_mm, waits=deps)
            tok_g1[(mc, tc)] = p.emit("act", lambda e, b1=b1, mc=mc, jt=jt: e.activation(
                out=g1t[jt][:], in_=b1[:], func=AF.Gelu, bias=u1b[:, mc:mc + 1]),
                waits=[tok_mm[(mc, tc)], tok_go.get((mc, tc - 2)),
                       tok_go.get((mc - 1, tc + 2))])
            tok_go[(mc, tc)] = p.emit("dve", lambda e, b2=b2, mc=mc, tc=tc, jt=jt:
                                      e.scalar_tensor_tensor(
                out=gout[mc % 2][:, tc * TCH:(tc + 1) * TCH], in0=b2[:],
                scalar=u2b[:, mc:mc + 1], op0=ALU.add, op1=ALU.mult, in1=g1t[jt][:]),
                waits=[tok_mm[(mc, tc)], tok_g1[(mc, tc)], tok_sto.get(mc - 2)])
        if mc + 2 < NF:
            wload_u(mc + 2)
        tok_sto[mc] = p.dma("sp", g1_fm[mc * 128:(mc + 1) * 128, :], gout[mc % 2][:],
                            waits=[tok_go[(mc, tc)] for tc in range(NT)])
    p.barrier()
    ust.close()
    rlst.close()

    dst_ = p.phase_stack()
    TH = 1024
    gc = p.sbuf("l1a_gc", [128, NF * TH], BF16, dst_)
    wdb = [p.sbuf(f"l1a_wdb{j}", [128, NF * 128], BF16, dst_) for j in range(2)]
    mt_ = [p.sbuf(f"l1a_mt{j}", [128, TH], BF16, dst_) for j in range(2)]
    tok_dmm = {}
    tok_ot = {}
    tok_ost = {}
    for th in range(2):
        gdep = [tok_dmm.get((th - 1, ND - 1, 1))]
        for mc in range(NF):
            p.dma("sp" if mc % 2 == 0 else "act", gc[:, mc * TH:(mc + 1) * TH],
                  g1_fm[mc * 128:(mc + 1) * 128, th * TH:(th + 1) * TH],
                  waits=gdep if mc < 2 else None)
        tok_wd = {}

        def wdload(dk, th=th):
            dep = [tok_dmm.get((th, dk - 2, 1)) if dk >= 2
                   else tok_dmm.get((th - 1, ND - 2 + dk, 1))]
            tok_wd[dk] = load_block_kpj(p, wdb[dk % 2], p.dram["downW_b"][dk], NF,
                                        waits=dep)
        wdload(0)
        wdload(1)
        gc_sp = ("d_sp", p.sem_count["d_sp"])
        gc_act = ("d_act", p.sem_count["d_act"])
        for dk in range(ND):
            j = dk % 2
            for stc in range(2):
                pb = banks[(dk % 2) * 2 + stc]
                prev_ot = (tok_ot.get((th, dk - 2, stc)) if dk >= 2
                           else tok_ot.get((th - 1, ND - 2 + dk, stc)))
                deps = [tok_wd[dk], gc_sp, gc_act, prev_ot]

                def dn_mm(e, j=j, stc=stc, pb=pb):
                    for mc in range(NF):
                        instr = e.matmul(
                            pb[:], wdb[j][:, mc * 128:(mc + 1) * 128],
                            gc[:, mc * TH + stc * TCH: mc * TH + (stc + 1) * TCH],
                            start=(mc == 0), stop=(mc == NF - 1))
                    return instr
                tok_dmm[(th, dk, stc)] = p.emit("pe", dn_mm, waits=deps)
                prev_st = (tok_ost.get((th, dk - 2)) if dk >= 2
                           else tok_ost.get((th - 1, ND - 2 + dk)))
                tok_ot[(th, dk, stc)] = p.emit(
                    "dve", lambda e, stc=stc, pb=pb, dk=dk, j=j: e.tensor_scalar(
                        out=mt_[j][:, stc * TCH:(stc + 1) * TCH], in0=pb[:],
                        scalar1=dnb[:, dk:dk + 1], op0=ALU.add, scalar2=None),
                    waits=[tok_dmm[(th, dk, stc)], prev_st])
            if dk + 2 < ND:
                wdload(dk + 2)
            tok_ost[(th, dk)] = p.dma(
                "sp", mlp_fm[dk * 128:(dk + 1) * 128, th * TH:(th + 1) * TH],
                mt_[dk % 2][:],
                waits=[tok_ot[(th, dk, 0)], tok_ot[(th, dk, 1)]])
    p.barrier()
    dst_.close()
    st.close()



NQ = 44

def emit_l1f(p, out_fm):
    st = p.phase_stack()
    banks = [p.psum(f"l1f_ps{i}", stack=st) for i in range(8)]
    xn_tm = p.dram["XN_TM"]
    gq_fm = p.dram["GQ_FM"]
    ones1 = p.sb["cst_ones1x128"]
    ones128 = p.sb["cst_ones128x1"]
    sg = p.sbuf("l1f_sg", [128, NQ], F32, st)
    su = p.sbuf("l1f_su", [128, NQ], F32, st)
    sd = p.sbuf("l1f_sd", [128, 16], F32, st)
    p.dma("sp", sg[:], p.dram["s_gate"])
    p.dma("sp", su[:], p.dram["s_up"])
    p.dma("sp", sd[:], p.dram["s_down"])
    csr = p.sbuf("l1f_csr", [1, T], F32, st)
    csrb = p.sbuf("l1f_csrb", [1, T], BF16, st)
    CSB = p.sbuf("l1f_csb", [128, T], F32, st)
    p.barrier()

    xst = p.phase_stack()
    xn_fm = [p.sbuf(f"l1f_xn{k}", [128, T], BF16, xst) for k in range(ND)]
    emit_load_xn_fm(p, xn_tm, xn_fm, "l1f")
    wst = p.phase_stack()
    wgb = [p.sbuf(f"l1f_wgb{j}", [128, ND * 128], BF16, wst) for j in range(2)]
    wub = [p.sbuf(f"l1f_wub{j}", [128, ND * 128], BF16, wst) for j in range(2)]
    gs_t = [p.sbuf(f"l1f_gst{j}", [128, TCH], BF16, wst) for j in range(2)]
    gout = [p.sbuf(f"l1f_gout{j}", [128, T], BF16, wst) for j in range(2)]
    tok_wl = {}
    tok_mm = {}
    tok_gs = {}
    tok_go = {}
    tok_store = {}

    def wload(mc):
        dep = [tok_mm.get((mc - 2, NT - 1))]
        load_block_kpj(p, wgb[mc % 2], p.dram["wqgT_b"][mc], ND, waits=dep)
        tok_wl[mc] = load_block_kpj(p, wub[mc % 2], p.dram["wquT_b"][mc], ND)
    wload(0)
    wload(1)
    for mc in range(NQ):
        j = mc % 2
        for tc in range(NT):
            bg = banks[(tc % 2) * 2]
            bu = banks[(tc % 2) * 2 + 1]
            deps = [tok_wl[mc], tok_gs.get((mc, tc - 2)), tok_go.get((mc, tc - 2)),
                    tok_gs.get((mc - 1, tc + 2)), tok_go.get((mc - 1, tc + 2))]

            def gu_mm(e, j=j, bg=bg, bu=bu, tc=tc):
                for k in range(ND):
                    e.matmul(bg[:], wgb[j][:, k * 128:(k + 1) * 128],
                             xn_fm[k][:, tc * TCH:(tc + 1) * TCH],
                             start=(k == 0), stop=(k == ND - 1))
                for k in range(ND):
                    instr = e.matmul(bu[:], wub[j][:, k * 128:(k + 1) * 128],
                                     xn_fm[k][:, tc * TCH:(tc + 1) * TCH],
                                     start=(k == 0), stop=(k == ND - 1))
                return instr
            tok_mm[(mc, tc)] = p.emit("pe", gu_mm, waits=deps)
            tok_gs[(mc, tc)] = p.emit("act", lambda e, bg=bg, mc=mc, tc=tc: e.activation(
                out=gs_t[tc % 2][:], in_=bg[:], func=AF.Silu, scale=sg[:, mc:mc + 1]),
                waits=[tok_mm[(mc, tc)], tok_go.get((mc, tc - 2)),
                       tok_go.get((mc - 1, tc + 2))])
            tok_go[(mc, tc)] = p.emit("dve", lambda e, bu=bu, mc=mc, tc=tc:
                                      e.scalar_tensor_tensor(
                out=gout[mc % 2][:, tc * TCH:(tc + 1) * TCH], in0=bu[:],
                scalar=su[:, mc:mc + 1], op0=ALU.mult, op1=ALU.mult,
                in1=gs_t[tc % 2][:]),
                waits=[tok_mm[(mc, tc)], tok_gs[(mc, tc)], tok_store.get(mc - 2)])
        if mc + 2 < NQ:
            wload(mc + 2)
        if mc >= 1:
            def cs_mm(e, mcl=mc - 1):
                for tc in range(NT):
                    instr = e.matmul(banks[4 + tc][0:1, :], ones128[:],
                                     gout[mcl % 2][:, tc * TCH:(tc + 1) * TCH],
                                     start=(mcl == 0), stop=(mcl == NQ - 1))
                return instr
            p.emit("pe", cs_mm, waits=[tok_go[(mc - 1, tc)] for tc in range(NT)])
            tok_store[mc - 1] = p.dma(
                "sp", gq_fm[(mc - 1) * 128:mc * 128, :], gout[(mc - 1) % 2][:],
                waits=[tok_go[(mc - 1, tc)] for tc in range(NT)])
    mcl = NQ - 1

    def cs_mm_last(e, mcl=mcl):
        for tc in range(NT):
            instr = e.matmul(banks[4 + tc][0:1, :], ones128[:],
                             gout[mcl % 2][:, tc * TCH:(tc + 1) * TCH],
                             start=False, stop=(tc == NT - 1) or True)
        return instr
    p.emit("pe", cs_mm_last, waits=[tok_go[(mcl, tc)] for tc in range(NT)])
    tok_store[mcl] = p.dma("sp", gq_fm[mcl * 128:(mcl + 1) * 128, :], gout[mcl % 2][:],
                           waits=[tok_go[(mcl, tc)] for tc in range(NT)])
    p.barrier()
    wst.close()
    xst.close()
    for tc in range(NT):
        tsl = slice(tc * TCH, (tc + 1) * TCH)
        p.emit("act", lambda e, tsl=tsl, pb=banks[4 + tc]: e.activation(
            out=csr[0:1, tsl], in_=pb[0:1, :], func=AF.Copy, scale=128.0))
    p.barrier()
    p.emit("act", lambda e: e.activation(out=csrb[:], in_=csr[:], func=AF.Copy))
    p.barrier()
    for tc in range(NT):
        tsl = slice(tc * TCH, (tc + 1) * TCH)
        p.emit("pe", lambda e, tsl=tsl, pb=banks[tc]: e.matmul(
            pb[:], ones1[:], csrb[0:1, tsl], start=True, stop=True))
    p.barrier()
    for tc in range(NT):
        tsl = slice(tc * TCH, (tc + 1) * TCH)
        p.emit("act", lambda e, tsl=tsl, pb=banks[tc]: e.activation(
            out=CSB[:, tsl], in_=pb[:], func=AF.Copy))
    p.barrier()

    dst_ = p.phase_stack()
    TH = 1024
    gc = p.sbuf("l1f_gc", [128, NQ * TH], BF16, dst_)
    wdb = [p.sbuf(f"l1f_wdb{j}", [128, NQ * 128], BF16, dst_) for j in range(2)]
    t1 = [p.sbuf(f"l1f_t1{j}", [128, TCH], F32, dst_) for j in range(2)]
    ot = [p.sbuf(f"l1f_ot{j}", [128, TH], BF16, dst_) for j in range(2)]
    tok_dmm = {}
    tok_ot = {}
    tok_ost = {}
    for th in range(2):
        gdep = [tok_dmm.get((th - 1, dk, stc)) for dk in (ND - 1, ND - 2)
                for stc in range(2)]
        tok_gc = None
        for mc in range(NQ):
            tok_gc = p.dma("sp" if mc % 2 == 0 else "act",
                           gc[:, mc * TH:(mc + 1) * TH],
                           gq_fm[mc * 128:(mc + 1) * 128, th * TH:(th + 1) * TH],
                           waits=gdep if mc < 2 else None)
        tok_gc2 = tok_gc
        tok_wd = {}

        def wdload(dk, th=th):
            dep = [tok_dmm.get((th, dk - 2, 1)) if dk >= 2
                   else tok_dmm.get((th - 1, ND - 2 + dk, 1))]
            tok_wd[dk] = load_block_kpj(p, wdb[dk % 2], p.dram["wqdT_b"][dk], NQ,
                                        waits=dep)
        wdload(0)
        wdload(1)
        gc_sp = ("d_sp", p.sem_count["d_sp"])
        gc_act = ("d_act", p.sem_count["d_act"])
        for dk in range(ND):
            j = dk % 2
            for stc in range(2):
                pb = banks[(dk % 2) * 2 + stc]
                gsl = slice(th * TH + stc * TCH, th * TH + (stc + 1) * TCH)
                prev_ot = (tok_ot.get((th, dk - 2, stc)) if dk >= 2
                           else tok_ot.get((th - 1, ND - 2 + dk, stc)))
                deps = [tok_wd[dk], gc_sp, gc_act, prev_ot]

                def dn_mm(e, j=j, stc=stc, pb=pb):
                    for mc in range(NQ):
                        instr = e.matmul(
                            pb[:], wdb[j][:, mc * 128:(mc + 1) * 128],
                            gc[:, mc * TH + stc * TCH: mc * TH + (stc + 1) * TCH],
                            start=(mc == 0), stop=(mc == NQ - 1))
                    return instr
                tok_dmm[(th, dk, stc)] = p.emit("pe", dn_mm, waits=deps)
                tk1 = p.emit("dve", lambda e, pb=pb, gsl=gsl, j=j: e.tensor_tensor(
                    out=t1[j][:], in0=pb[:], in1=CSB[:, gsl], op=ALU.subtract),
                    waits=[tok_dmm[(th, dk, stc)], prev_ot])
                prev_st = (tok_ost.get((th, dk - 2)) if dk >= 2
                           else tok_ost.get((th - 1, ND - 2 + dk)))
                tok_ot[(th, dk, stc)] = p.emit(
                    "dve", lambda e, stc=stc, dk=dk, j=j: e.tensor_scalar(
                        out=ot[j][:, stc * TCH:(stc + 1) * TCH], in0=t1[j][:],
                        scalar1=sd[:, dk:dk + 1], op0=ALU.mult, scalar2=None),
                    waits=[tk1, prev_st])
            if dk + 2 < ND:
                wdload(dk + 2)
            tok_ost[(th, dk)] = p.dma(
                "sp", out_fm[dk * 128:(dk + 1) * 128, th * TH:(th + 1) * TH], ot[dk % 2][:],
                waits=[tok_ot[(th, dk, 0)], tok_ot[(th, dk, 1)]])
    p.barrier()
    dst_.close()
    st.close()


L0A_INPUTS = ["nu_log", "theta_log", "BreT", "BimT", "CreT", "CimT", "l0a_D"]
L0F_INPUTS = ["gwT", "moe_wg_hc", "moe_wu_hc", "moe_wd_dk", "shg", "shwg_b", "shwu_b", "shwd_b"]
L1A_INPUTS = ["n1w", "n2w", "leftW_b", "rightW_b", "gaW_b", "gxW_b", "outW_b",
              "left_b", "right_b", "conv_b", "ga_b", "gx_b", "out_b", "a_param",
              "conv_w", "up1_b", "up2_b", "down_b", "up1W_b", "up2W_b", "downW_b"]
L1F_INPUTS = ["wqgT_b", "wquT_b", "wqdT_b", "s_gate", "s_up", "s_down"]

SB_INPUTS = {"l0a": L0A_INPUTS, "l0f": L0F_INPUTS, "l1a": L1A_INPUTS,
             "l1f": L1F_INPUTS}


def build_subblock_program(name):
    p = Prog(name)
    r_in = p.dram_t("r_in", [T, D], F32, "ExternalInput")
    r_out = p.dram_t("r_out", [T, D], F32, "ExternalOutput")
    for n in CONST_NAMES:
        p.dram_t(n, list(np.shape(_HOST[n])), F32, "ExternalInput")
    declare_inputs(p, _HOST, SB_INPUTS[name])
    xn_tm = p.dram_t("XN_TM", [T, D], BF16, "Internal")
    out_fm = p.dram_t("OUT_FM", [D, T], BF16, "Internal")
    load_consts(p)
    emit_ln(p, r_in, xn_tm, name)
    if name == "l0a":
        emit_l0a(p, out_fm)
        adds = [out_fm]
    elif name == "l0f":
        p.dram_t("OUT_ACC", [D, T], F32, "Internal")
        p.dram_t("CBR_D", [8, T], F32, "Internal")
        emit_l0f(p, out_fm)
        adds = [out_fm]
    elif name == "l1a":
        x1_fm = p.dram_t("X1_FM", [D, T], BF16, "Internal")
        p.dram_t("G1_FM", [6144, T], BF16, "Internal")
        p.dram_t("LF_FM", [D, T], BF16, "Internal")
        emit_l1a(p, x1_fm, out_fm)
        adds = [x1_fm, out_fm]
    elif name == "l1f":
        p.dram_t("GQ_FM", [5632, T], BF16, "Internal")
        emit_l1f(p, out_fm)
        adds = [out_fm]
    emit_resid(p, r_in, adds, r_out, name)
    return p.finalize()


_HOST = None
_PROG_CACHE = {}
SB_ORDER = ["l0a", "l0f", "l1a", "l1f"]
FUSED = True


def build_fused_program(blocks=None):
    import os
    blocks = blocks or os.environ.get("FUSE_BLOCKS", "l0a,l0f,l1a,l1f").split(",")
    p = Prog("fused")
    r0 = p.dram_t("r_in", [T, D], F32, "ExternalInput")
    rout = p.dram_t("r_out", [T, D], F32, "ExternalOutput")
    rmid = [p.dram_t(f"rm{i}", [T, D], F32, "Internal") for i in range(len(blocks) - 1)]
    chain_in = [r0] + rmid
    chain_out = rmid + [rout]
    for n in CONST_NAMES:
        p.dram_t(n, list(np.shape(_HOST[n])), F32, "ExternalInput")
    for name in blocks:
        declare_inputs(p, _HOST, SB_INPUTS[name])
    xn_tm = p.dram_t("XN_TM", [T, D], BF16, "Internal")
    out_fm = p.dram_t("OUT_FM", [D, T], BF16, "Internal")
    if "l0f" in blocks:
        p.dram_t("OUT_ACC", [D, T], F32, "Internal")
        p.dram_t("CBR_D", [8, T], F32, "Internal")
    if "l1a" in blocks:
        x1_fm = p.dram_t("X1_FM", [D, T], BF16, "Internal")
        p.dram_t("G1_FM", [6144, T], BF16, "Internal")
        p.dram_t("LF_FM", [D, T], BF16, "Internal")
    if "l1f" in blocks:
        p.dram_t("GQ_FM", [5632, T], BF16, "Internal")
    load_consts(p)
    for i, name in enumerate(blocks):
        rin, rdst = chain_in[i], chain_out[i]
        if i == 0:
            emit_ln(p, rin, xn_tm, name)
        if name == "l0a":
            emit_l0a(p, out_fm)
            adds = [out_fm]
        elif name == "l0f":
            emit_l0f(p, out_fm)
            adds = [out_fm]
        elif name == "l1a":
            emit_l1a(p, x1_fm, out_fm)
            adds = [x1_fm, out_fm]
        elif name == "l1f":
            emit_l1f(p, out_fm)
            adds = [out_fm]
        if i < len(blocks) - 1:
            emit_resid_ln(p, rin, adds, rdst, xn_tm, name)
        else:
            emit_resid(p, rin, adds, rdst, name)
    return p.finalize()


def kernel(**inputs):
    global _HOST
    _HOST = host_prep(inputs)
    x = inputs["L0_attn_x"]
    if FUSED:
        if "fused" not in _PROG_CACHE:
            _PROG_CACHE["fused"] = build_fused_program()
        nc = _PROG_CACHE["fused"]
        import os
        blocks = os.environ.get("FUSE_BLOCKS", "l0a,l0f,l1a,l1f").split(",")
        all_names = list(CONST_NAMES)
        for name in blocks:
            all_names += SB_INPUTS[name]
        in_maps = []
        for b in range(8):
            m = {"r_in": np.ascontiguousarray(x[b])}
            for n in all_names:
                m[n] = _HOST[n]
            in_maps.append(m)
        res = run_bass_kernel_spmd(nc, in_maps, core_ids=list(range(8)))
        return np.stack([res.results[b]["r_out"] for b in range(8)], axis=0)
    r = [np.ascontiguousarray(x[b]) for b in range(8)]
    for name in SB_ORDER:
        if name not in _PROG_CACHE:
            _PROG_CACHE[name] = build_subblock_program(name)
        nc = _PROG_CACHE[name]
        in_maps = []
        for b in range(8):
            m = {"r_in": r[b]}
            for n in CONST_NAMES + SB_INPUTS[name]:
                m[n] = _HOST[n]
            in_maps.append(m)
        res = run_bass_kernel_spmd(nc, in_maps, core_ids=list(range(8)))
        r = [res.results[b]["r_out"] for b in range(8)]
    return np.stack(r, axis=0)
